# version 02
# baseline (speedup 1.0000x reference)
import numpy as np
import ml_dtypes
from contextlib import ExitStack

import concourse.bass as bass
import concourse.mybir as mybir
from concourse.bass_utils import run_bass_kernel_spmd

F32 = mybir.dt.float32
BF16 = mybir.dt.bfloat16
AF = mybir.ActivationFunctionType
ALU = mybir.AluOpType
AX = mybir.AxisListType.X

NTOK = 4096
D = 1024
P = 1024
DFF0 = 4096
HID1 = 2048
NEXP = 8
C = 512
NCH = NTOK // C

I32 = mybir.dt.int32

CONST_VALS = [-1.0, np.pi / 2, -0.5, 0.5, 1e-5]
C_NEG1, C_HALFPI, C_NEGHALF, C_HALF, C_EPSLN = range(5)
TWO_PI = float(2 * np.pi)

_SEM_BASE = {}
_rust_wait_ge = bass.BassEngine.wait_ge


def _wait_ge_offset(self, sem, value, *a, **k):
    base = _SEM_BASE.get(getattr(sem, "num", None), 0)
    return _rust_wait_ge(self, sem, value + base, *a, **k)


bass.BassEngine.wait_ge = _wait_ge_offset


class SemPool:
    def __init__(self, nc):
        self.nc = nc
        self.free = []
        self.n = 0

    def acquire(self):
        if self.free:
            slot = self.free.pop()
        else:
            slot = [self.nc.alloc_semaphore(name=f"pool{self.n}"), 0]
            self.n += 1
        _SEM_BASE[slot[0].num] = slot[1]
        return slot

    def release_all(self, slots):
        totals = {}
        for i in self.nc.all_instructions():
            si = getattr(i, "sync_info", None)
            if si is None:
                continue
            for u in si.on_update:
                if u.sync_type == "semaphore" and u.update_mode in ("sem-add-imm", "sem-inc"):
                    totals[u.id] = totals.get(u.id, 0) + u.update_value
        for slot in slots:
            slot[1] = totals.get(slot[0].num, 0)
            self.free.append(slot)


class VS:
    def __init__(self, ph, name):
        self.s = ph.sem(name)
        self.n = 0

    def mark(self, inst):
        inst.then_inc(self.s, 1)
        self.n += 1

    def wait(self, eng):
        eng.wait_ge(self.s, self.n)



class Ph:
    def __init__(self, nc, stack, tag, pool=None):
        self.nc, self.stack, self.tag = nc, stack, tag
        self.pool = pool
        self._slots = []

    def sem(self, name, pooled=True):
        if self.pool is None or not pooled:
            return self.nc.alloc_semaphore(name=f"{self.tag}_{name}")
        slot = self.pool.acquire()
        self._slots.append(slot)
        return slot[0]

    def release(self):
        self.nc.all_engine_barrier()
        if self.pool is not None and self._slots:
            self.pool.release_all(self._slots)
        self._slots = []

    def sb(self, name, shape, dt):
        return self.stack.enter_context(self.nc.sbuf_tensor(f"{self.tag}_{name}", shape, dt))

    def ps(self, name, shape, dt=F32):
        return self.stack.enter_context(self.nc.psum_tensor(f"{self.tag}_{name}", shape, dt))


def _bcast_row(block, ph, nc, name, src_row_sb, ncols, ones_f32, done_events):
    dst = ph.sb(name, [128, ncols], F32)
    ps = ph.ps(f"{name}_ps", [128, min(ncols, 512)], F32)
    return dst, ps



def build_program(phases=("p1", "p2", "p3", "p5", "p6"), debug_out=(), debug_in=()):
    nc = bass.Bass("TRN2", target_bir_lowering=False, debug=False,
                   detect_race_conditions=False)
    _SEM_BASE.clear()

    def dram(name, shape, dt, kind="Internal"):
        if name in debug_out:
            kind = "ExternalOutput"
        if name in debug_in:
            kind = "ExternalInput"
        return nc.dram_tensor(name, shape, dt, kind=kind).ap()

    x_in = nc.dram_tensor("x", [NTOK, D], F32, kind="ExternalInput").ap()
    nu_log = nc.dram_tensor("nu_log", [P], F32, kind="ExternalInput").ap()
    theta_log = nc.dram_tensor("theta_log", [P], F32, kind="ExternalInput").ap()
    gamma_log = nc.dram_tensor("gamma_log", [P], F32, kind="ExternalInput").ap()
    B_re = nc.dram_tensor("B_re", [P, D], F32, kind="ExternalInput").ap()
    B_im = nc.dram_tensor("B_im", [P, D], F32, kind="ExternalInput").ap()
    C_re = nc.dram_tensor("C_re", [D, P], F32, kind="ExternalInput").ap()
    C_im = nc.dram_tensor("C_im", [D, P], F32, kind="ExternalInput").ap()
    Dvec = nc.dram_tensor("Dvec", [1, D], F32, kind="ExternalInput").ap()
    r0_w = nc.dram_tensor("r0_w", [D, NEXP], F32, kind="ExternalInput").ap()
    r0_b = nc.dram_tensor("r0_b", [1, NEXP], F32, kind="ExternalInput").ap()
    up0 = nc.dram_tensor("up0", [NEXP, DFF0, D], F32, kind="ExternalInput").ap()
    dn0 = nc.dram_tensor("dn0", [NEXP, D, DFF0], F32, kind="ExternalInput").ap()
    t_gamma = nc.dram_tensor("t_gamma", [1, D], F32, kind="ExternalInput").ap()
    t_wq = nc.dram_tensor("t_wq", [D, 128], F32, kind="ExternalInput").ap()
    t_wkv = nc.dram_tensor("t_wkv", [D, 256], F32, kind="ExternalInput").ap()
    t_wout = nc.dram_tensor("t_wout", [128, D], F32, kind="ExternalInput").ap()
    ln1_s = nc.dram_tensor("ln1_s", [1, D], F32, kind="ExternalInput").ap()
    ln1_h = nc.dram_tensor("ln1_h", [1, D], F32, kind="ExternalInput").ap()
    a_wq = nc.dram_tensor("a_wq", [D, D], F32, kind="ExternalInput").ap()
    a_wk = nc.dram_tensor("a_wk", [D, D], F32, kind="ExternalInput").ap()
    a_wv = nc.dram_tensor("a_wv", [D, D], F32, kind="ExternalInput").ap()
    a_wo = nc.dram_tensor("a_wo", [D, D], F32, kind="ExternalInput").ap()
    a_bo = nc.dram_tensor("a_bo", [1, D], F32, kind="ExternalInput").ap()
    ln2_s = nc.dram_tensor("ln2_s", [1, D], F32, kind="ExternalInput").ap()
    ln2_h = nc.dram_tensor("ln2_h", [1, D], F32, kind="ExternalInput").ap()
    g1_w = nc.dram_tensor("g1_w", [D, NEXP], F32, kind="ExternalInput").ap()
    w1 = nc.dram_tensor("w1", [NEXP, D, HID1], F32, kind="ExternalInput").ap()
    w2 = nc.dram_tensor("w2", [NEXP, D, HID1], F32, kind="ExternalInput").ap()
    w3 = nc.dram_tensor("w3", [NEXP, HID1, D], F32, kind="ExternalInput").ap()
    consts = nc.dram_tensor("consts", [128, len(CONST_VALS)], F32, kind="ExternalInput").ap()
    ones_f = nc.dram_tensor("ones_f", [1, 128], F32, kind="ExternalInput").ap()
    iota_bc = nc.dram_tensor("iota_bc", [128, C], F32, kind="ExternalInput").ap()
    ident_b = nc.dram_tensor("ident_b", [128, 128], BF16, kind="ExternalInput").ap()
    cmask = nc.dram_tensor("cmask", [4, 128, 512], BF16, kind="ExternalInput").ap()

    out = nc.dram_tensor("out", [NTOK, D], F32, kind="ExternalOutput").ap()

    x_bf = dram("x_bf", [NTOK, D], BF16)
    s1_f = dram("s1_f", [NTOK, D], F32)
    s1_b = dram("s1_b", [NTOK, D], BF16)
    s2_f = dram("s2_f", [NTOK, D], F32)
    s2_b = dram("s2_b", [NTOK, D], BF16)
    s3_f = dram("s3_f", [NTOK, D], F32)
    s3_b = dram("s3_b", [NTOK, D], BF16)
    s4_f = dram("s4_f", [NTOK, D], F32)
    s4_b = dram("s4_b", [NTOK, D], BF16)
    tmp_b = dram("tmp_b", [NTOK, D], BF16)
    xs_re = dram("xs_re", [P, NTOK], BF16)
    xs_im = dram("xs_im", [P, NTOK], BF16)
    bre_b = dram("bre_b", [P, D], BF16)
    bim_b = dram("bim_b", [P, D], BF16)
    cre_b = dram("cre_b", [D, P], BF16)
    cim_b = dram("cim_b", [D, P], BF16)
    up0_b = dram("up0_b", [NEXP, DFF0, D], BF16)
    dn0_b = dram("dn0_b", [NEXP, D, DFF0], BF16)
    w1_b = dram("w1_b", [NEXP, D, HID1], BF16)
    w2_b = dram("w2_b", [NEXP, D, HID1], BF16)
    w3_b = dram("w3_b", [NEXP, HID1, D], BF16)
    q_sc = dram("q_sc", [D, NTOK], BF16)
    k_sc = dram("k_sc", [D, NTOK], BF16)
    v_sc = dram("v_sc", [NTOK, 16 * 65], BF16)

    ctx = {k: v for k, v in locals().items() if not k.startswith("_")}

    with ExitStack() as stack:
        glob = Ph(nc, stack, "g")
        pool = SemPool(nc)
        glob.sempool = pool
        cast_a = glob.sem("cast_a")
        cast_m0 = glob.sem("cast_m0")
        cast_m1 = glob.sem("cast_m1")
        cast_done = {"a": cast_a, "m0": cast_m0, "m1": cast_m1}
        cnt = {"a": 80, "m0": 256, "m1": 384}

        consts_sb = glob.sb("consts", [128, len(CONST_VALS)], F32)
        ones_sb = glob.sb("ones", [1, 128], F32)
        ones_bsb = glob.sb("ones_b", [128, 1], BF16)
        d1_b = glob.sb("d1_b", [128, D], F32)
        rb0_b = glob.sb("rb0_b", [128, NEXP], F32)
        cmask_sb = glob.sb("cmask", [128, 4 * 512], BF16)
        nus = glob.sb("nus", [128, NCH], F32)
        ths = glob.sb("ths", [128, NCH], F32)
        gls = glob.sb("gls", [128, NCH], F32)

        C_AP = lambda i: consts_sb[:, i:i + 1]

        with ExitStack() as pst:
            ph = Ph(nc, pst, "p0", pool)
            ld = ph.sem("ld")
            bc_done = ph.sem("bc")
            bc_ev = ph.sem("bc_ev")
            rows_sb = ph.sb("rows", [1, D + NEXP], F32)
            bps = [ph.ps(f"bps{i}", [128, 512], F32) for i in range(2)]
            nld = 0

            _bl = nc.Block(); block = _bl.__enter__()

            @block.scalar
            def _(sc):
                nonlocal nld
                sc.dma_start(consts_sb[:, :], consts).then_inc(ld, 16)
                sc.dma_start(ones_sb[:, :], ones_f).then_inc(ld, 16)
                sc.dma_start(cmask_sb[:, :].rearrange("p (a n) -> p a n", a=4),
                             cmask.rearrange("a p n -> p a n")).then_inc(ld, 16)
                sc.dma_start(rows_sb[:, 0:D], Dvec).then_inc(ld, 16)
                sc.dma_start(rows_sb[:, D:D + NEXP], r0_b).then_inc(ld, 16)
                for g in range(NCH):
                    sc.dma_start(nus[:, g:g + 1],
                                 nu_log[g * 128:(g + 1) * 128].unsqueeze(1)).then_inc(ld, 16)
                    sc.dma_start(ths[:, g:g + 1],
                                 theta_log[g * 128:(g + 1) * 128].unsqueeze(1)).then_inc(ld, 16)
                    sc.dma_start(gls[:, g:g + 1],
                                 gamma_log[g * 128:(g + 1) * 128].unsqueeze(1)).then_inc(ld, 16)
                nld = 5 + 3 * NCH

            @block.tensor
            def _(t):
                t.wait_ge(ld, 16 * nld)
                k = 0
                for h in range(2):
                    if k >= 2:
                        t.wait_ge(bc_ev, k - 1)
                    t.matmul(bps[k % 2][:, :], ones_sb[:, :],
                             rows_sb[:, h * 512:(h + 1) * 512],
                             start=True, stop=True).then_inc(bc_done, 1)
                    k += 1
                t.wait_ge(bc_ev, k - 1)
                t.matmul(bps[k % 2][:, 0:NEXP], ones_sb[:, :],
                         rows_sb[:, D:D + NEXP], start=True, stop=True).then_inc(bc_done, 1)

            @block.vector
            def _(v):
                k = 0
                for h in range(2):
                    v.wait_ge(bc_done, k + 1)
                    v.tensor_copy(d1_b[:, h * 512:(h + 1) * 512],
                                  bps[k % 2][:, :]).then_inc(bc_ev, 1)
                    k += 1
                v.wait_ge(bc_done, k + 1)
                v.tensor_copy(rb0_b[:, :], bps[k % 2][:, 0:NEXP]).then_inc(bc_ev, 1)
                v.tensor_scalar_add(d1_b[:, :], d1_b[:, :], 1.0)
                v.memset(ones_bsb[:, :], 1.0).then_inc(bc_done, 1)

            @block.gpsimd
            def _(g):
                g.dma_start(x_bf, x_in).then_inc(cast_a, 16)
                for m_src, m_dst in ((B_re, bre_b), (B_im, bim_b), (C_re, cre_b), (C_im, cim_b)):
                    g.dma_start(m_dst, m_src).then_inc(cast_a, 16)
                if "p1" not in phases:
                    for e in range(NEXP):
                        g.dma_start(up0_b[e], up0[e]).then_inc(cast_m0, 16)
                        g.dma_start(dn0_b[e], dn0[e]).then_inc(cast_m0, 16)
                    for e in range(NEXP):
                        g.dma_start(w1_b[e], w1[e]).then_inc(cast_m1, 16)
                        g.dma_start(w2_b[e], w2[e]).then_inc(cast_m1, 16)
                        g.dma_start(w3_b[e], w3[e]).then_inc(cast_m1, 16)

            _bl.__exit__(None, None, None)
        ph.release()

        nc.all_engine_barrier()

        if "p1" in phases:
            _emit_p1(nc, stack, ctx, glob, cast_done, cnt,
                     consts_sb, C_AP, d1_b, ones_sb, nus, ths, gls)

        if "p2" in phases:
            _emit_moe(nc, stack, ctx, glob, cast_done, cnt, consts_sb, C_AP,
                      layer=0, rb0_b=rb0_b)

        if "p3" in phases:
            _emit_taylor(nc, stack, ctx, glob, cast_done, cnt, consts_sb, C_AP,
                         ones_sb)

        if "p5" in phases:
            _emit_attn(nc, stack, ctx, glob, cast_done, cnt, consts_sb, C_AP,
                       ones_sb, cmask_sb)

        if "p6" in phases:
            _emit_moe1(nc, stack, ctx, glob, cast_done, cnt, consts_sb, C_AP,
                       ones_sb)

    return nc



def ln_stats_dve(v, vs, pre, W, sq_t, mean_raw, var_raw, t1, var_t):
    v.reduce_sum(mean_raw[:, :], pre, axis=AX)
    v.tensor_tensor(sq_t, pre, pre, ALU.mult)
    v.reduce_sum(var_raw[:, :], sq_t, axis=AX)
    vs.mark(v.tensor_tensor(t1[:, :], mean_raw[:, :], mean_raw[:, :], ALU.mult))
    vs.wait(v)
    vs.mark(v.tensor_scalar(var_t[:, :], var_raw[:, :], 1.0 / W, None, ALU.mult))
    vs.wait(v)
    return v.scalar_tensor_tensor(var_t[:, :], t1[:, :], -1.0 / (W * W), var_t[:, :],
                                  ALU.mult, ALU.add)


def ln_rstd_act(sc, var_t, rstd_t, t_act, C_AP):
    sc.activation(t_act[:, :], var_t[:, :], AF.Ln, bias=C_AP(C_EPSLN))
    return sc.activation(rstd_t[:, :], t_act[:, :], AF.Exp, scale=C_AP(C_NEGHALF))


def ln_apply_dve(v, vs, pre, out_ap, mean_raw, rstd_t, nm_t, W):
    vs.mark(v.scalar_tensor_tensor(nm_t[:, :], mean_raw[:, :], -1.0 / W, rstd_t[:, :],
                                   ALU.mult, ALU.mult))
    vs.wait(v)
    return v.tensor_scalar(out_ap, pre, rstd_t[:, 0:1], nm_t[:, 0:1], ALU.mult, ALU.add)



def _emit_p1(nc, stack, ctx, glob, cast_done, cnt, consts_sb, C_AP, d1_b, ones_sb,
             nus_t, ths_t, gls_t):
    x_in, x_bf = ctx["x_in"], ctx["x_bf"]
    bre_b, bim_b = ctx["bre_b"], ctx["bim_b"]
    xs_re, xs_im = ctx["xs_re"], ctx["xs_im"]
    up0, dn0, up0_b, dn0_b = ctx["up0"], ctx["dn0"], ctx["up0_b"], ctx["dn0_b"]
    iota_bc = ctx["iota_bc"]
    cast_a, cast_m0 = cast_done["a"], cast_done["m0"]

    with ExitStack() as pst:
        ph = Ph(nc, pst, "p1", glob.sempool)
        Bt = {pl: [ph.sb(f"bt{pl}{h}", [128, P], BF16) for h in range(8)] for pl in (0, 1)}
        xfm = [ph.sb(f"xfm{b}", [128, 2048], BF16) for b in range(2)]
        iota_sb = ph.sb("iota", [128, C], F32)
        cos_t = ph.sb("cos_t", [128, C], F32)
        sin_t = ph.sb("sin_t", [128, C], F32)
        cosg = ph.sb("cosg", [128, C], F32)
        sing = ph.sb("sing", [128, C], F32)
        rbc = ph.sb("rbc", [128, C], F32)
        tba = ph.sb("tba", [128, C], F32)
        tbb = ph.sb("tbb", [128, C], F32)
        tbi = ph.sb("tbi", [128, C], I32)
        ur = ph.sb("ur", [128, C], F32)
        ui = ph.sb("ui", [128, C], F32)
        t1 = ph.sb("t1", [128, C], F32)
        t2 = ph.sb("t2", [128, C], F32)
        g1 = ph.sb("g1", [128, C], F32)
        g2 = ph.sb("g2", [128, C], F32)
        gpt1 = ph.sb("gpt1", [128, C], F32)
        gpt2 = ph.sb("gpt2", [128, C], F32)
        w_re = [ph.sb(f"wre{b}", [128, C], F32) for b in range(2)]
        w_im = [ph.sb(f"wim{b}", [128, C], F32) for b in range(2)]
        xsre_buf = ph.sb("xsreb", [128, NTOK], BF16)
        xsim_buf = ph.sb("xsimb", [128, NTOK], BF16)
        nu8 = ph.sb("nu8", [128, NCH], F32)
        r8 = ph.sb("r8", [128, NCH], F32)
        th8 = ph.sb("th8", [128, NCH], F32)
        gam8 = ph.sb("gam8", [128, NCH], F32)
        thC8 = ph.sb("thC8", [128, NCH], F32)
        psiC8 = ph.sb("psiC8", [128, NCH], F32)
        phiC8 = ph.sb("phiC8", [128, NCH], F32)
        aphiC8 = ph.sb("aphiC8", [128, NCH], F32)
        mC8 = ph.sb("mC8", [128, NCH], F32)
        kC8i = ph.sb("kC8i", [128, NCH], I32)
        kC8f = ph.sb("kC8f", [128, NCH], F32)
        sinC8 = ph.sb("sinC8", [128, NCH], F32)
        cosC8 = ph.sb("cosC8", [128, NCH], F32)
        nsC8 = ph.sb("nsC8", [128, NCH], F32)
        sA = ph.sb("sA", [128, 1], F32)
        sB = ph.sb("sB", [128, 1], F32)
        init_re = ph.sb("init_re", [128, 1], F32)
        init_im = ph.sb("init_im", [128, 1], F32)
        ps_re = [ph.ps(f"psr{i}", [128, C], F32) for i in range(4)]
        ps_im = [ph.ps(f"psi{i}", [128, C], F32) for i in range(4)]

        ld_bt = ph.sem("ld_bt")
        ld_io = ph.sem("ld_io")
        ld_xfm = [ph.sem(f"ld_xfm{b}") for b in range(2)]
        pe_hc = ph.sem("pe_hc")
        rot_done = ph.sem("rot_done")
        scan_done = ph.sem("scan_done")
        chunk_done = ph.sem("chunk_done")
        gp_done = ph.sem("gp_done")
        st1 = ph.sem("st1")
        prep_a = ph.sem("prep_a")
        prep_b = ph.sem("prep_b")
        prep_c = ph.sem("prep_c")
        tb_v = ph.sem("tb_v")
        tb_a = ph.sem("tb_a")
        vs = VS(ph, "vs")

        _bl = nc.Block(); block = _bl.__enter__()

        @block.sync
        def _(s):
            s.wait_ge(cast_a, cnt["a"])
            for h in range(8):
                s.dma_start(Bt[0][h][:, :], bre_b[0:P, h * 128:(h + 1) * 128],
                            transpose=True).then_inc(ld_bt, 16)
            for h in range(8):
                s.dma_start(Bt[1][h][:, :], bim_b[0:P, h * 128:(h + 1) * 128],
                            transpose=True).then_inc(ld_bt, 16)
            for pg in range(8):
                for half in range(2):
                    for hc in range(8):
                        gi = (pg * 2 + half) * 8 + hc
                        if gi >= 2:
                            s.wait_ge(pe_hc, gi - 1)
                        s.dma_start(xfm[gi % 2][:, :],
                                    x_bf[half * 2048:(half + 1) * 2048,
                                         hc * 128:(hc + 1) * 128],
                                    transpose=True).then_inc(ld_xfm[gi % 2], 16)

        @block.scalar
        def _(sc):
            sc.dma_start(iota_sb[:, :], iota_bc).then_inc(ld_io, 16)
            sc.activation(nu8[:, :], nus_t[:, :], AF.Exp)
            sc.activation(r8[:, :], nu8[:, :], AF.Exp, scale=C_AP(C_NEG1))
            sc.activation(th8[:, :], ths_t[:, :], AF.Exp)
            sc.activation(gam8[:, :], gls_t[:, :], AF.Exp).then_inc(prep_a, 1)
            sc.wait_ge(prep_b, 1)
            sc.activation(aphiC8[:, :], phiC8[:, :], AF.Abs)
            sc.activation(sinC8[:, :], phiC8[:, :], AF.Sin)
            sc.activation(cosC8[:, :], aphiC8[:, :], AF.Sin,
                          scale=C_AP(C_NEG1), bias=C_AP(C_HALFPI)).then_inc(prep_c, 1)
            for pg in range(9):
                if pg >= 1:
                    sc.wait_ge(chunk_done, pg * 8)
                    sc.wait_ge(gp_done, pg * 8)
                    sc.dma_start(xs_re[(pg - 1) * 128:pg * 128, :],
                                 xsre_buf[:, :]).then_inc(st1, 16)
                    sc.dma_start(xs_im[(pg - 1) * 128:pg * 128, :],
                                 xsim_buf[:, :]).then_inc(st1, 16)
                if pg < 8:
                    sc.wait_ge(tb_v, pg + 1)
                    sc.activation(tbb[:, :], tba[:, :], AF.Abs)
                    sc.activation(sin_t[:, :], tba[:, :], AF.Sin)
                    sc.activation(cos_t[:, :], tbb[:, :], AF.Sin,
                                  scale=C_AP(C_NEG1), bias=C_AP(C_HALFPI)).then_inc(tb_a, 1)
            sc.wait_ge(st1, 32 * 8)

        @block.tensor
        def _(t):
            t.wait_ge(ld_bt, 16 * 16)
            for pg in range(8):
                for half in range(2):
                    hi = pg * 2 + half
                    for hc in range(8):
                        gi = hi * 8 + hc
                        t.wait_ge(ld_xfm[gi % 2], 16 * (gi // 2 + 1))
                        if hc == 0 and hi >= 1:
                            t.wait_ge(rot_done, 4 * hi)
                        last = None
                        for nch in range(4):
                            last = t.matmul(ps_re[nch][:, :],
                                            Bt[0][hc][:, pg * 128:(pg + 1) * 128],
                                            xfm[gi % 2][:, nch * C:(nch + 1) * C],
                                            start=(hc == 0), stop=(hc == 7))
                            last = t.matmul(ps_im[nch][:, :],
                                            Bt[1][hc][:, pg * 128:(pg + 1) * 128],
                                            xfm[gi % 2][:, nch * C:(nch + 1) * C],
                                            start=(hc == 0), stop=(hc == 7))
                        last.then_inc(pe_hc, 1)

        @block.vector
        def _(v):
            v.wait_ge(prep_a, 1)
            vs.mark(v.tensor_single_scalar(thC8[:, :], th8[:, :], 512.0, ALU.mult))
            vs.wait(v)
            vs.mark(v.tensor_single_scalar(psiC8[:, :], thC8[:, :], 1.0 / TWO_PI, ALU.mult))
            vs.wait(v)
            vs.mark(v.tensor_copy(kC8i[:, :], psiC8[:, :]))
            vs.wait(v)
            vs.mark(v.tensor_copy(kC8f[:, :], kC8i[:, :]))
            vs.wait(v)
            vs.mark(v.scalar_tensor_tensor(psiC8[:, :], kC8f[:, :], -TWO_PI, thC8[:, :],
                                           ALU.mult, ALU.add))
            vs.wait(v)
            vs.mark(v.tensor_single_scalar(mC8[:, :], psiC8[:, :], float(np.pi), ALU.is_gt))
            vs.wait(v)
            v.scalar_tensor_tensor(phiC8[:, :], mC8[:, :], -TWO_PI, psiC8[:, :],
                                   ALU.mult, ALU.add).then_inc(prep_b, 1)
            v.wait_ge(prep_c, 1)
            vs.mark(v.tensor_scalar_mul(nsC8[:, :], sinC8[:, :], -1.0))
            vs.wait(v)
            v.wait_ge(ld_io, 16)
            for pg in range(8):
                g_ap = lambda tile: tile[:, pg:pg + 1]
                if pg >= 1:
                    v.wait_ge(gp_done, pg * 8)
                    v.wait_ge(st1, 32 * pg)
                v.tensor_scalar(tba[:, :], iota_sb[:, :], g_ap(th8), None, ALU.mult)
                v.tensor_single_scalar(tbb[:, :], tba[:, :], 1.0 / TWO_PI, ALU.mult)
                v.tensor_copy(tbi[:, :], tbb[:, :])
                v.tensor_copy(tbb[:, :], tbi[:, :])
                v.scalar_tensor_tensor(tba[:, :], tbb[:, :], -TWO_PI, tba[:, :],
                                       ALU.mult, ALU.add)
                v.tensor_single_scalar(tbb[:, :], tba[:, :], float(np.pi), ALU.is_gt)
                v.scalar_tensor_tensor(tba[:, :], tbb[:, :], -TWO_PI, tba[:, :],
                                       ALU.mult, ALU.add).then_inc(tb_v, 1)
                v.tensor_scalar(rbc[:, :], iota_sb[:, :], 0.0, g_ap(r8), ALU.mult, ALU.add)
                v.wait_ge(tb_a, pg + 1)
                v.tensor_scalar(cosg[:, :], cos_t[:, :], g_ap(gam8), None, ALU.mult)
                v.tensor_scalar(sing[:, :], sin_t[:, :], g_ap(gam8), None, ALU.mult)
                for c in range(8):
                    half = c // 4
                    nch = c % 4
                    gc = pg * 8 + c
                    v.wait_ge(pe_hc, (pg * 2 + half + 1) * 8)
                    pr, pi_ = ps_re[nch], ps_im[nch]
                    v.tensor_tensor(t1[:, :], pr[:, :], cosg[:, :], ALU.mult)
                    v.tensor_tensor(t2[:, :], pi_[:, :], sing[:, :], ALU.mult)
                    v.tensor_tensor(ur[:, :], t1[:, :], t2[:, :], ALU.add)
                    v.tensor_tensor(t1[:, :], pi_[:, :], cosg[:, :], ALU.mult)
                    v.tensor_tensor(t2[:, :], pr[:, :], sing[:, :], ALU.mult)
                    v.tensor_tensor(ui[:, :], t1[:, :], t2[:, :], ALU.subtract).then_inc(rot_done, 1)
                    if gc >= 2:
                        v.wait_ge(gp_done, gc - 1)
                    wi_re, wi_im = w_re[c % 2], w_im[c % 2]
                    init_r = init_re[:, 0:1] if c > 0 else 0.0
                    init_i = init_im[:, 0:1] if c > 0 else 0.0
                    v.tensor_tensor_scan(wi_re[:, :], rbc[:, :], ur[:, :], init_r,
                                         ALU.mult, ALU.add)
                    v.tensor_tensor_scan(wi_im[:, :], rbc[:, :], ui[:, :], init_i,
                                         ALU.mult, ALU.add).then_inc(scan_done, 1)
                    if c < 7:
                        v.wait_ge(scan_done, gc + 1)
                        vs.mark(v.tensor_scalar(sA[:, :], wi_re[:, C - 1:C],
                                                g_ap(cosC8), None, ALU.mult))
                        vs.mark(v.tensor_scalar(sB[:, :], wi_im[:, C - 1:C],
                                                g_ap(cosC8), None, ALU.mult))
                        vs.wait(v)
                        v.scalar_tensor_tensor(init_re[:, :], wi_im[:, C - 1:C],
                                               g_ap(nsC8), sA[:, :], ALU.mult, ALU.add)
                        v.scalar_tensor_tensor(init_im[:, :], wi_re[:, C - 1:C],
                                               g_ap(sinC8), sB[:, :], ALU.mult, ALU.add)
                    v.tensor_tensor(g1[:, :], wi_re[:, :], cos_t[:, :], ALU.mult)
                    v.tensor_tensor(g2[:, :], wi_im[:, :], sin_t[:, :], ALU.mult)
                    v.tensor_tensor(xsre_buf[:, c * C:(c + 1) * C], g1[:, :], g2[:, :],
                                    ALU.subtract).then_inc(chunk_done, 1)

        @block.gpsimd
        def _(g):
            for pg in range(8):
                e = pg
                g.dma_start(up0_b[e], up0[e]).then_inc(cast_m0, 16)
                g.dma_start(dn0_b[e], dn0[e]).then_inc(cast_m0, 16)
                for c in range(8):
                    gc = pg * 8 + c
                    g.wait_ge(scan_done, gc + 1)
                    wi_re, wi_im = w_re[c % 2], w_im[c % 2]
                    g.tensor_tensor(gpt1[:, :], wi_im[:, :], cos_t[:, :], ALU.mult)
                    g.tensor_tensor(gpt2[:, :], wi_re[:, :], sin_t[:, :], ALU.mult)
                    g.tensor_tensor(xsim_buf[:, c * C:(c + 1) * C], gpt1[:, :],
                                    gpt2[:, :], ALU.add).then_inc(gp_done, 1)

        _bl.__exit__(None, None, None)
        ph.release()

    _emit_p1_pass2(nc, stack, ctx, glob, cast_done, cnt, C_AP, d1_b)



def _emit_p1_pass2(nc, stack, ctx, glob, cast_done, cnt, C_AP, d1_b):
    x_in = ctx["x_in"]
    cre_b, cim_b = ctx["cre_b"], ctx["cim_b"]
    xs_re, xs_im = ctx["xs_re"], ctx["xs_im"]
    s1_f, s1_b = ctx["s1_f"], ctx["s1_b"]
    w1, w2, w3 = ctx["w1"], ctx["w2"], ctx["w3"]
    w1_b, w2_b, w3_b = ctx["w1_b"], ctx["w2_b"], ctx["w3_b"]
    cast_m1 = cast_done["m1"]
    NSPAN = 8
    SW = 512

    with ExitStack() as pst:
        ph = Ph(nc, pst, "p1b", glob.sempool)
        Ct = {pl: [ph.sb(f"ct{pl}{pc}", [128, D], BF16) for pc in range(8)]
              for pl in (0, 1)}
        xst = [[[ph.sb(f"xs{pl}{pg}{b}", [128, SW], BF16) for b in range(2)]
                for pg in range(8)] for pl in (0, 1)]
        xtile = [ph.sb(f"xt{b}", [128, D], F32) for b in range(4)]
        pre4 = [ph.sb(f"pre{i}", [128, D], F32) for i in range(4)]
        sq_t = ph.sb("sq", [128, D], F32)
        mean_raw = ph.sb("meanr", [128, 1], F32)
        var_raw = ph.sb("varr", [128, 1], F32)
        var_t = ph.sb("var", [128, 1], F32)
        rstd_t = ph.sb("rstd", [128, 1], F32)
        nm_t = ph.sb("nm", [128, 1], F32)
        tt1 = ph.sb("tt1", [128, 1], F32)
        tact = ph.sb("tact", [128, 1], F32)
        of32 = [ph.sb(f"of{b}", [128, 4 * D], F32) for b in range(2)]
        obf = [ph.sb(f"ob{b}", [128, 4 * D], BF16) for b in range(2)]
        psc = [ph.ps(f"psc{i}", [128, SW], F32) for i in range(8)]

        ld_ct = ph.sem("ld_ct")
        ld_xs = [ph.sem(f"ld_xs{b}") for b in range(2)]
        ld_x = [ph.sem(f"ld_x{b}") for b in range(4)]
        neg_done = ph.sem("neg")
        mm_done = ph.sem("mm")
        evac_done = ph.sem("evac")
        ln_v = ph.sem("ln_v")
        ln_r = ph.sem("ln_r")
        stage_done = ph.sem("stage")
        st2 = ph.sem("st2")
        vs = VS(ph, "vs")

        _bl = nc.Block(); block = _bl.__enter__()

        @block.sync
        def _(s):
            for pc in range(8):
                s.dma_start(Ct[0][pc][:, :], cre_b[0:D, pc * 128:(pc + 1) * 128],
                            transpose=True).then_inc(ld_ct, 16)
            for pc in range(8):
                s.dma_start(Ct[1][pc][:, :], cim_b[0:D, pc * 128:(pc + 1) * 128],
                            transpose=True).then_inc(ld_ct, 16)

        def _emit_span_loads(sc, sp):
            b = sp % 2
            if sp >= 2:
                sc.wait_ge(mm_done, (sp - 1) * 8)
            for pl in range(2):
                src = xs_re if pl == 0 else xs_im
                for pg in range(8):
                    sc.dma_start(xst[pl][pg][b][:, :],
                                 src[pg * 128:(pg + 1) * 128, sp * SW:(sp + 1) * SW]
                                 ).then_inc(ld_xs[b], 16)
            for sub in range(4):
                gs = sp * 4 + sub
                if gs >= 4:
                    sc.wait_ge(evac_done, (gs - 3) * 2)
                sc.dma_start(xtile[gs % 4][:, :],
                             x_in[sp * SW + sub * 128: sp * SW + (sub + 1) * 128, :]
                             ).then_inc(ld_x[gs % 4], 16)

        @block.scalar
        def _(sc):
            _emit_span_loads(sc, 0)
            for sp in range(NSPAN + 1):
                if sp >= 1:
                    for sub in range(4):
                        gs = (sp - 1) * 4 + sub
                        sc.wait_ge(ln_v, gs + 1)
                        ln_rstd_act(sc, var_t, rstd_t, tact, C_AP).then_inc(ln_r, 1)
                if sp + 1 < NSPAN:
                    _emit_span_loads(sc, sp + 1)
                if sp >= 1:
                    sc.wait_ge(stage_done, sp)
                    sc.dma_start(s1_f[(sp - 1) * SW:sp * SW, :].rearrange("(s p) d -> p s d", p=128),
                                 of32[(sp - 1) % 2][:, :].rearrange("p (s d) -> p s d", s=4)
                                 ).then_inc(st2, 16)
                    sc.dma_start(s1_b[(sp - 1) * SW:sp * SW, :].rearrange("(s p) d -> p s d", p=128),
                                 obf[(sp - 1) % 2][:, :].rearrange("p (s d) -> p s d", s=4)
                                 ).then_inc(st2, 16)

        @block.vector
        def _(v):
            v.wait_ge(ld_ct, 16 * 16)
            inst = None
            for pc in range(8):
                inst = v.tensor_scalar_mul(Ct[1][pc][:, :], Ct[1][pc][:, :], -1.0)
            inst.then_inc(neg_done, 1)
            for sp in range(NSPAN):
                for sub in range(4):
                    gs = sp * 4 + sub
                    v.wait_ge(ld_x[gs % 4], 16 * (gs // 4 + 1))
                    v.tensor_tensor(pre4[sub][:, :], xtile[gs % 4][:, :], d1_b[:, :], ALU.mult)
                    for half in range(2):
                        v.wait_ge(mm_done, sp * 8 + sub * 2 + half + 1)
                        v.tensor_tensor(pre4[sub][:, half * SW:(half + 1) * SW],
                                        pre4[sub][:, half * SW:(half + 1) * SW],
                                        psc[sub * 2 + half][:, :], ALU.add).then_inc(evac_done, 1)
                for sub in range(4):
                    gs = sp * 4 + sub
                    ln_stats_dve(v, vs, pre4[sub][:, :], D, sq_t[:, :], mean_raw, var_raw,
                                 tt1, var_t).then_inc(ln_v, 1)
                    v.wait_ge(ln_r, gs + 1)
                    if sp >= 2 and sub == 0:
                        v.wait_ge(st2, 32 * (sp - 1))
                    ln_apply_dve(v, vs, pre4[sub][:, :], of32[sp % 2][:, sub * D:(sub + 1) * D],
                                 mean_raw, rstd_t, nm_t, D)
                    inst = v.tensor_copy(obf[sp % 2][:, sub * D:(sub + 1) * D],
                                         of32[sp % 2][:, sub * D:(sub + 1) * D])
                inst.then_inc(stage_done, 1)

        @block.tensor
        def _(t):
            t.wait_ge(ld_ct, 16 * 16)
            t.wait_ge(neg_done, 1)
            for sp in range(NSPAN):
                t.wait_ge(ld_xs[sp % 2], 16 * 16 * (sp // 2 + 1))
                for sub in range(4):
                    for half in range(2):
                        gg = sp * 8 + sub * 2 + half
                        if gg >= 8:
                            t.wait_ge(evac_done, gg - 7)
                        last = None
                        for pg in range(8):
                            for pl in range(2):
                                k = pg * 2 + pl
                                last = t.matmul(psc[sub * 2 + half][:, :],
                                                xst[pl][pg][sp % 2][:, sub * 128:(sub + 1) * 128],
                                                Ct[pl][pg][:, half * SW:(half + 1) * SW],
                                                start=(k == 0), stop=(k == 15))
                        last.then_inc(mm_done, 1)

        @block.gpsimd
        def _(g):
            for e in range(NEXP):
                g.dma_start(w1_b[e], w1[e]).then_inc(cast_m1, 16)
                g.dma_start(w2_b[e], w2[e]).then_inc(cast_m1, 16)
                g.dma_start(w3_b[e], w3[e]).then_inc(cast_m1, 16)

        _bl.__exit__(None, None, None)
        ph.release()




def _emit_moe(nc, stack, ctx, glob, cast_done, cnt, consts_sb, C_AP, layer,
              rb0_b=None, l2s_b=None, l2h_b=None):
    assert layer == 0
    s_in_b, s_in_f = ctx["s1_b"], ctx["s1_f"]
    s_out_f, s_out_b = ctx["s2_f"], ctx["s2_b"]
    up_b, dn_b = ctx["up0_b"], ctx["dn0_b"]
    r_w = ctx["r0_w"]
    cast_m0 = cast_done["m0"]
    TT_ = 512
    NSUB = 4
    NT = NTOK // TT_
    NQ = 4
    QW = DFF0 // NQ
    NUPG = 32
    NDNG = NSUB * 2

    with ExitStack() as pst:
        ph = Ph(nc, pst, "p2", glob.sempool)
        fm = [[ph.sb(f"fm{b}_{k}", [128, TT_], BF16) for k in range(8)] for b in range(2)]
        rfm = [[ph.sb(f"rfm{b}_{k}", [128, TT_], BF16) for k in range(8)] for b in range(2)]
        upT = [[ph.sb(f"upT{b}_{k}", [128, QW], BF16) for k in range(8)] for b in range(2)]
        dnT = [ph.sb(f"dnT{c}", [128, D], BF16) for c in range(32)]
        h_sb = ph.sb("h_sb", [128, NUPG * TT_], BF16)
        acc = ph.sb("acc", [128, NSUB * D], F32)
        obf = ph.sb("obf", [128, NSUB * D], BF16)
        xres = [ph.sb(f"xres{i}", [128, D], F32) for i in range(2)]
        rw_sb = ph.sb("rw", [128, 8 * NEXP], BF16)
        comb = ph.sb("comb", [128, NT * NSUB * NEXP], F32)
        lg = ph.sb("lg", [128, NEXP], F32)
        xm = ph.sb("xm", [128, NEXP], F32)
        pr_t = ph.sb("pr_t", [128, NEXP], F32)
        p2_t = ph.sb("p2_t", [128, NEXP], F32)
        m1_t = ph.sb("m1_t", [128, NEXP], F32)
        m2_t = ph.sb("m2_t", [128, NEXP], F32)
        mm_t = ph.sb("mm_t", [128, NEXP], F32)
        mx1 = ph.sb("mx1", [128, 1], F32)
        mx2 = ph.sb("mx2", [128, 1], F32)
        sm1 = ph.sb("sm1", [128, 1], F32)
        rs1 = ph.sb("rs1", [128, 1], F32)
        cn1 = ph.sb("cn1", [128, 1], F32)
        f_t = ph.sb("f_t", [128, 1], F32)
        sq_t = ph.sb("sq", [128, D], F32)
        mean_raw = ph.sb("meanr", [128, 1], F32)
        var_raw = ph.sb("varr", [128, 1], F32)
        var_t = ph.sb("var", [128, 1], F32)
        rstd_t = ph.sb("rstd", [128, 1], F32)
        nm_t = ph.sb("nm", [128, 1], F32)
        tt1 = ph.sb("tt1", [128, 1], F32)
        tact = ph.sb("tact", [128, 1], F32)
        up_ps = [ph.ps(f"up_ps{b}", [128, TT_], F32) for b in range(2)]
        dn_ps = [ph.ps(f"dn_ps{b}", [128, 512], F32) for b in range(2)]
        rt_ps = ph.ps("rt_ps", [128, NEXP], F32)

        ld_rf = [ph.sem(f"ld_rf{b}") for b in range(2)]
        ld_rw = ph.sem("ld_rw", pooled=False)
        ld_fm = [ph.sem(f"ld_fm{b}") for b in range(2)]
        ld_w = [ph.sem(f"ld_w{b}") for b in range(2)]
        ld_d = ph.sem("ld_d")
        ld_xr = ph.sem("ld_xr", pooled=False)
        rt_mm = ph.sem("rt_mm")
        rt_v = ph.sem("rt_v")
        rt_a = ph.sem("rt_a")
        rt_done = ph.sem("rt_done")
        up_done = ph.sem("up_done")
        gelu_done = ph.sem("gelu_done")
        dn_done = ph.sem("dn_done")
        evac_done = ph.sem("evac_done")
        ln_v = ph.sem("ln_v")
        ln_r = ph.sem("ln_r")
        st_o = ph.sem("st_o", pooled=False)
        stg = ph.sem("stg")
        vs = VS(ph, "vs")

        _bl = nc.Block(); block = _bl.__enter__()

        @block.sync
        def _(s):
            s.wait_ge(cast_m0, cnt["m0"])
            for T in range(NT):
                if T >= 2:
                    s.wait_ge(rt_mm, (T - 1) * NSUB)
                for k in range(8):
                    s.dma_start(rfm[T % 2][k][:, :],
                                s_in_b[T * TT_:(T + 1) * TT_, k * 128:(k + 1) * 128],
                                transpose=True).then_inc(ld_rf[T % 2], 16)
            for T in range(NT):
                if T >= 2:
                    s.wait_ge(up_done, (T - 1) * NEXP * NUPG)
                for k in range(8):
                    s.dma_start(fm[T % 2][k][:, :],
                                s_in_b[T * TT_:(T + 1) * TT_, k * 128:(k + 1) * 128],
                                transpose=True).then_inc(ld_fm[T % 2], 16)
                for x in range(NEXP):
                    p = T * NEXP + x
                    for q in range(NQ):
                        g = p * NQ + q
                        if g >= 2:
                            s.wait_ge(up_done, (g - 1) * (NUPG // NQ))
                        for k in range(8):
                            s.dma_start(upT[g % 2][k][:, :],
                                        up_b[x, q * QW:(q + 1) * QW, k * 128:(k + 1) * 128],
                                        transpose=True).then_inc(ld_w[g % 2], 16)
                    if p >= 1:
                        s.wait_ge(dn_done, p * NDNG)
                    for c in range(32):
                        s.dma_start(dnT[c][:, :],
                                    dn_b[x, 0:D, c * 128:(c + 1) * 128],
                                    transpose=True).then_inc(ld_d, 16)

        @block.scalar
        def _(sc):
            for T in range(NT):
                for sub in range(NSUB):
                    i = T * NSUB + sub
                    sc.wait_ge(rt_v, i + 1)
                    sc.activation(pr_t[:, :], xm[:, :], AF.Exp).then_inc(rt_a, 1)
            for T in range(NT):
                if T >= 1:
                    for sub in range(NSUB):
                        gs = (T - 1) * NSUB + sub
                        sc.wait_ge(ln_v, gs + 1)
                        ln_rstd_act(sc, var_t, rstd_t, tact, C_AP).then_inc(ln_r, 1)
                for x in range(NEXP):
                    pidx = T * NEXP + x
                    if pidx >= 1:
                        sc.wait_ge(dn_done, pidx * NDNG)
                    for m in range(NUPG):
                        gu = pidx * NUPG + m
                        sc.wait_ge(up_done, gu + 1)
                        sc.activation(h_sb[:, m * TT_:(m + 1) * TT_],
                                      up_ps[gu % 2][:, :], AF.Gelu).then_inc(gelu_done, 1)
            for sub in range(NSUB):
                gs = (NT - 1) * NSUB + sub
                sc.wait_ge(ln_v, gs + 1)
                ln_rstd_act(sc, var_t, rstd_t, tact, C_AP).then_inc(ln_r, 1)

        @block.tensor
        def _(t):
            t.wait_ge(ld_rw, 16)
            for T in range(NT):
                t.wait_ge(ld_rf[T % 2], 16 * 8 * (T // 2 + 1))
                for sub in range(NSUB):
                    i = T * NSUB + sub
                    if i >= 1:
                        t.wait_ge(rt_done, i)
                    last = None
                    for k in range(8):
                        last = t.matmul(rt_ps[:, :],
                                        rfm[T % 2][k][:, sub * 128:(sub + 1) * 128],
                                        rw_sb[:, k * NEXP:(k + 1) * NEXP],
                                        start=(k == 0), stop=(k == 7))
                    last.then_inc(rt_mm, 1)
            for T in range(NT):
                t.wait_ge(ld_fm[T % 2], 16 * 8 * (T // 2 + 1))
                for x in range(NEXP):
                    pidx = T * NEXP + x
                    for q in range(NQ):
                        g = pidx * NQ + q
                        t.wait_ge(ld_w[g % 2], 16 * 8 * (g // 2 + 1))
                        for mm in range(NUPG // NQ):
                            m = q * (NUPG // NQ) + mm
                            gu = pidx * NUPG + m
                            if gu >= 2:
                                t.wait_ge(gelu_done, gu - 1)
                            last = None
                            for k in range(8):
                                last = t.matmul(up_ps[gu % 2][:, :],
                                                upT[g % 2][k][:, mm * 128:(mm + 1) * 128],
                                                fm[T % 2][k][:, :],
                                                start=(k == 0), stop=(k == 7))
                            last.then_inc(up_done, 1)
                    t.wait_ge(ld_d, 16 * 32 * (pidx + 1))
                    t.wait_ge(gelu_done, (pidx + 1) * NUPG)
                    for sub in range(NSUB):
                        for nh in range(2):
                            gd = pidx * NDNG + sub * 2 + nh
                            if gd >= 2:
                                t.wait_ge(evac_done, gd - 1)
                            last = None
                            for c in range(32):
                                last = t.matmul(dn_ps[gd % 2][:, :],
                                                h_sb[:, c * TT_ + sub * 128: c * TT_ + (sub + 1) * 128],
                                                dnT[c][:, nh * 512:(nh + 1) * 512],
                                                start=(c == 0), stop=(c == 31))
                            last.then_inc(dn_done, 1)

        @block.vector
        def _(v):
            for T in range(NT):
                for sub in range(NSUB):
                    i = T * NSUB + sub
                    v.wait_ge(rt_mm, i + 1)
                    v.tensor_tensor(lg[:, :], rt_ps[:, :], rb0_b[:, :],
                                    ALU.add).then_inc(rt_done, 1)
                    v.wait_ge(rt_done, i + 1)
                    vs.mark(v.reduce_max(mx1[:, :], lg[:, :], axis=AX))
                    vs.wait(v)
                    v.tensor_scalar(xm[:, :], lg[:, :], mx1[:, 0:1], None,
                                    ALU.subtract).then_inc(rt_v, 1)
                    v.wait_ge(rt_a, i + 1)
                    vs.mark(v.reduce_sum(sm1[:, :], pr_t[:, :], axis=AX))
                    vs.wait(v)
                    vs.mark(v.reciprocal(rs1[:, :], sm1[:, :]))
                    vs.wait(v)
                    vs.mark(v.tensor_scalar(pr_t[:, :], pr_t[:, :], rs1[:, 0:1], None, ALU.mult))
                    vs.wait(v)
                    vs.mark(v.reduce_max(mx1[:, :], pr_t[:, :], axis=AX))
                    vs.wait(v)
                    vs.mark(v.tensor_scalar(m1_t[:, :], pr_t[:, :], mx1[:, 0:1], None, ALU.is_ge))
                    vs.wait(v)
                    vs.mark(v.tensor_tensor(p2_t[:, :], pr_t[:, :], m1_t[:, :], ALU.mult))
                    vs.wait(v)
                    v.tensor_tensor(p2_t[:, :], pr_t[:, :], p2_t[:, :], ALU.subtract)
                    vs.mark(v.reduce_sum(cn1[:, :], m1_t[:, :], axis=AX))
                    vs.wait(v)
                    vs.mark(v.reduce_max(mx2[:, :], p2_t[:, :], axis=AX))
                    vs.mark(v.tensor_scalar(f_t[:, :], cn1[:, :], -1.0, 2.0, ALU.mult, ALU.add))
                    vs.wait(v)
                    vs.mark(v.tensor_scalar(m2_t[:, :], p2_t[:, :], mx2[:, 0:1], None, ALU.is_ge))
                    vs.mark(v.tensor_single_scalar(f_t[:, :], f_t[:, :], 0.0, ALU.max))
                    vs.wait(v)
                    vs.mark(v.scalar_tensor_tensor(mm_t[:, :], m2_t[:, :], f_t[:, 0:1],
                                                   m1_t[:, :], ALU.mult, ALU.add))
                    vs.wait(v)
                    v.tensor_tensor(comb[:, i * NEXP:(i + 1) * NEXP], pr_t[:, :],
                                    mm_t[:, :], ALU.mult)
            for T in range(NT):
                for x in range(NEXP):
                    pidx = T * NEXP + x
                    for sub in range(NSUB):
                        for nh in range(2):
                            gd = pidx * NDNG + sub * 2 + nh
                            v.wait_ge(dn_done, gd + 1)
                            if x == 0 and sub == 0 and nh == 0 and T >= 1:
                                v.wait_ge(st_o, 32 * T)
                            dst = acc[:, sub * D + nh * 512: sub * D + (nh + 1) * 512]
                            cap = comb[:, (T * NSUB + sub) * NEXP + x:
                                       (T * NSUB + sub) * NEXP + x + 1]
                            if x == 0:
                                v.tensor_scalar(dst, dn_ps[gd % 2][:, :], cap, None,
                                                ALU.mult).then_inc(evac_done, 1)
                            else:
                                v.scalar_tensor_tensor(dst, dn_ps[gd % 2][:, :], cap,
                                                       dst, ALU.mult, ALU.add).then_inc(evac_done, 1)
                inst = None
                for sub in range(NSUB):
                    gs = T * NSUB + sub
                    accs = acc[:, sub * D:(sub + 1) * D]
                    v.wait_ge(ld_xr, 16 * (gs + 1))
                    v.tensor_tensor(accs, accs, xres[gs % 2][:, :], ALU.add)
                    ln_stats_dve(v, vs, accs, D, sq_t[:, :], mean_raw, var_raw,
                                 tt1, var_t).then_inc(ln_v, 1)
                    v.wait_ge(ln_r, gs + 1)
                    ln_apply_dve(v, vs, accs, accs, mean_raw, rstd_t, nm_t, D)
                    inst = v.tensor_copy(obf[:, sub * D:(sub + 1) * D], accs)
                inst.then_inc(stg, 1)

        @block.gpsimd
        def _(g):
            g.dma_start(rw_sb[:, :].rearrange("p (k e) -> p k e", k=8),
                        r_w.rearrange("(k p) e -> p k e", p=128)).then_inc(ld_rw, 16)
            for T in range(NT + 1):
                if T >= 1:
                    g.wait_ge(stg, T)
                    g.dma_start(s_out_f[(T - 1) * TT_:T * TT_, :].rearrange("(s p) d -> p s d", p=128),
                                acc[:, :].rearrange("p (s d) -> p s d", s=NSUB)).then_inc(st_o, 16)
                    g.dma_start(s_out_b[(T - 1) * TT_:T * TT_, :].rearrange("(s p) d -> p s d", p=128),
                                obf[:, :].rearrange("p (s d) -> p s d", s=NSUB)).then_inc(st_o, 16)
                if T < NT:
                    for sub in range(NSUB):
                        gs = T * NSUB + sub
                        if gs >= 2:
                            g.wait_ge(ln_v, gs - 1)
                        g.dma_start(xres[gs % 2][:, :],
                                    s_in_f[T * TT_ + sub * 128: T * TT_ + (sub + 1) * 128, :]
                                    ).then_inc(ld_xr, 16)
            g.wait_ge(st_o, 32 * NT)

        _bl.__exit__(None, None, None)
        ph.release()




def _emit_taylor(nc, stack, ctx, glob, cast_done, cnt, consts_sb, C_AP, ones_sb):
    s_in_f, s_in_b = ctx["s2_f"], ctx["s2_b"]
    s_out_f, s_out_b = ctx["s3_f"], ctx["s3_b"]
    tmp_b = ctx["tmp_b"]
    t_gamma, t_wq, t_wkv, t_wout = ctx["t_gamma"], ctx["t_wq"], ctx["t_wkv"], ctx["t_wout"]
    ident_b = ctx["ident_b"]
    NH = 8
    DH = 16
    FB = 384
    NSP = 8
    NCHK = 32

    with ExitStack() as pst:
        ph = Ph(nc, pst, "p3", glob.sempool)
        tg_b = ph.sb("tg_b", [128, D], F32)
        row_st = ph.sb("row_st", [1, D], F32)
        id_sb = ph.sb("id_sb", [128, 128], BF16)
        wqkv = ph.sb("wqkv", [128, 8 * FB], BF16)
        wout_sb = ph.sb("wout", [128, D], BF16)
        fm = [[ph.sb(f"fm{b}_{k}", [128, 512], BF16) for k in range(8)] for b in range(2)]
        xa = [ph.sb(f"xa{i}", [128, D], F32) for i in range(2)]
        nrm = ph.sb("nrm", [128, D], F32)
        stg_a = ph.sb("stg_a", [128, 4 * D], BF16)
        q_all = ph.sb("q_all", [128, NCHK * 128], BF16)
        k_bf = ph.sb("k_bf", [128, 128], BF16)
        va = ph.sb("va", [128, NH * 17], BF16)
        Fk = ph.sb("Fk", [128, NH * FB], BF16)
        Fq = ph.sb("Fq", [128, NH * FB], BF16)
        fqf = ph.sb("fqf", [128, 24 * 128], BF16)
        kvf = ph.sb("kvf", [128, 3 * 136], BF16)
        ctx_t = ph.sb("ctx_t", [128, 128], BF16)
        ctxf = ph.sb("ctxf", [128, 128], BF16)
        xr = [ph.sb(f"xr{i}", [128, D], F32) for i in range(2)]
        pre_t = ph.sb("pre_t", [128, D], F32)
        stg_f = ph.sb("stg_f", [128, 4 * D], F32)
        stg_b = ph.sb("stg_b", [128, 4 * D], BF16)
        dmx = ph.sb("dmx", [128, 1], F32)
        rec = ph.sb("rec", [128, 1], F32)
        ssum = ph.sb("ssum", [128, 1], F32)
        rn_t = ph.sb("rn_t", [128, 1], F32)
        sq_t = ph.sb("sq", [128, D], F32)
        mean_raw = ph.sb("meanr", [128, 1], F32)
        var_raw = ph.sb("varr", [128, 1], F32)
        var_t = ph.sb("var", [128, 1], F32)
        rstd_t = ph.sb("rstd", [128, 1], F32)
        nm_t = ph.sb("nm", [128, 1], F32)
        tt1 = ph.sb("tt1", [128, 1], F32)
        tact = ph.sb("tact", [128, 1], F32)
        qkv_ps = [ph.ps(f"qkv{b}", [128, 512], F32) for b in range(2)]
        tr_ps = [ph.ps(f"tr{b}", [128, 128], BF16) for b in range(2)]
        _kv_cm = [nc.psum_tensor(f"p3_kv{s}", [128, NH * 17], F32) for s in range(3)]
        kv_ps = [cm.__enter__() for cm in _kv_cm]
        for cm in reversed(_kv_cm):
            cm.__exit__(None, None, None)
        out_ps = ph.ps("out_ps", [128, NH * 17], F32)
        wo_ps = ph.ps("wo_ps", [128, D], F32)

        ld_g = ph.sem("ld_g", pooled=False)
        ld_w = ph.sem("ld_w", pooled=False)
        ld_xa = ph.sem("ld_xa", pooled=False)
        ld_fm = [ph.sem(f"ld_fm{b}") for b in range(2)]
        ld_xr = ph.sem("ld_xr", pooled=False)
        bcd = ph.sem("bcd")
        bce = ph.sem("bce")
        an_v = ph.sem("an_v")
        xa_done = ph.sem("xa_done")
        an_a = ph.sem("an_a")
        stga = ph.sem("stga")
        st_a = ph.sem("st_a", pooled=False)
        qkv_mm = ph.sem("qkv_mm")
        qe_done = ph.sem("qe")
        fb_done = ph.sem("fb")
        kv_done = ph.sem("kv")
        kve = ph.sem("kve")
        fq_done = ph.sem("fq")
        tr_mm = ph.sem("tr_mm")
        tr_ev = ph.sem("tr_ev")
        om_done = ph.sem("om")
        nrm_done = ph.sem("nrmd")
        ctr_mm = ph.sem("ctr")
        ctr_ev = ph.sem("ctre")
        wo_done = ph.sem("wo")
        fe_done = ph.sem("fe")
        ln_v = ph.sem("ln_v")
        ln_r = ph.sem("ln_r")
        stgf = ph.sem("stgf")
        st_o = ph.sem("st_o", pooled=False)
        vs = VS(ph, "vs")

        _bl = nc.Block(); block = _bl.__enter__()

        @block.sync
        def _(s):
            for sp in range(NSP):
                if sp >= 2:
                    s.wait_ge(qkv_mm, (sp - 1) * 4)
                s.wait_ge(st_a, 16 * (sp + 1))
                for k in range(8):
                    s.dma_start(fm[sp % 2][k][:, :],
                                tmp_b[sp * 512:(sp + 1) * 512, k * 128:(k + 1) * 128],
                                transpose=True).then_inc(ld_fm[sp % 2], 16)

        @block.gpsimd
        def _(g):
            g.dma_start(wqkv[:, :].rearrange("p (k f) -> p k f", k=8)[:, :, 0:128],
                        t_wq.rearrange("(k p) q -> p k q", p=128)).then_inc(ld_w, 16)
            g.dma_start(wqkv[:, :].rearrange("p (k f) -> p k f", k=8)[:, :, 128:384],
                        t_wkv.rearrange("(k p) q -> p k q", p=128)).then_inc(ld_w, 16)
            g.dma_start(wout_sb[:, :], t_wout).then_inc(ld_w, 16)
            g.dma_start(row_st[:, :], t_gamma).then_inc(ld_g, 16)
            g.dma_start(id_sb[:, :], ident_b).then_inc(ld_w, 16)
            for sp in range(NSP + 1):
                if sp >= 1:
                    g.wait_ge(stga, sp)
                    g.dma_start(tmp_b[(sp - 1) * 512:sp * 512, :].rearrange("(s p) d -> p s d", p=128),
                                stg_a[:, :].rearrange("p (s d) -> p s d", s=4)).then_inc(st_a, 16)
                if sp < NSP:
                    for sub in range(4):
                        gs = sp * 4 + sub
                        if gs >= 2:
                            g.wait_ge(xa_done, gs - 1)
                        g.dma_start(xa[gs % 2][:, :],
                                    s_in_f[gs * 128:(gs + 1) * 128, :]).then_inc(ld_xa, 16)
            for sp in range(NSP + 1):
                if sp >= 1:
                    g.wait_ge(stgf, sp)
                    g.dma_start(s_out_f[(sp - 1) * 512:sp * 512, :].rearrange("(s p) d -> p s d", p=128),
                                stg_f[:, :].rearrange("p (s d) -> p s d", s=4)).then_inc(st_o, 16)
                    g.dma_start(s_out_b[(sp - 1) * 512:sp * 512, :].rearrange("(s p) d -> p s d", p=128),
                                stg_b[:, :].rearrange("p (s d) -> p s d", s=4)).then_inc(st_o, 16)
                if sp < NSP:
                    for sub in range(4):
                        i = sp * 4 + sub
                        if i >= 2:
                            g.wait_ge(fe_done, i - 1)
                        g.dma_start(xr[i % 2][:, :],
                                    s_in_f[i * 128:(i + 1) * 128, :]).then_inc(ld_xr, 16)
            g.wait_ge(st_o, 32 * NSP)

        @block.tensor
        def _(t):
            t.wait_ge(ld_g, 16)
            for h in range(2):
                t.matmul(qkv_ps[h][:, :], ones_sb[:, :], row_st[:, h * 512:(h + 1) * 512],
                         start=True, stop=True).then_inc(bcd, 1)
            t.wait_ge(ld_w, 16 * 4)
            t.wait_ge(bce, 2)
            for i in range(NCHK):
                sp = i // 4
                t.wait_ge(ld_fm[sp % 2], 16 * 8 * (sp // 2 + 1))
                if i >= 2:
                    t.wait_ge(qe_done, i - 1)
                last = None
                for k in range(8):
                    last = t.matmul(qkv_ps[i % 2][:, 0:FB],
                                    fm[sp % 2][k][:, (i % 4) * 128:(i % 4 + 1) * 128],
                                    wqkv[:, k * FB:(k + 1) * FB],
                                    start=(k == 0), stop=(k == 7))
                last.then_inc(qkv_mm, 1)
                t.wait_ge(fb_done, i + 1)
                last = None
                for h in range(NH):
                    for sl in range(3):
                        last = t.matmul(kv_ps[sl][:, h * 17:(h + 1) * 17],
                                        Fk[:, h * FB + sl * 128: h * FB + (sl + 1) * 128],
                                        va[:, h * 17:(h + 1) * 17],
                                        start=(i == 0 and h == 0),
                                        stop=(i == NCHK - 1 and h == NH - 1))
                last.then_inc(kv_done, 1)
            for i in range(NCHK):
                t.wait_ge(fq_done, i + 1)
                for tsl in range(24):
                    gt = i * 24 + tsl
                    if gt >= 2:
                        t.wait_ge(tr_ev, gt - 1)
                    if tsl == 0 and i >= 1:
                        t.wait_ge(ctr_ev, i)
                    t.transpose(tr_ps[gt % 2][:, :], Fq[:, tsl * 128:(tsl + 1) * 128],
                                id_sb[:, :]).then_inc(tr_mm, 1)
                if i == 0:
                    t.wait_ge(kve, 1)
                t.wait_ge(tr_ev, i * 24 + 24)
                if i >= 1:
                    t.wait_ge(nrm_done, i * 8)
                last = None
                for h in range(NH):
                    for sl in range(3):
                        last = t.matmul(out_ps[:, h * 17:(h + 1) * 17],
                                        fqf[:, (h * 3 + sl) * 128:(h * 3 + sl + 1) * 128],
                                        kvf[:, sl * 136 + h * 17: sl * 136 + (h + 1) * 17],
                                        start=(sl == 0), stop=(sl == 2))
                    last.then_inc(om_done, 1)
                t.wait_ge(nrm_done, i * 8 + 8)
                if i >= 1:
                    t.wait_ge(ctr_ev, i)
                t.transpose(tr_ps[0][:, :], ctx_t[:, :], id_sb[:, :]).then_inc(ctr_mm, 1)
                t.wait_ge(ctr_ev, i + 1)
                if i >= 1:
                    t.wait_ge(fe_done, i)
                t.matmul(wo_ps[:, 0:512], ctxf[:, :], wout_sb[:, 0:512],
                         start=True, stop=True)
                t.matmul(wo_ps[:, 512:1024], ctxf[:, :], wout_sb[:, 512:1024],
                         start=True, stop=True).then_inc(wo_done, 1)

        @block.scalar
        def _(sc):
            for gs in range(32):
                sc.wait_ge(an_v, gs + 1)
                sc.activation(tact[:, :], ssum[:, :], AF.Ln)
                sc.activation(rn_t[:, :], tact[:, :], AF.Exp,
                              scale=C_AP(C_HALF)).then_inc(an_a, 1)
            for i in range(32):
                sc.wait_ge(ln_v, i + 1)
                ln_rstd_act(sc, var_t, rstd_t, tact, C_AP).then_inc(ln_r, 1)

        @block.vector
        def _(v):
            for h in range(2):
                v.wait_ge(bcd, h + 1)
                v.tensor_copy(tg_b[:, h * 512:(h + 1) * 512], qkv_ps[h][:, :]).then_inc(bce, 1)
            v.memset(va[:, :], 0.0)
            v.memset(va[:, :].rearrange("p (h e) -> p h e", h=NH)[:, :, 16:17], 1.0)
            v.memset(Fk[:, :], 0.0)
            v.memset(Fq[:, :], 0.0)
            v.memset(Fk[:, :].rearrange("p (h f) -> p h f", h=NH)[:, :, 272:273], 1.0)
            v.memset(Fq[:, :].rearrange("p (h f) -> p h f", h=NH)[:, :, 272:273], 1.0)
            for gs in range(32):
                v.wait_ge(ld_xa, 16 * (gs + 1))
                xat = xa[gs % 2]
                v.tensor_tensor(sq_t[:, :], xat[:, :], xat[:, :], ALU.mult)
                v.reduce_sum(ssum[:, :], sq_t[:, :], axis=AX).then_inc(an_v, 1)
                v.wait_ge(an_a, gs + 1)
                vs.mark(v.tensor_single_scalar(rn_t[:, :], rn_t[:, :], 1e-12, ALU.max))
                vs.wait(v)
                vs.mark(v.reciprocal(rec[:, :], rn_t[:, :]))
                vs.wait(v)
                vs.mark(v.tensor_scalar(rec[:, :], rec[:, :], 32.0, None, ALU.mult))
                vs.wait(v)
                v.tensor_scalar(nrm[:, :], xat[:, :], rec[:, 0:1], None,
                                ALU.mult).then_inc(xa_done, 1)
                v.tensor_tensor(nrm[:, :], nrm[:, :], tg_b[:, :], ALU.mult)
                if gs % 4 == 0 and gs >= 4:
                    v.wait_ge(st_a, 16 * (gs // 4))
                inst = v.tensor_copy(stg_a[:, (gs % 4) * D:(gs % 4 + 1) * D], nrm[:, :])
                if gs % 4 == 3:
                    inst.then_inc(stga, 1)
            for i in range(NCHK):
                v.wait_ge(qkv_mm, i + 1)
                v.tensor_scalar(q_all[:, i * 128:(i + 1) * 128], qkv_ps[i % 2][:, 0:128],
                                0.25, None, ALU.mult)
                v.tensor_copy(k_bf[:, :], qkv_ps[i % 2][:, 128:256])
                inst = v.tensor_copy(va[:, :].rearrange("p (h e) -> p h e", h=NH)[:, :, 0:16],
                                     qkv_ps[i % 2][:, 256:384].rearrange("p (h e) -> p h e", h=NH)
                                     )
                inst.then_inc(qe_done, 1)
                v.tensor_copy(Fk[:, :].rearrange("p (h f) -> p h f", h=NH)[:, :, 0:16],
                              k_bf[:, :].rearrange("p (h j) -> p h j", h=NH))
                kr = k_bf[:, :].rearrange("p (h j) -> p h j", h=NH)
                for iq in range(16):
                    inst = v.tensor_tensor(
                        Fk[:, :].rearrange("p (h f) -> p h f", h=NH)[:, :, 16 + iq * 16:16 + (iq + 1) * 16],
                        kr, kr[:, :, iq:iq + 1].broadcast_to([128, NH, 16]), ALU.mult)
                inst.then_inc(fb_done, 1)
            v.wait_ge(kv_done, NCHK)
            for sl in range(3):
                inst = v.tensor_copy(kvf[:, sl * 136:(sl + 1) * 136], kv_ps[sl][:, :])
            inst.then_inc(kve, 1)
            for i in range(NCHK):
                qr = q_all[:, i * 128:(i + 1) * 128].rearrange("p (h j) -> p h j", h=NH)
                v.tensor_copy(Fq[:, :].rearrange("p (h f) -> p h f", h=NH)[:, :, 0:16], qr)
                for iq in range(16):
                    inst = v.tensor_tensor(
                        Fq[:, :].rearrange("p (h f) -> p h f", h=NH)[:, :, 16 + iq * 16:16 + (iq + 1) * 16],
                        qr, qr[:, :, iq:iq + 1].broadcast_to([128, NH, 16]), ALU.mult)
                inst.then_inc(fq_done, 1)
                for tsl in range(24):
                    gt = i * 24 + tsl
                    v.wait_ge(tr_mm, gt + 1)
                    v.tensor_copy(fqf[:, tsl * 128:(tsl + 1) * 128],
                                  tr_ps[gt % 2][:, :]).then_inc(tr_ev, 1)
                v.wait_ge(om_done, (i + 1) * 8)
                for h in range(NH):
                    vs.mark(v.tensor_single_scalar(dmx[:, :],
                                                   out_ps[:, h * 17 + 16:h * 17 + 17],
                                                   1e-5, ALU.max))
                    vs.wait(v)
                    vs.mark(v.reciprocal(rec[:, :], dmx[:, :]))
                    vs.wait(v)
                    v.tensor_scalar(ctx_t[:, h * 16:(h + 1) * 16],
                                    out_ps[:, h * 17:h * 17 + 16],
                                    rec[:, 0:1], None, ALU.mult).then_inc(nrm_done, 1)
                v.wait_ge(ctr_mm, i + 1)
                v.tensor_copy(ctxf[:, :], tr_ps[0][:, :]).then_inc(ctr_ev, 1)
                v.wait_ge(wo_done, i + 1)
                v.wait_ge(ld_xr, 16 * (i + 1))
                v.tensor_tensor(pre_t[:, :], xr[i % 2][:, :], wo_ps[:, :], ALU.add
                                ).then_inc(fe_done, 1)
                ln_stats_dve(v, vs, pre_t[:, :], D, sq_t[:, :], mean_raw, var_raw,
                             tt1, var_t).then_inc(ln_v, 1)
                v.wait_ge(ln_r, i + 1)
                if i >= 4 and i % 4 == 0:
                    v.wait_ge(st_o, 32 * (i // 4 - 1) + 32)
                ln_apply_dve(v, vs, pre_t[:, :], stg_f[:, (i % 4) * D:(i % 4 + 1) * D],
                             mean_raw, rstd_t, nm_t, D)
                inst = v.tensor_copy(stg_b[:, (i % 4) * D:(i % 4 + 1) * D],
                                     stg_f[:, (i % 4) * D:(i % 4 + 1) * D])
                if i % 4 == 3:
                    inst.then_inc(stgf, 1)

        _bl.__exit__(None, None, None)
        ph.release()



def _emit_attn(nc, stack, ctx, glob, cast_done, cnt, consts_sb, C_AP, ones_sb,
               cmask_sb):
    import os as _os
    s_in_f, s_in_b = ctx["s3_f"], ctx["s3_b"]
    s_out_f, s_out_b = ctx["s4_f"], ctx["s4_b"]
    tmp_b = ctx["tmp_b"]
    q_sc, k_sc, v_sc = ctx["q_sc"], ctx["k_sc"], ctx["v_sc"]
    ln1_s, ln1_h, a_bo = ctx["ln1_s"], ctx["ln1_h"], ctx["a_bo"]
    a_wq, a_wk, a_wv, a_wo = ctx["a_wq"], ctx["a_wk"], ctx["a_wv"], ctx["a_wo"]
    NQP = int(_os.environ.get("P5_NQP", "8"))
    NGRP = int(_os.environ.get("P5_NGRP", "4"))
    VW = 4 * 65

    sched = []
    n_sc = n_ex = n_pv = n_msk = 0
    nld = [0, 0]
    last_ex_in_bank = {}
    last_pv_of_slot = {}
    last_score_user_of_kv = {}
    for qp in range(NQP):
        NKC = (qp + 1) * 4
        for grp in range(NGRP):
            gi = qp * NGRP + grp
            b = gi % 2
            need_sc = last_score_user_of_kv.get(b, 0)
            sched.append(("kvld", qp, grp, b, need_sc, nld[b]))
            nld[b] += 16 * 5
            for kc in range(NKC):
                for hh in range(4):
                    diag = (kc * 128 >= qp * 512)
                    off = kc * 128 - qp * 512
                    ex_prev = last_ex_in_bank.get(hh, 0)
                    slot = (n_ex) % 4
                    pv_prev = last_pv_of_slot.get(slot, 0)
                    sched.append(("sc", qp, grp, kc, hh, b, ex_prev, n_sc))
                    n_sc += 1
                    sched.append(("ex", qp, grp, kc, hh, slot, pv_prev, n_ex, n_sc))
                    n_ex += 1
                    last_ex_in_bank[hh] = n_ex
                    if diag:
                        sched.append(("msk", slot, off // 128, n_ex, n_msk))
                        n_msk += 1
                    sched.append(("pv", qp, grp, kc, hh, slot, b,
                                  n_ex, n_msk if diag else None, n_pv, kc == 0,
                                  kc == NKC - 1))
                    n_pv += 1
                    last_pv_of_slot[slot] = n_pv
            last_score_user_of_kv[b] = n_sc
            for hh in range(4):
                sched.append(("den", qp, grp, hh, n_pv))
        sched.append(("wo", qp))
        sched.append(("fin", qp))

    with ExitStack() as pst:
        ph = Ph(nc, pst, "p5", glob.sempool)
        l1s_b = ph.sb("l1s_b", [128, D], F32)
        l1h_b = ph.sb("l1h_b", [128, D], F32)
        bo_b = ph.sb("bo_b", [128, D], F32)
        rows3 = ph.sb("rows3", [1, 3 * D], F32)
        wo_sb = ph.sb("wo_sb", [128, 8 * D], BF16)
        fm = [[ph.sb(f"fm{b}_{k}", [128, 512], BF16) for k in range(8)] for b in range(2)]
        xa = [ph.sb(f"xa{i}", [128, D], F32) for i in range(2)]
        nrm = ph.sb("nrm", [128, D], F32)
        stg_a = ph.sb("stg_a", [128, 4 * D], BF16)
        pb = [ph.sb(f"pb{i}", [128, 512], BF16) for i in range(4)]
        ctx_all = [ph.sb(f"ctxa{p}", [128, 512], BF16) for p in range(8)]
        rd_sb = ph.sb("rd_sb", [1, 512], F32)
        rb_sb = ph.sb("rb_sb", [64, 512], F32)
        vtmp = ph.sb("vtmp", [128, D], BF16)
        _wcm = [nc.sbuf_tensor("p5_wq_sb", [128, 8 * D], BF16),
                nc.sbuf_tensor("p5_wk_sb", [128, 8 * D], BF16),
                nc.sbuf_tensor("p5_wv_sb", [128, 8 * D], BF16),
                nc.sbuf_tensor("p5_qstg", [128, 8 * 512], BF16),
                nc.sbuf_tensor("p5_kstg", [128, 8 * 512], BF16),
                nc.sbuf_tensor("p5_vstg", [128, 4 * 1040], BF16)]
        wq_sb, wk_sb, wv_sb, qstg, kstg, vstg = [cm.__enter__() for cm in _wcm]
        for cm in reversed(_wcm):
            cm.__exit__(None, None, None)
        kb = [[ph.sb(f"kb{b}_{j}", [128, NTOK], BF16) for j in range(2)] for b in range(2)]
        qb = [[ph.sb(f"qb{b}_{j}", [128, 512], BF16) for j in range(2)] for b in range(2)]
        vb = [ph.sb(f"vb{b}", [128, 32 * VW], BF16) for b in range(2)]
        xr = [ph.sb(f"xr{i}", [128, D], F32) for i in range(2)]
        stg_f = ph.sb("stg_f", [128, 4 * D], F32)
        stg_b2 = ph.sb("stg_b2", [128, 4 * D], BF16)
        sq_t = ph.sb("sq", [128, D], F32)
        mean_raw = ph.sb("meanr", [128, 1], F32)
        var_raw = ph.sb("varr", [128, 1], F32)
        var_t = ph.sb("var", [128, 1], F32)
        rstd_t = ph.sb("rstd", [128, 1], F32)
        nm_t = ph.sb("nm", [128, 1], F32)
        tt1 = ph.sb("tt1", [128, 1], F32)
        tact = ph.sb("tact", [128, 1], F32)

        ld_r3 = ph.sem("ld_r3", pooled=False)
        ld_wo = ph.sem("ld_wo", pooled=False)
        ld_wq = ph.sem("ld_wq", pooled=False)
        ld_xa = ph.sem("ld_xa", pooled=False)
        xa_done = ph.sem("xa_done")
        ld_fm = [ph.sem(f"ld_fm{b}") for b in range(2)]
        an_v = ph.sem("an_v")
        an_a = ph.sem("an_a")
        stga = ph.sem("stga")
        st_a = ph.sem("st_a", pooled=False)
        qk_mm = ph.sem("qk_mm")
        qk_ev = ph.sem("qk_ev")
        st_qkv = ph.sem("st_qkv", pooled=False)
        bcd = ph.sem("bcd")
        bce = ph.sem("bce")
        ld_kv = [ph.sem(f"ld_kv{b}", pooled=False) for b in range(2)]
        sc_sem = ph.sem("sc")
        ex_sem = ph.sem("ex")
        msk_sem = ph.sem("msk")
        pv_sem = ph.sem("pv")
        den_sem = ph.sem("den")
        bcm_sem = ph.sem("bcm")
        rbc_sem = ph.sem("rbc")
        ectx_sem = ph.sem("ectx")
        wo_sem = ph.sem("wo")
        fe_sem = ph.sem("fe")
        ld_xr = ph.sem("ld_xr", pooled=False)
        stgf = ph.sem("stgf")
        st_o = ph.sem("st_o", pooled=False)
        vs = VS(ph, "vs")


        _pp = [nc.psum_tensor(f"p5_pp{b}", [128, 512], F32) for b in range(2)]
        pp_ps = [cm.__enter__() for cm in _pp]
        _vp = [nc.psum_tensor(f"p5_vp{h}", [128, 512], F32) for h in range(2)]
        vp_ps = [cm.__enter__() for cm in _vp]
        for cm in reversed(_vp):
            cm.__exit__(None, None, None)
        for cm in reversed(_pp):
            cm.__exit__(None, None, None)
        ctx_ps = [ph.ps(f"cx{h}", [65, 512], F32) for h in range(4)]
        _sc = nc.psum_tensor("p5_sc", [128, 2048], F32)
        sc_ps = _sc.__enter__()
        _sc.__exit__(None, None, None)
        wo_ps = ph.ps("wo_ps", [128, D], F32)

        _bl = nc.Block(); block = _bl.__enter__()

        @block.sync
        def _(s):
            for sp in range(NQP):
                if sp >= 2:
                    s.wait_ge(qk_mm, (sp - 1) * 20)
                s.wait_ge(st_a, 16 * (sp + 1))
                for k in range(8):
                    s.dma_start(fm[sp % 2][k][:, :],
                                tmp_b[sp * 512:(sp + 1) * 512, k * 128:(k + 1) * 128],
                                transpose=True).then_inc(ld_fm[sp % 2], 16)

        @block.gpsimd
        def _(g):
            g.dma_start(rows3[:, 0:D], ln1_s).then_inc(ld_r3, 16)
            g.dma_start(rows3[:, D:2 * D], ln1_h).then_inc(ld_r3, 16)
            g.dma_start(rows3[:, 2 * D:3 * D], a_bo).then_inc(ld_r3, 16)
            g.dma_start(wo_sb[:, :].rearrange("p (c q) -> p c q", c=8),
                        a_wo.rearrange("(c p) q -> p c q", p=128)).then_inc(ld_wo, 16)
            for wsrc, wdst in ((a_wq, wq_sb), (a_wk, wk_sb), (a_wv, wv_sb)):
                g.dma_start(wdst[:, :].rearrange("p (c q) -> p c q", c=8),
                            wsrc.rearrange("(c p) q -> p c q", p=128)).then_inc(ld_wq, 16)
            for sp in range(NQP + 1):
                if sp >= 1:
                    g.wait_ge(stga, sp)
                    g.dma_start(tmp_b[(sp - 1) * 512:sp * 512, :].rearrange("(s p) d -> p s d", p=128),
                                stg_a[:, :].rearrange("p (s d) -> p s d", s=4)).then_inc(st_a, 16)
                if sp < NQP:
                    for sub in range(4):
                        gs = sp * 4 + sub
                        if gs >= 2:
                            g.wait_ge(xa_done, gs - 1)
                        g.dma_start(xa[gs % 2][:, :],
                                    s_in_f[gs * 128:(gs + 1) * 128, :]).then_inc(ld_xa, 16)
            for sp in range(NQP):
                g.wait_ge(qk_ev, sp * 20 + 20)
                g.dma_start(q_sc[0:D, sp * 512:(sp + 1) * 512].rearrange("(m p) c -> p m c", p=128),
                            qstg[:, :].rearrange("p (m c) -> p m c", m=8)).then_inc(st_qkv, 16)
                g.dma_start(k_sc[0:D, sp * 512:(sp + 1) * 512].rearrange("(m p) c -> p m c", p=128),
                            kstg[:, :].rearrange("p (m c) -> p m c", m=8)).then_inc(st_qkv, 16)
                g.dma_start(v_sc[sp * 512:(sp + 1) * 512, :].rearrange("(s p) e -> p s e", p=128),
                            vstg[:, :].rearrange("p (s e) -> p s e", s=4)).then_inc(st_qkv, 16)
            for ev in sched:
                if ev[0] == "kvld":
                    _, qp, grp, b, need_sc, base = ev
                    S = (qp + 1) * 512
                    NKC = (qp + 1) * 4
                    if qp == 0 and grp == 0:
                        g.wait_ge(st_qkv, 16 * 3 * NQP)
                    if need_sc > 0:
                        g.wait_ge(sc_sem, need_sc)
                    for j in range(2):
                        g.dma_start(kb[b][j][:, 0:S],
                                    k_sc[grp * 256 + j * 128: grp * 256 + (j + 1) * 128, 0:S]
                                    ).then_inc(ld_kv[b], 16)
                        g.dma_start(qb[b][j][:, :],
                                    q_sc[grp * 256 + j * 128: grp * 256 + (j + 1) * 128,
                                         qp * 512:(qp + 1) * 512]).then_inc(ld_kv[b], 16)
                    g.dma_start(vb[b][:, 0:NKC * VW].rearrange("p (c e) -> p c e", c=NKC),
                                v_sc[0:S, grp * VW:(grp + 1) * VW].rearrange("(c p) e -> p c e", p=128)
                                ).then_inc(ld_kv[b], 16)
                elif ev[0] == "fin":
                    qp = ev[1]
                    if qp >= 1:
                        g.wait_ge(stgf, qp)
                        g.dma_start(s_out_f[(qp - 1) * 512:qp * 512, :].rearrange("(s p) d -> p s d", p=128),
                                    stg_f[:, :].rearrange("p (s d) -> p s d", s=4)).then_inc(st_o, 16)
                        g.dma_start(s_out_b[(qp - 1) * 512:qp * 512, :].rearrange("(s p) d -> p s d", p=128),
                                    stg_b2[:, :].rearrange("p (s d) -> p s d", s=4)).then_inc(st_o, 16)
                    for sub in range(4):
                        i = qp * 4 + sub
                        if i >= 2:
                            g.wait_ge(fe_sem, i - 1)
                        g.dma_start(xr[i % 2][:, :],
                                    s_in_f[i * 128:(i + 1) * 128, :]).then_inc(ld_xr, 16)
            g.wait_ge(stgf, NQP)
            g.dma_start(s_out_f[(NQP - 1) * 512:NQP * 512, :].rearrange("(s p) d -> p s d", p=128),
                        stg_f[:, :].rearrange("p (s d) -> p s d", s=4)).then_inc(st_o, 16)
            g.dma_start(s_out_b[(NQP - 1) * 512:NQP * 512, :].rearrange("(s p) d -> p s d", p=128),
                        stg_b2[:, :].rearrange("p (s d) -> p s d", s=4)).then_inc(st_o, 16)
            g.wait_ge(st_o, 32 * NQP)

        @block.scalar
        def _(sc):
            for gs in range(NQP * 4):
                sc.wait_ge(an_v, gs + 1)
                ln_rstd_act(sc, var_t, rstd_t, tact, C_AP).then_inc(an_a, 1)
            for ev in sched:
                if ev[0] == "ex":
                    _, qp, grp, kc, hh, slot, pv_prev, ei, sc_need = ev
                    sc.wait_ge(sc_sem, sc_need)
                    if pv_prev > 0:
                        sc.wait_ge(pv_sem, pv_prev)
                    sc.activation(pb[slot][:, :], sc_ps[:, hh * 512:(hh + 1) * 512],
                                  AF.Exp).then_inc(ex_sem, 1)

        @block.tensor
        def _(t):
            t.wait_ge(ld_r3, 48)
            k6 = 0
            for i3 in range(3):
                for hf in range(2):
                    if k6 >= 2:
                        t.wait_ge(bce, k6 - 1)
                    t.matmul(pp_ps[k6 % 2][:, :], ones_sb[:, :],
                             rows3[:, i3 * D + hf * 512: i3 * D + (hf + 1) * 512],
                             start=True, stop=True).then_inc(bcd, 1)
                    k6 += 1
            t.wait_ge(ld_wq, 48)
            t.wait_ge(bce, 6)
            for sp in range(NQP):
                t.wait_ge(ld_fm[sp % 2], 16 * 8 * (sp // 2 + 1))
                gidx = sp * 20
                for m in range(8):
                    if gidx >= 2:
                        t.wait_ge(qk_ev, gidx - 1)
                    last = None
                    for k in range(8):
                        last = t.matmul(pp_ps[gidx % 2][:, :],
                                        wq_sb[:, k * D + m * 128: k * D + (m + 1) * 128],
                                        fm[sp % 2][k][:, :], start=(k == 0), stop=(k == 7))
                    last.then_inc(qk_mm, 1)
                    gidx += 1
                for m in range(8):
                    if gidx >= 2:
                        t.wait_ge(qk_ev, gidx - 1)
                    last = None
                    for k in range(8):
                        last = t.matmul(pp_ps[gidx % 2][:, :],
                                        wk_sb[:, k * D + m * 128: k * D + (m + 1) * 128],
                                        fm[sp % 2][k][:, :], start=(k == 0), stop=(k == 7))
                    last.then_inc(qk_mm, 1)
                    gidx += 1
                for sub in range(4):
                    if sub >= 1:
                        t.wait_ge(qk_ev, sp * 20 + 16 + sub)
                    elif sp >= 1:
                        t.wait_ge(qk_ev, sp * 20)
                    last = None
                    for hf in range(2):
                        for k in range(8):
                            last = t.matmul(vp_ps[hf][:, :],
                                            fm[sp % 2][k][:, sub * 128:(sub + 1) * 128],
                                            wv_sb[:, k * D + hf * 512: k * D + (hf + 1) * 512],
                                            start=(k == 0), stop=(k == 7))
                    last.then_inc(qk_mm, 1)
                    gidx += 1
            den_cnt = 0
            ectx_cnt = 0
            for ev in sched:
                if ev[0] == "sc":
                    _, qp, grp, kc, hh, b, ex_prev, sci = ev
                    gi = qp * NGRP + grp
                    if kc == 0 and hh == 0:
                        t.wait_ge(ld_kv[b], nld_target(gi))
                        if grp == 0 and qp >= 1:
                            t.wait_ge(fe_sem, qp * 4)
                        if gi >= 1:
                            t.wait_ge(ectx_sem, gi * 4)
                    if ex_prev > 0:
                        t.wait_ge(ex_sem, ex_prev)
                    jj = hh // 2
                    base = (hh % 2) * 64
                    t.matmul(sc_ps[:, hh * 512:(hh + 1) * 512],
                             kb[b][jj][base:base + 64, kc * 128:(kc + 1) * 128],
                             qb[b][jj][base:base + 64, :],
                             start=True, stop=True).then_inc(sc_sem, 1)
                elif ev[0] == "pv":
                    _, qp, grp, kc, hh, slot, b, ex_need, msk_need, pvi, first, last_kc = ev
                    if msk_need is not None:
                        t.wait_ge(msk_sem, msk_need)
                    else:
                        t.wait_ge(ex_sem, ex_need)
                    t.matmul(ctx_ps[hh][:, :],
                             vb[b][:, kc * VW + hh * 65: kc * VW + (hh + 1) * 65],
                             pb[slot][:, :],
                             start=first, stop=last_kc).then_inc(pv_sem, 1)
                elif ev[0] == "den":
                    _, qp, grp, hh, pv_need = ev
                    t.wait_ge(den_sem, den_cnt + 1)
                    den_cnt += 1
                    t.matmul(sc_ps[0:64, hh * 512:(hh + 1) * 512], ones_sb[:, 0:64],
                             rd_sb[:, :], start=True, stop=True).then_inc(bcm_sem, 1)
                elif ev[0] == "wo":
                    qp = ev[1]
                    t.wait_ge(ectx_sem, (qp + 1) * 16)
                    for sub in range(4):
                        if qp * 4 + sub >= 1:
                            t.wait_ge(fe_sem, qp * 4 + sub)
                        last = None
                        for pair in range(8):
                            for hf in range(2):
                                last = t.matmul(wo_ps[:, hf * 512:(hf + 1) * 512],
                                                ctx_all[pair][:, sub * 128:(sub + 1) * 128],
                                                wo_sb[:, pair * D + hf * 512: pair * D + (hf + 1) * 512],
                                                start=(pair == 0), stop=(pair == 7))
                        last.then_inc(wo_sem, 1)

        @block.vector
        def _(v):
            k6 = 0
            for i3, dstt in enumerate((l1s_b, l1h_b, bo_b)):
                for hf in range(2):
                    v.wait_ge(bcd, k6 + 1)
                    v.tensor_copy(dstt[:, hf * 512:(hf + 1) * 512],
                                  pp_ps[k6 % 2][:, :]).then_inc(bce, 1)
                    k6 += 1
            v.memset(vstg[:, :].rearrange("p (s h e) -> p s h e", s=4, h=16)[:, :, :, 64:65], 1.0)
            for gs in range(NQP * 4):
                v.wait_ge(ld_xa, 16 * (gs + 1))
                xat = xa[gs % 2]
                ln_stats_dve(v, vs, xat[:, :], D, sq_t[:, :], mean_raw, var_raw,
                             tt1, var_t).then_inc(an_v, 1)
                v.wait_ge(an_a, gs + 1)
                ln_apply_dve(v, vs, xat[:, :], nrm[:, :], mean_raw, rstd_t, nm_t, D)
                v.engine_nop().then_inc(xa_done, 1)
                v.tensor_tensor(nrm[:, :], nrm[:, :], l1s_b[:, :], ALU.mult)
                v.tensor_tensor(nrm[:, :], nrm[:, :], l1h_b[:, :], ALU.add)
                if gs % 4 == 0 and gs >= 4:
                    v.wait_ge(st_a, 16 * (gs // 4))
                inst = v.tensor_copy(stg_a[:, (gs % 4) * D:(gs % 4 + 1) * D], nrm[:, :])
                if gs % 4 == 3:
                    inst.then_inc(stga, 1)
            for sp in range(NQP):
                gidx = sp * 20
                if sp >= 1:
                    v.wait_ge(st_qkv, 16 * 3 * sp)
                for m in range(8):
                    v.wait_ge(qk_mm, gidx + 1)
                    v.tensor_scalar(qstg[:, m * 512:(m + 1) * 512], pp_ps[gidx % 2][:, :],
                                    0.125, None, ALU.mult).then_inc(qk_ev, 1)
                    gidx += 1
                for m in range(8):
                    v.wait_ge(qk_mm, gidx + 1)
                    v.tensor_copy(kstg[:, m * 512:(m + 1) * 512],
                                  pp_ps[gidx % 2][:, :]).then_inc(qk_ev, 1)
                    gidx += 1
                for sub in range(4):
                    v.wait_ge(qk_mm, gidx + 1)
                    v.tensor_copy(vtmp[:, 0:512], vp_ps[0][:, :])
                    v.tensor_copy(vtmp[:, 512:1024], vp_ps[1][:, :])
                    v.tensor_copy(vstg[:, :].rearrange("p (s h e) -> p s h e", s=4, h=16)
                                  [:, sub, :, 0:64],
                                  vtmp[:, :].rearrange("p (h e) -> p h e", h=16)
                                  ).then_inc(qk_ev, 1)
                    gidx += 1
            n_ex2 = 0
            den_cnt = 0
            bcm_cnt = 0
            ectx_cnt = 0
            for ev in sched:
                if ev[0] == "msk":
                    _, slot, moff, ex_need, mi = ev
                    v.wait_ge(ex_sem, ex_need)
                    v.tensor_tensor(pb[slot][:, :], pb[slot][:, :],
                                    cmask_sb[:, moff * 512:(moff + 1) * 512],
                                    ALU.mult).then_inc(msk_sem, 1)
                elif ev[0] == "den":
                    _, qp, grp, hh, pv_need = ev
                    v.wait_ge(pv_sem, pv_need)
                    vs.mark(v.reciprocal(rd_sb[:, :], ctx_ps[hh][64:65, :]))
                    vs.wait(v)
                    v.engine_nop().then_inc(den_sem, 1)
                    den_cnt += 1
                    v.wait_ge(bcm_sem, den_cnt)
                    v.tensor_copy(rb_sb[:, :], sc_ps[0:64, hh * 512:(hh + 1) * 512])
                    h_glob = grp * 4 + hh
                    pair = h_glob // 2
                    base = (h_glob % 2) * 64
                    v.tensor_tensor(ctx_all[pair][base:base + 64, :],
                                    ctx_ps[hh][0:64, :], rb_sb[:, :],
                                    ALU.mult).then_inc(ectx_sem, 1)
                elif ev[0] == "fin":
                    qp = ev[1]
                    inst = None
                    for sub in range(4):
                        i = qp * 4 + sub
                        v.wait_ge(wo_sem, i + 1)
                        v.wait_ge(ld_xr, 16 * (i + 1))
                        if sub == 0 and qp >= 1:
                            v.wait_ge(st_o, 32 * qp)
                        dst = stg_f[:, sub * D:(sub + 1) * D]
                        v.tensor_tensor(dst, wo_ps[:, :], bo_b[:, :], ALU.add)
                        v.tensor_tensor(dst, dst, xr[i % 2][:, :], ALU.add).then_inc(fe_sem, 1)
                        inst = v.tensor_copy(stg_b2[:, sub * D:(sub + 1) * D], dst)
                    inst.then_inc(stgf, 1)

        _bl.__exit__(None, None, None)
        ph.release()


def nld_target(gi):
    b = gi % 2
    n = 0
    for g2 in range(gi + 1):
        if g2 % 2 == b:
            n += 16 * 5
    return n



def _emit_moe1(nc, stack, ctx, glob, cast_done, cnt, consts_sb, C_AP, ones_sb):
    s4f, s3f = ctx["s4_f"], ctx["s3_f"]
    out = ctx["out"]
    tmp_b = ctx["tmp_b"]
    g1_w = ctx["g1_w"]
    ln2_s, ln2_h = ctx["ln2_s"], ctx["ln2_h"]
    w1_b, w2_b, w3_b = ctx["w1_b"], ctx["w2_b"], ctx["w3_b"]
    cast_m1 = cast_done["m1"]
    TT_ = 512
    NSUB = 4
    NT = NTOK // TT_
    NQ = 4
    QW = HID1 // NQ
    NUPG = 16
    NDNG = NSUB * 2

    with ExitStack() as pst:
        ph = Ph(nc, pst, "p6", glob.sempool)
        l2s_b = ph.sb("l2s_b", [128, D], F32)
        l2h_b = ph.sb("l2h_b", [128, D], F32)
        rows2 = ph.sb("rows2", [1, 2 * D], F32)
        fm = [[ph.sb(f"fm{b}_{k}", [128, TT_], BF16) for k in range(8)] for b in range(2)]
        rfm = [[ph.sb(f"rfm{b}_{k}", [128, TT_], BF16) for k in range(8)] for b in range(2)]
        w1t = [[ph.sb(f"w1t{b}_{k}", [128, QW], BF16) for k in range(8)] for b in range(2)]
        w2t = [[ph.sb(f"w2t{b}_{k}", [128, QW], BF16) for k in range(8)] for b in range(2)]
        w3t = [ph.sb(f"w3t{c}", [128, D], BF16) for c in range(NUPG)]
        h_sb = ph.sb("h_sb", [128, NUPG * TT_], BF16)
        s_sb = [ph.sb(f"s_sb{b}", [128, TT_], F32) for b in range(2)]
        acc = ph.sb("acc", [128, NSUB * D], F32)
        xres = [ph.sb(f"xres{i}", [128, D], F32) for i in range(2)]
        x3res = [ph.sb(f"x3res{i}", [128, D], F32) for i in range(2)]
        rw_sb = ph.sb("rw", [128, 8 * NEXP], BF16)
        comb = ph.sb("comb", [128, NT * NSUB * NEXP], F32)
        xa = [ph.sb(f"xa{i}", [128, D], F32) for i in range(2)]
        nrm = ph.sb("nrm", [128, D], F32)
        stg_a = ph.sb("stg_a", [128, 4 * D], BF16)
        lg = ph.sb("lg", [128, NEXP], F32)
        s1v = ph.sb("s1v", [128, 1], F32)
        s2v = ph.sb("s2v", [128, 1], F32)
        dif = ph.sb("dif", [128, 1], F32)
        w1v = ph.sb("w1v", [128, 1], F32)
        w2v = ph.sb("w2v", [128, 1], F32)
        sc2 = ph.sb("sc2", [128, NEXP], F32)
        m1_t = ph.sb("m1_t", [128, NEXP], F32)
        m2_t = ph.sb("m2_t", [128, NEXP], F32)
        mm_t = ph.sb("mm_t", [128, NEXP], F32)
        cn1 = ph.sb("cn1", [128, 1], F32)
        f_t = ph.sb("f_t", [128, 1], F32)
        sq_t = ph.sb("sq", [128, D], F32)
        mean_raw = ph.sb("meanr", [128, 1], F32)
        var_raw = ph.sb("varr", [128, 1], F32)
        var_t = ph.sb("var", [128, 1], F32)
        rstd_t = ph.sb("rstd", [128, 1], F32)
        nm_t = ph.sb("nm", [128, 1], F32)
        tt1 = ph.sb("tt1", [128, 1], F32)
        tact = ph.sb("tact", [128, 1], F32)
        h1_ps = [ph.ps(f"h1ps{b}", [128, TT_], F32) for b in range(2)]
        h2_ps = [ph.ps(f"h2ps{b}", [128, TT_], F32) for b in range(2)]
        dn_ps = [ph.ps(f"dnps{b}", [128, 512], F32) for b in range(2)]
        rt_ps = ph.ps("rt_ps", [128, NEXP], F32)

        ld_r2 = ph.sem("ld_r2", pooled=False)
        ld_rw = ph.sem("ld_rw", pooled=False)
        ld_xa = ph.sem("ld_xa", pooled=False)
        xa_done = ph.sem("xa_done")
        an_v = ph.sem("an_v")
        an_a = ph.sem("an_a")
        stga = ph.sem("stga")
        st_a = ph.sem("st_a", pooled=False)
        bcd = ph.sem("bcd")
        bce = ph.sem("bce")
        ld_rf = [ph.sem(f"ld_rf{b}") for b in range(2)]
        ld_fm = [ph.sem(f"ld_fm{b}") for b in range(2)]
        ld_w = [ph.sem(f"ld_w{b}") for b in range(2)]
        ld_d = ph.sem("ld_d")
        ld_xr = ph.sem("ld_xr", pooled=False)
        rt_mm = ph.sem("rt_mm")
        rt_v = ph.sem("rt_v")
        rt_a = ph.sem("rt_a")
        rt_done = ph.sem("rt_done")
        up_mm = ph.sem("up_mm")
        silu_done = ph.sem("silu")
        hprod = ph.sem("hprod")
        dn_done = ph.sem("dn_done")
        evac_done = ph.sem("evac_done")
        ln_v = ph.sem("ln_v")
        ln_r = ph.sem("ln_r")
        st_o = ph.sem("st_o", pooled=False)
        stg2 = ph.sem("stg2")
        vs = VS(ph, "vs")

        _bl = nc.Block(); block = _bl.__enter__()

        def _emit_wloads(s, p):
            x = p % NEXP
            for q in range(NQ):
                g = p * NQ + q
                if g >= 2:
                    s.wait_ge(up_mm, (g - 1) * (NUPG // NQ))
                for k in range(8):
                    s.dma_start(w1t[g % 2][k][:, :],
                                w1_b[x, k * 128:(k + 1) * 128, q * QW:(q + 1) * QW]
                                ).then_inc(ld_w[g % 2], 16)
                    s.dma_start(w2t[g % 2][k][:, :],
                                w2_b[x, k * 128:(k + 1) * 128, q * QW:(q + 1) * QW]
                                ).then_inc(ld_w[g % 2], 16)
            if p >= 1:
                s.wait_ge(dn_done, p * NDNG)
            for c in range(NUPG):
                s.dma_start(w3t[c][:, :],
                            w3_b[x, c * 128:(c + 1) * 128, 0:D]).then_inc(ld_d, 16)

        @block.sync
        def _(s):
            s.wait_ge(cast_m1, cnt["m1"])
            for T in range(NT):
                if T >= 2:
                    s.wait_ge(rt_mm, (T - 1) * NSUB)
                s.wait_ge(st_a, 16 * (T + 1))
                for k in range(8):
                    s.dma_start(rfm[T % 2][k][:, :],
                                tmp_b[T * TT_:(T + 1) * TT_, k * 128:(k + 1) * 128],
                                transpose=True).then_inc(ld_rf[T % 2], 16)
                if T >= 2:
                    s.wait_ge(up_mm, (T - 1) * NEXP * NUPG)
                for k in range(8):
                    s.dma_start(fm[T % 2][k][:, :],
                                tmp_b[T * TT_:(T + 1) * TT_, k * 128:(k + 1) * 128],
                                transpose=True).then_inc(ld_fm[T % 2], 16)
                for x in range(NEXP):
                    _emit_wloads(s, T * NEXP + x)


        @block.gpsimd
        def _(g):
            g.dma_start(rw_sb[:, :].rearrange("p (k e) -> p k e", k=8),
                        g1_w.rearrange("(k p) e -> p k e", p=128)).then_inc(ld_rw, 16)
            g.dma_start(rows2[:, 0:D], ln2_s).then_inc(ld_r2, 16)
            g.dma_start(rows2[:, D:2 * D], ln2_h).then_inc(ld_r2, 16)
            for sp in range(NT + 1):
                if sp >= 1:
                    g.wait_ge(stga, sp)
                    g.dma_start(tmp_b[(sp - 1) * 512:sp * 512, :].rearrange("(s p) d -> p s d", p=128),
                                stg_a[:, :].rearrange("p (s d) -> p s d", s=4)).then_inc(st_a, 16)
                if sp < NT:
                    for sub in range(4):
                        gs = sp * 4 + sub
                        if gs >= 2:
                            g.wait_ge(xa_done, gs - 1)
                        g.dma_start(xa[gs % 2][:, :],
                                    s4f[gs * 128:(gs + 1) * 128, :]).then_inc(ld_xa, 16)
            for T in range(NT + 1):
                if T >= 1:
                    g.wait_ge(stg2, T)
                    g.dma_start(out[(T - 1) * TT_:T * TT_, :].rearrange("(s p) d -> p s d", p=128),
                                acc[:, :].rearrange("p (s d) -> p s d", s=NSUB)).then_inc(st_o, 16)
                if T < NT:
                    for sub in range(NSUB):
                        gs = T * NSUB + sub
                        if gs >= 2:
                            g.wait_ge(ln_v, 2 * (gs - 2) + 1)
                        g.dma_start(xres[gs % 2][:, :],
                                    s4f[T * TT_ + sub * 128: T * TT_ + (sub + 1) * 128, :]
                                    ).then_inc(ld_xr, 16)
                        g.dma_start(x3res[gs % 2][:, :],
                                    s3f[T * TT_ + sub * 128: T * TT_ + (sub + 1) * 128, :]
                                    ).then_inc(ld_xr, 16)
            g.wait_ge(st_o, 16 * NT)

        @block.tensor
        def _(t):
            t.wait_ge(ld_r2, 32)
            for k2 in range(2):
                t.matmul(h1_ps[k2][:, :], ones_sb[:, :], rows2[:, k2 * 512:(k2 + 1) * 512],
                         start=True, stop=True)
            for k2 in range(2):
                t.matmul(h2_ps[k2][:, :], ones_sb[:, :],
                         rows2[:, D + k2 * 512:D + (k2 + 1) * 512],
                         start=True, stop=True).then_inc(bcd, 1)
            t.wait_ge(bce, 4)
            t.wait_ge(ld_rw, 16)
            for T in range(NT):
                t.wait_ge(ld_rf[T % 2], 16 * 8 * (T // 2 + 1))
                for sub in range(NSUB):
                    i = T * NSUB + sub
                    if i >= 1:
                        t.wait_ge(rt_done, i)
                    last = None
                    for k in range(8):
                        last = t.matmul(rt_ps[:, :],
                                        rfm[T % 2][k][:, sub * 128:(sub + 1) * 128],
                                        rw_sb[:, k * NEXP:(k + 1) * NEXP],
                                        start=(k == 0), stop=(k == 7))
                    last.then_inc(rt_mm, 1)
                t.wait_ge(ld_fm[T % 2], 16 * 8 * (T // 2 + 1))
                for x in range(NEXP):
                    pidx = T * NEXP + x
                    for q in range(NQ):
                        gq = pidx * NQ + q
                        t.wait_ge(ld_w[gq % 2], 16 * 16 * (gq // 2 + 1))
                        for mm in range(NUPG // NQ):
                            m = q * (NUPG // NQ) + mm
                            gu = pidx * NUPG + m
                            if gu >= 2:
                                t.wait_ge(hprod, gu - 1)
                            last = None
                            for k in range(8):
                                last = t.matmul(h1_ps[gu % 2][:, :],
                                                w1t[gq % 2][k][:, mm * 128:(mm + 1) * 128],
                                                fm[T % 2][k][:, :],
                                                start=(k == 0), stop=(k == 7))
                            for k in range(8):
                                last = t.matmul(h2_ps[gu % 2][:, :],
                                                w2t[gq % 2][k][:, mm * 128:(mm + 1) * 128],
                                                fm[T % 2][k][:, :],
                                                start=(k == 0), stop=(k == 7))
                            last.then_inc(up_mm, 1)
                    t.wait_ge(ld_d, 16 * NUPG * (pidx + 1))
                    t.wait_ge(hprod, (pidx + 1) * NUPG)
                    for sub in range(NSUB):
                        for nh in range(2):
                            gd = pidx * NDNG + sub * 2 + nh
                            if gd >= 2:
                                t.wait_ge(evac_done, gd - 1)
                            last = None
                            for c in range(NUPG):
                                last = t.matmul(dn_ps[gd % 2][:, :],
                                                h_sb[:, c * TT_ + sub * 128: c * TT_ + (sub + 1) * 128],
                                                w3t[c][:, nh * 512:(nh + 1) * 512],
                                                start=(c == 0), stop=(c == NUPG - 1))
                            last.then_inc(dn_done, 1)

        @block.vector
        def _(v):
            v.wait_ge(bcd, 1)
            for k2 in range(2):
                v.tensor_copy(l2s_b[:, k2 * 512:(k2 + 1) * 512], h1_ps[k2][:, :]).then_inc(bce, 1)
            for k2 in range(2):
                v.tensor_copy(l2h_b[:, k2 * 512:(k2 + 1) * 512], h2_ps[k2][:, :]).then_inc(bce, 1)
            for gs in range(NT * 4):
                v.wait_ge(ld_xa, 16 * (gs + 1))
                xat = xa[gs % 2]
                ln_stats_dve(v, vs, xat[:, :], D, sq_t[:, :], mean_raw, var_raw,
                             tt1, var_t).then_inc(an_v, 1)
                v.wait_ge(an_a, gs + 1)
                ln_apply_dve(v, vs, xat[:, :], nrm[:, :], mean_raw, rstd_t, nm_t, D)
                v.engine_nop().then_inc(xa_done, 1)
                v.tensor_tensor(nrm[:, :], nrm[:, :], l2s_b[:, :], ALU.mult)
                v.tensor_tensor(nrm[:, :], nrm[:, :], l2h_b[:, :], ALU.add)
                if gs % 4 == 0 and gs >= 4:
                    v.wait_ge(st_a, 16 * (gs // 4))
                inst = v.tensor_copy(stg_a[:, (gs % 4) * D:(gs % 4 + 1) * D], nrm[:, :])
                if gs % 4 == 3:
                    inst.then_inc(stga, 1)
            for T in range(NT):
                for sub in range(NSUB):
                    i = T * NSUB + sub
                    v.wait_ge(rt_mm, i + 1)
                    v.tensor_copy(lg[:, :], rt_ps[:, :]).then_inc(rt_done, 1)
                    v.wait_ge(rt_done, i + 1)
                    vs.mark(v.reduce_max(s1v[:, :], lg[:, :], axis=AX))
                    vs.wait(v)
                    vs.mark(v.tensor_scalar(m1_t[:, :], lg[:, :], s1v[:, 0:1], None, ALU.is_ge))
                    vs.wait(v)
                    vs.mark(v.scalar_tensor_tensor(sc2[:, :], m1_t[:, :], -1e30, lg[:, :],
                                                   ALU.mult, ALU.add))
                    vs.mark(v.reduce_sum(cn1[:, :], m1_t[:, :], axis=AX))
                    vs.wait(v)
                    vs.mark(v.reduce_max(s2v[:, :], sc2[:, :], axis=AX))
                    vs.mark(v.tensor_scalar(f_t[:, :], cn1[:, :], -1.0, 2.0, ALU.mult, ALU.add))
                    vs.wait(v)
                    vs.mark(v.tensor_scalar(m2_t[:, :], sc2[:, :], s2v[:, 0:1], None, ALU.is_ge))
                    vs.mark(v.tensor_tensor(dif[:, :], s1v[:, :], s2v[:, :], ALU.subtract))
                    vs.mark(v.tensor_single_scalar(f_t[:, :], f_t[:, :], 0.0, ALU.max))
                    v.engine_nop().then_inc(rt_v, 1)
                    v.wait_ge(rt_a, i + 1)
                    vs.mark(v.tensor_scalar(w2v[:, :], w1v[:, :], -1.0, 1.0, ALU.mult, ALU.add))
                    vs.wait(v)
                    vs.mark(v.tensor_scalar(m2_t[:, :], m2_t[:, :], w2v[:, 0:1], None, ALU.mult))
                    vs.wait(v)
                    vs.mark(v.tensor_scalar(m2_t[:, :], m2_t[:, :], f_t[:, 0:1], None, ALU.mult))
                    vs.wait(v)
                    v.scalar_tensor_tensor(comb[:, i * NEXP:(i + 1) * NEXP], m1_t[:, :],
                                           w1v[:, 0:1], m2_t[:, :], ALU.mult, ALU.add)
                for x in range(NEXP):
                    pidx = T * NEXP + x
                    for m in range(NUPG):
                        gu = pidx * NUPG + m
                        v.wait_ge(silu_done, gu + 1)
                        v.tensor_tensor(h_sb[:, m * TT_:(m + 1) * TT_],
                                        s_sb[gu % 2][:, :], h2_ps[gu % 2][:, :],
                                        ALU.mult).then_inc(hprod, 1)
                    for sub in range(NSUB):
                        for nh in range(2):
                            gd = pidx * NDNG + sub * 2 + nh
                            v.wait_ge(dn_done, gd + 1)
                            if x == 0 and sub == 0 and nh == 0 and T >= 1:
                                v.wait_ge(st_o, 16 * T)
                            dst = acc[:, sub * D + nh * 512: sub * D + (nh + 1) * 512]
                            cap = comb[:, (T * NSUB + sub) * NEXP + x:
                                       (T * NSUB + sub) * NEXP + x + 1]
                            if x == 0:
                                v.tensor_scalar(dst, dn_ps[gd % 2][:, :], cap, None,
                                                ALU.mult).then_inc(evac_done, 1)
                            else:
                                v.scalar_tensor_tensor(dst, dn_ps[gd % 2][:, :], cap,
                                                       dst, ALU.mult, ALU.add).then_inc(evac_done, 1)
                inst = None
                for sub in range(NSUB):
                    gs = T * NSUB + sub
                    accs = acc[:, sub * D:(sub + 1) * D]
                    v.wait_ge(ld_xr, 32 * (gs + 1))
                    v.tensor_tensor(accs, accs, xres[gs % 2][:, :], ALU.add)
                    v.tensor_tensor(accs, accs, x3res[gs % 2][:, :], ALU.add).then_inc(ln_v, 1)
                    ln_stats_dve(v, vs, accs, D, sq_t[:, :], mean_raw, var_raw,
                                 tt1, var_t).then_inc(ln_v, 1)
                    v.wait_ge(ln_r, gs + 1)
                    inst = ln_apply_dve(v, vs, accs, accs, mean_raw, rstd_t, nm_t, D)
                inst.then_inc(stg2, 1)

        @block.scalar
        def _(sc):
            for gs in range(NT * 4):
                sc.wait_ge(an_v, gs + 1)
                ln_rstd_act(sc, var_t, rstd_t, tact, C_AP).then_inc(an_a, 1)
            for T in range(NT):
                if T >= 1:
                    for sub in range(NSUB):
                        gs = (T - 1) * NSUB + sub
                        sc.wait_ge(ln_v, 2 * gs + 2)
                        ln_rstd_act(sc, var_t, rstd_t, tact, C_AP).then_inc(ln_r, 1)
                for sub in range(NSUB):
                    i = T * NSUB + sub
                    sc.wait_ge(rt_v, i + 1)
                    sc.activation(w1v[:, :], dif[:, :], AF.Sigmoid).then_inc(rt_a, 1)
                for x in range(NEXP):
                    pidx = T * NEXP + x
                    if pidx >= 1:
                        sc.wait_ge(dn_done, pidx * NDNG)
                    for m in range(NUPG):
                        gu = pidx * NUPG + m
                        sc.wait_ge(up_mm, gu + 1)
                        sc.activation(s_sb[gu % 2][:, :], h1_ps[gu % 2][:, :],
                                      AF.Silu).then_inc(silu_done, 1)
            for sub in range(NSUB):
                gs = (NT - 1) * NSUB + sub
                sc.wait_ge(ln_v, 2 * gs + 2)
                ln_rstd_act(sc, var_t, rstd_t, tact, C_AP).then_inc(ln_r, 1)

        _bl.__exit__(None, None, None)
        ph.release()


def kernel(**inputs):
    consts = np.tile(np.array([CONST_VALS], dtype=np.float32), (128, 1))
    ones_f = np.ones((1, 128), dtype=np.float32)
    iota_bc = np.tile(np.arange(C, dtype=np.float32)[None, :], (128, 1))
    ident_b = np.eye(128, dtype=ml_dtypes.bfloat16)
    cm = np.zeros((4, 128, 512), dtype=np.float32)
    ki = np.arange(128)[:, None]
    qj = np.arange(512)[None, :]
    for o in range(4):
        cm[o] = (o * 128 + ki) <= qj
    cmask = cm.astype(ml_dtypes.bfloat16)

    x_full = np.asarray(inputs["L0_attn_inputs"], dtype=np.float32)

    def w(name):
        return np.ascontiguousarray(np.asarray(inputs[name], dtype=np.float32))

    shared = {
        "nu_log": w("L0_attn_nu_log"), "theta_log": w("L0_attn_theta_log"),
        "gamma_log": w("L0_attn_gamma_log"),
        "B_re": w("L0_attn_B_re"), "B_im": w("L0_attn_B_im"),
        "C_re": w("L0_attn_C_re"), "C_im": w("L0_attn_C_im"),
        "Dvec": w("L0_attn_D").reshape(1, D),
        "r0_w": w("L0_ffn_router_w"), "r0_b": w("L0_ffn_router_b").reshape(1, NEXP),
        "up0": w("L0_ffn_expert_up"), "dn0": w("L0_ffn_expert_down"),
        "t_gamma": w("L1_attn_gamma").reshape(1, D),
        "t_wq": w("L1_attn_w_q"), "t_wkv": w("L1_attn_w_kv"), "t_wout": w("L1_attn_w_out"),
        "ln1_s": w("L1_ffn_ln1_scale").reshape(1, D), "ln1_h": w("L1_ffn_ln1_shift").reshape(1, D),
        "a_wq": w("L1_ffn_wq"), "a_wk": w("L1_ffn_wk"), "a_wv": w("L1_ffn_wv"),
        "a_wo": w("L1_ffn_wo"), "a_bo": w("L1_ffn_bo").reshape(1, D),
        "ln2_s": w("L1_ffn_ln2_scale").reshape(1, D), "ln2_h": w("L1_ffn_ln2_shift").reshape(1, D),
        "g1_w": w("L1_ffn_gate_w"),
        "w1": w("L1_ffn_w1"), "w2": w("L1_ffn_w2"), "w3": w("L1_ffn_w3"),
        "consts": consts, "ones_f": ones_f, "cmask": cmask,
        "iota_bc": iota_bc, "ident_b": ident_b,
    }

    nc = build_program()
    in_maps = []
    for i in range(8):
        m = dict(shared)
        m["x"] = np.ascontiguousarray(x_full[i])
        in_maps.append(m)
    res = run_bass_kernel_spmd(nc, in_maps, core_ids=list(range(8)))
    return np.stack([r["out"] for r in res.results], axis=0)


# version 05
# speedup vs baseline: 1.0001x; 1.0001x over previous
import numpy as np
import ml_dtypes
from contextlib import ExitStack

import concourse.bass as bass
import concourse.mybir as mybir
from concourse.bass_utils import run_bass_kernel_spmd

F32 = mybir.dt.float32
BF16 = mybir.dt.bfloat16
AF = mybir.ActivationFunctionType
ALU = mybir.AluOpType
AX = mybir.AxisListType.X

NTOK = 4096
D = 1024
P = 1024
DFF0 = 4096
HID1 = 2048
NEXP = 8
C = 512
NCH = NTOK // C

I32 = mybir.dt.int32

CONST_VALS = [-1.0, np.pi / 2, -0.5, 0.5, 1e-5]
C_NEG1, C_HALFPI, C_NEGHALF, C_HALF, C_EPSLN = range(5)
TWO_PI = float(2 * np.pi)

_SEM_BASE = {}
_rust_wait_ge = bass.BassEngine.wait_ge


def _wait_ge_offset(self, sem, value, *a, **k):
    base = _SEM_BASE.get(getattr(sem, "num", None), 0)
    return _rust_wait_ge(self, sem, value + base, *a, **k)


bass.BassEngine.wait_ge = _wait_ge_offset


class SemPool:
    def __init__(self, nc):
        self.nc = nc
        self.free = []
        self.n = 0

    def acquire(self):
        if self.free:
            slot = self.free.pop()
        else:
            slot = [self.nc.alloc_semaphore(name=f"pool{self.n}"), 0]
            self.n += 1
        _SEM_BASE[slot[0].num] = slot[1]
        return slot

    def release_all(self, slots):
        totals = {}
        for i in self.nc.all_instructions():
            si = getattr(i, "sync_info", None)
            if si is None:
                continue
            for u in si.on_update:
                if u.sync_type == "semaphore" and u.update_mode in ("sem-add-imm", "sem-inc"):
                    totals[u.id] = totals.get(u.id, 0) + u.update_value
        for slot in slots:
            slot[1] = totals.get(slot[0].num, 0)
            self.free.append(slot)


class VS:
    def __init__(self, ph, name):
        self.s = ph.sem(name)
        self.n = 0

    def mark(self, inst):
        inst.then_inc(self.s, 1)
        self.n += 1

    def wait(self, eng):
        eng.wait_ge(self.s, self.n)



class Ph:
    def __init__(self, nc, stack, tag, pool=None):
        self.nc, self.stack, self.tag = nc, stack, tag
        self.pool = pool
        self._slots = []

    def sem(self, name, pooled=True):
        if self.pool is None or not pooled:
            return self.nc.alloc_semaphore(name=f"{self.tag}_{name}")
        slot = self.pool.acquire()
        self._slots.append(slot)
        return slot[0]

    def release(self):
        self.nc.all_engine_barrier()
        if self.pool is not None and self._slots:
            self.pool.release_all(self._slots)
        self._slots = []

    def sb(self, name, shape, dt):
        return self.stack.enter_context(self.nc.sbuf_tensor(f"{self.tag}_{name}", shape, dt))

    def ps(self, name, shape, dt=F32):
        return self.stack.enter_context(self.nc.psum_tensor(f"{self.tag}_{name}", shape, dt))


def _bcast_row(block, ph, nc, name, src_row_sb, ncols, ones_f32, done_events):
    dst = ph.sb(name, [128, ncols], F32)
    ps = ph.ps(f"{name}_ps", [128, min(ncols, 512)], F32)
    return dst, ps



def build_program(phases=("p1", "p2", "p3", "p5", "p6"), debug_out=(), debug_in=()):
    nc = bass.Bass("TRN2", target_bir_lowering=False, debug=False,
                   detect_race_conditions=False)
    _SEM_BASE.clear()

    def dram(name, shape, dt, kind="Internal"):
        if name in debug_out:
            kind = "ExternalOutput"
        if name in debug_in:
            kind = "ExternalInput"
        return nc.dram_tensor(name, shape, dt, kind=kind).ap()

    x_in = nc.dram_tensor("x", [NTOK, D], F32, kind="ExternalInput").ap()
    nu_log = nc.dram_tensor("nu_log", [P], F32, kind="ExternalInput").ap()
    theta_log = nc.dram_tensor("theta_log", [P], F32, kind="ExternalInput").ap()
    gamma_log = nc.dram_tensor("gamma_log", [P], F32, kind="ExternalInput").ap()
    B_re = nc.dram_tensor("B_re", [P, D], F32, kind="ExternalInput").ap()
    B_im = nc.dram_tensor("B_im", [P, D], F32, kind="ExternalInput").ap()
    C_re = nc.dram_tensor("C_re", [D, P], F32, kind="ExternalInput").ap()
    C_im = nc.dram_tensor("C_im", [D, P], F32, kind="ExternalInput").ap()
    Dvec = nc.dram_tensor("Dvec", [1, D], F32, kind="ExternalInput").ap()
    r0_w = nc.dram_tensor("r0_w", [D, NEXP], F32, kind="ExternalInput").ap()
    r0_b = nc.dram_tensor("r0_b", [1, NEXP], F32, kind="ExternalInput").ap()
    up0 = nc.dram_tensor("up0", [NEXP, DFF0, D], F32, kind="ExternalInput").ap()
    dn0 = nc.dram_tensor("dn0", [NEXP, D, DFF0], F32, kind="ExternalInput").ap()
    t_gamma = nc.dram_tensor("t_gamma", [1, D], F32, kind="ExternalInput").ap()
    t_wq = nc.dram_tensor("t_wq", [D, 128], F32, kind="ExternalInput").ap()
    t_wkv = nc.dram_tensor("t_wkv", [D, 256], F32, kind="ExternalInput").ap()
    t_wout = nc.dram_tensor("t_wout", [128, D], F32, kind="ExternalInput").ap()
    ln1_s = nc.dram_tensor("ln1_s", [1, D], F32, kind="ExternalInput").ap()
    ln1_h = nc.dram_tensor("ln1_h", [1, D], F32, kind="ExternalInput").ap()
    a_wq = nc.dram_tensor("a_wq", [D, D], F32, kind="ExternalInput").ap()
    a_wk = nc.dram_tensor("a_wk", [D, D], F32, kind="ExternalInput").ap()
    a_wv = nc.dram_tensor("a_wv", [D, D], F32, kind="ExternalInput").ap()
    a_wo = nc.dram_tensor("a_wo", [D, D], F32, kind="ExternalInput").ap()
    a_bo = nc.dram_tensor("a_bo", [1, D], F32, kind="ExternalInput").ap()
    ln2_s = nc.dram_tensor("ln2_s", [1, D], F32, kind="ExternalInput").ap()
    ln2_h = nc.dram_tensor("ln2_h", [1, D], F32, kind="ExternalInput").ap()
    g1_w = nc.dram_tensor("g1_w", [D, NEXP], F32, kind="ExternalInput").ap()
    w1 = nc.dram_tensor("w1", [NEXP, D, HID1], F32, kind="ExternalInput").ap()
    w2 = nc.dram_tensor("w2", [NEXP, D, HID1], F32, kind="ExternalInput").ap()
    w3 = nc.dram_tensor("w3", [NEXP, HID1, D], F32, kind="ExternalInput").ap()
    consts = nc.dram_tensor("consts", [128, len(CONST_VALS)], F32, kind="ExternalInput").ap()
    ones_f = nc.dram_tensor("ones_f", [1, 128], F32, kind="ExternalInput").ap()
    iota_bc = nc.dram_tensor("iota_bc", [128, C], F32, kind="ExternalInput").ap()
    ident_b = nc.dram_tensor("ident_b", [128, 128], BF16, kind="ExternalInput").ap()
    cmask = nc.dram_tensor("cmask", [4, 128, 512], BF16, kind="ExternalInput").ap()

    out = nc.dram_tensor("out", [NTOK, D], F32, kind="ExternalOutput").ap()

    x_bf = dram("x_bf", [NTOK, D], BF16)
    s1_f = dram("s1_f", [NTOK, D], F32)
    s1_b = dram("s1_b", [NTOK, D], BF16)
    s2_f = dram("s2_f", [NTOK, D], F32)
    s2_b = dram("s2_b", [NTOK, D], BF16)
    s3_f = dram("s3_f", [NTOK, D], F32)
    s3_b = dram("s3_b", [NTOK, D], BF16)
    s4_f = dram("s4_f", [NTOK, D], F32)
    s4_b = dram("s4_b", [NTOK, D], BF16)
    tmp_b = dram("tmp_b", [NTOK, D], BF16)
    xs_re = dram("xs_re", [P, NTOK], BF16)
    xs_im = dram("xs_im", [P, NTOK], BF16)
    bre_b = dram("bre_b", [P, D], BF16)
    bim_b = dram("bim_b", [P, D], BF16)
    cre_b = dram("cre_b", [D, P], BF16)
    cim_b = dram("cim_b", [D, P], BF16)
    up0_b = dram("up0_b", [NEXP, DFF0, D], BF16)
    dn0_b = dram("dn0_b", [NEXP, D, DFF0], BF16)
    w1_b = dram("w1_b", [NEXP, D, HID1], BF16)
    w2_b = dram("w2_b", [NEXP, D, HID1], BF16)
    w3_b = dram("w3_b", [NEXP, HID1, D], BF16)
    q_sc = dram("q_sc", [D, NTOK], BF16)
    k_sc = dram("k_sc", [D, NTOK], BF16)
    v_sc = dram("v_sc", [NTOK, 16 * 65], BF16)

    ctx = {k: v for k, v in locals().items() if not k.startswith("_")}

    with ExitStack() as stack:
        glob = Ph(nc, stack, "g")
        pool = SemPool(nc)
        glob.sempool = pool
        cast_a = glob.sem("cast_a")
        cast_m0 = glob.sem("cast_m0")
        cast_m1 = glob.sem("cast_m1")
        cast_done = {"a": cast_a, "m0": cast_m0, "m1": cast_m1}
        cnt = {"a": 80, "m0": 256, "m1": 384}

        consts_sb = glob.sb("consts", [128, len(CONST_VALS)], F32)
        ones_sb = glob.sb("ones", [1, 128], F32)
        ones_bsb = glob.sb("ones_b", [128, 1], BF16)
        d1_b = glob.sb("d1_b", [128, D], F32)
        rb0_b = glob.sb("rb0_b", [128, NEXP], F32)
        cmask_sb = glob.sb("cmask", [128, 4 * 512], BF16)
        nus = glob.sb("nus", [128, NCH], F32)
        ths = glob.sb("ths", [128, NCH], F32)
        gls = glob.sb("gls", [128, NCH], F32)

        C_AP = lambda i: consts_sb[:, i:i + 1]

        with ExitStack() as pst:
            ph = Ph(nc, pst, "p0", pool)
            ld = ph.sem("ld")
            bc_done = ph.sem("bc")
            bc_ev = ph.sem("bc_ev")
            rows_sb = ph.sb("rows", [1, D + NEXP], F32)
            bps = [ph.ps(f"bps{i}", [128, 512], F32) for i in range(2)]
            nld = 0

            _bl = nc.Block(); block = _bl.__enter__()

            @block.scalar
            def _(sc):
                nonlocal nld
                sc.dma_start(consts_sb[:, :], consts).then_inc(ld, 16)
                sc.dma_start(ones_sb[:, :], ones_f).then_inc(ld, 16)
                sc.dma_start(cmask_sb[:, :].rearrange("p (a n) -> p a n", a=4),
                             cmask.rearrange("a p n -> p a n")).then_inc(ld, 16)
                sc.dma_start(rows_sb[:, 0:D], Dvec).then_inc(ld, 16)
                sc.dma_start(rows_sb[:, D:D + NEXP], r0_b).then_inc(ld, 16)
                for g in range(NCH):
                    sc.dma_start(nus[:, g:g + 1],
                                 nu_log[g * 128:(g + 1) * 128].unsqueeze(1)).then_inc(ld, 16)
                    sc.dma_start(ths[:, g:g + 1],
                                 theta_log[g * 128:(g + 1) * 128].unsqueeze(1)).then_inc(ld, 16)
                    sc.dma_start(gls[:, g:g + 1],
                                 gamma_log[g * 128:(g + 1) * 128].unsqueeze(1)).then_inc(ld, 16)
                nld = 5 + 3 * NCH

            @block.tensor
            def _(t):
                t.wait_ge(ld, 16 * nld)
                k = 0
                for h in range(2):
                    if k >= 2:
                        t.wait_ge(bc_ev, k - 1)
                    t.matmul(bps[k % 2][:, :], ones_sb[:, :],
                             rows_sb[:, h * 512:(h + 1) * 512],
                             start=True, stop=True).then_inc(bc_done, 1)
                    k += 1
                t.wait_ge(bc_ev, k - 1)
                t.matmul(bps[k % 2][:, 0:NEXP], ones_sb[:, :],
                         rows_sb[:, D:D + NEXP], start=True, stop=True).then_inc(bc_done, 1)

            @block.vector
            def _(v):
                k = 0
                for h in range(2):
                    v.wait_ge(bc_done, k + 1)
                    v.tensor_copy(d1_b[:, h * 512:(h + 1) * 512],
                                  bps[k % 2][:, :]).then_inc(bc_ev, 1)
                    k += 1
                v.wait_ge(bc_done, k + 1)
                v.tensor_copy(rb0_b[:, :], bps[k % 2][:, 0:NEXP]).then_inc(bc_ev, 1)
                v.tensor_scalar_add(d1_b[:, :], d1_b[:, :], 1.0)
                v.memset(ones_bsb[:, :], 1.0).then_inc(bc_done, 1)

            @block.gpsimd
            def _(g):
                g.dma_start(x_bf, x_in).then_inc(cast_a, 16)
                for m_src, m_dst in ((B_re, bre_b), (B_im, bim_b), (C_re, cre_b), (C_im, cim_b)):
                    g.dma_start(m_dst, m_src).then_inc(cast_a, 16)
                if "p1" not in phases:
                    for e in range(NEXP):
                        g.dma_start(up0_b[e], up0[e]).then_inc(cast_m0, 16)
                        g.dma_start(dn0_b[e], dn0[e]).then_inc(cast_m0, 16)
                    for e in range(NEXP):
                        g.dma_start(w1_b[e], w1[e]).then_inc(cast_m1, 16)
                        g.dma_start(w2_b[e], w2[e]).then_inc(cast_m1, 16)
                        g.dma_start(w3_b[e], w3[e]).then_inc(cast_m1, 16)

            _bl.__exit__(None, None, None)
        ph.release()

        nc.all_engine_barrier()

        if "p1" in phases:
            with nc.named_scope("p1"):
                _emit_p1(nc, stack, ctx, glob, cast_done, cnt,
                         consts_sb, C_AP, d1_b, ones_sb, nus, ths, gls)

        if "p2" in phases:
            with nc.named_scope("p2"):
                _emit_moe(nc, stack, ctx, glob, cast_done, cnt, consts_sb, C_AP,
                          layer=0, rb0_b=rb0_b)

        if "p3" in phases:
            with nc.named_scope("p3"):
                _emit_taylor(nc, stack, ctx, glob, cast_done, cnt, consts_sb, C_AP,
                             ones_sb)

        if "p5" in phases:
            with nc.named_scope("p5"):
                _emit_attn(nc, stack, ctx, glob, cast_done, cnt, consts_sb, C_AP,
                           ones_sb, cmask_sb)

        if "p6" in phases:
            with nc.named_scope("p6"):
                _emit_moe1(nc, stack, ctx, glob, cast_done, cnt, consts_sb, C_AP,
                           ones_sb)

    return nc



def ln_stats_dve(v, vs, pre, W, sq_t, mean_raw, var_raw, t1, var_t):
    v.reduce_sum(mean_raw[:, :], pre, axis=AX)
    v.tensor_tensor(sq_t, pre, pre, ALU.mult)
    v.reduce_sum(var_raw[:, :], sq_t, axis=AX)
    vs.mark(v.tensor_tensor(t1[:, :], mean_raw[:, :], mean_raw[:, :], ALU.mult))
    vs.wait(v)
    vs.mark(v.tensor_scalar(var_t[:, :], var_raw[:, :], 1.0 / W, None, ALU.mult))
    vs.wait(v)
    return v.scalar_tensor_tensor(var_t[:, :], t1[:, :], -1.0 / (W * W), var_t[:, :],
                                  ALU.mult, ALU.add)


def ln_rstd_act(sc, var_t, rstd_t, t_act, C_AP):
    sc.activation(t_act[:, :], var_t[:, :], AF.Ln, bias=C_AP(C_EPSLN))
    return sc.activation(rstd_t[:, :], t_act[:, :], AF.Exp, scale=C_AP(C_NEGHALF))


def ln_apply_dve(v, vs, pre, out_ap, mean_raw, rstd_t, nm_t, W):
    vs.mark(v.scalar_tensor_tensor(nm_t[:, :], mean_raw[:, :], -1.0 / W, rstd_t[:, :],
                                   ALU.mult, ALU.mult))
    vs.wait(v)
    return v.tensor_scalar(out_ap, pre, rstd_t[:, 0:1], nm_t[:, 0:1], ALU.mult, ALU.add)



def _emit_p1(nc, stack, ctx, glob, cast_done, cnt, consts_sb, C_AP, d1_b, ones_sb,
             nus_t, ths_t, gls_t):
    x_in, x_bf = ctx["x_in"], ctx["x_bf"]
    bre_b, bim_b = ctx["bre_b"], ctx["bim_b"]
    xs_re, xs_im = ctx["xs_re"], ctx["xs_im"]
    up0, dn0, up0_b, dn0_b = ctx["up0"], ctx["dn0"], ctx["up0_b"], ctx["dn0_b"]
    iota_bc = ctx["iota_bc"]
    cast_a, cast_m0 = cast_done["a"], cast_done["m0"]

    with ExitStack() as pst:
        ph = Ph(nc, pst, "p1", glob.sempool)
        Bt = {pl: [ph.sb(f"bt{pl}{h}", [128, P], BF16) for h in range(8)] for pl in (0, 1)}
        xfm = [ph.sb(f"xfm{b}", [128, 2048], BF16) for b in range(2)]
        iota_sb = ph.sb("iota", [128, C], F32)
        cos_t = ph.sb("cos_t", [128, C], F32)
        sin_t = ph.sb("sin_t", [128, C], F32)
        cosg = ph.sb("cosg", [128, C], F32)
        sing = ph.sb("sing", [128, C], F32)
        rbc = ph.sb("rbc", [128, C], F32)
        tba = ph.sb("tba", [128, C], F32)
        tbb = ph.sb("tbb", [128, C], F32)
        tbi = ph.sb("tbi", [128, C], I32)
        ur = ph.sb("ur", [128, C], F32)
        ui = ph.sb("ui", [128, C], F32)
        t1 = ph.sb("t1", [128, C], F32)
        t2 = ph.sb("t2", [128, C], F32)
        g1 = ph.sb("g1", [128, C], F32)
        g2 = ph.sb("g2", [128, C], F32)
        gpt1 = ph.sb("gpt1", [128, C], F32)
        gpt2 = ph.sb("gpt2", [128, C], F32)
        w_re = [ph.sb(f"wre{b}", [128, C], F32) for b in range(2)]
        w_im = [ph.sb(f"wim{b}", [128, C], F32) for b in range(2)]
        xsre_buf = ph.sb("xsreb", [128, NTOK], BF16)
        xsim_buf = ph.sb("xsimb", [128, NTOK], BF16)
        nu8 = ph.sb("nu8", [128, NCH], F32)
        r8 = ph.sb("r8", [128, NCH], F32)
        th8 = ph.sb("th8", [128, NCH], F32)
        gam8 = ph.sb("gam8", [128, NCH], F32)
        thC8 = ph.sb("thC8", [128, NCH], F32)
        psiC8 = ph.sb("psiC8", [128, NCH], F32)
        phiC8 = ph.sb("phiC8", [128, NCH], F32)
        aphiC8 = ph.sb("aphiC8", [128, NCH], F32)
        mC8 = ph.sb("mC8", [128, NCH], F32)
        kC8i = ph.sb("kC8i", [128, NCH], I32)
        kC8f = ph.sb("kC8f", [128, NCH], F32)
        sinC8 = ph.sb("sinC8", [128, NCH], F32)
        cosC8 = ph.sb("cosC8", [128, NCH], F32)
        nsC8 = ph.sb("nsC8", [128, NCH], F32)
        sA = ph.sb("sA", [128, 1], F32)
        sB = ph.sb("sB", [128, 1], F32)
        init_re = ph.sb("init_re", [128, 1], F32)
        init_im = ph.sb("init_im", [128, 1], F32)
        ps_re = [ph.ps(f"psr{i}", [128, C], F32) for i in range(4)]
        ps_im = [ph.ps(f"psi{i}", [128, C], F32) for i in range(4)]

        ld_bt = ph.sem("ld_bt")
        ld_io = ph.sem("ld_io")
        ld_xfm = [ph.sem(f"ld_xfm{b}") for b in range(2)]
        pe_hc = ph.sem("pe_hc")
        rot_done = ph.sem("rot_done")
        scan_done = ph.sem("scan_done")
        chunk_done = ph.sem("chunk_done")
        gp_done = ph.sem("gp_done")
        st1 = ph.sem("st1")
        prep_a = ph.sem("prep_a")
        prep_b = ph.sem("prep_b")
        prep_c = ph.sem("prep_c")
        tb_v = ph.sem("tb_v")
        tb_a = ph.sem("tb_a")
        vs = VS(ph, "vs")

        _bl = nc.Block(); block = _bl.__enter__()

        @block.sync
        def _(s):
            s.wait_ge(cast_a, cnt["a"])
            for h in range(8):
                s.dma_start(Bt[0][h][:, :], bre_b[0:P, h * 128:(h + 1) * 128],
                            transpose=True).then_inc(ld_bt, 16)
            for h in range(8):
                s.dma_start(Bt[1][h][:, :], bim_b[0:P, h * 128:(h + 1) * 128],
                            transpose=True).then_inc(ld_bt, 16)
            for pg in range(8):
                for half in range(2):
                    for hc in range(8):
                        gi = (pg * 2 + half) * 8 + hc
                        if gi >= 2:
                            s.wait_ge(pe_hc, gi - 1)
                        s.dma_start(xfm[gi % 2][:, :],
                                    x_bf[half * 2048:(half + 1) * 2048,
                                         hc * 128:(hc + 1) * 128],
                                    transpose=True).then_inc(ld_xfm[gi % 2], 16)

        @block.scalar
        def _(sc):
            sc.dma_start(iota_sb[:, :], iota_bc).then_inc(ld_io, 16)
            sc.activation(nu8[:, :], nus_t[:, :], AF.Exp)
            sc.activation(r8[:, :], nu8[:, :], AF.Exp, scale=C_AP(C_NEG1))
            sc.activation(th8[:, :], ths_t[:, :], AF.Exp)
            sc.activation(gam8[:, :], gls_t[:, :], AF.Exp).then_inc(prep_a, 1)
            sc.wait_ge(prep_b, 1)
            sc.activation(aphiC8[:, :], phiC8[:, :], AF.Abs)
            sc.activation(sinC8[:, :], phiC8[:, :], AF.Sin)
            sc.activation(cosC8[:, :], aphiC8[:, :], AF.Sin,
                          scale=C_AP(C_NEG1), bias=C_AP(C_HALFPI)).then_inc(prep_c, 1)
            for pg in range(9):
                if pg >= 1:
                    sc.wait_ge(chunk_done, pg * 8)
                    sc.wait_ge(gp_done, pg * 8)
                    sc.dma_start(xs_re[(pg - 1) * 128:pg * 128, :],
                                 xsre_buf[:, :]).then_inc(st1, 16)
                    sc.dma_start(xs_im[(pg - 1) * 128:pg * 128, :],
                                 xsim_buf[:, :]).then_inc(st1, 16)
                if pg < 8:
                    sc.wait_ge(tb_v, pg + 1)
                    sc.activation(tbb[:, :], tba[:, :], AF.Abs)
                    sc.activation(sin_t[:, :], tba[:, :], AF.Sin)
                    sc.activation(cos_t[:, :], tbb[:, :], AF.Sin,
                                  scale=C_AP(C_NEG1), bias=C_AP(C_HALFPI)).then_inc(tb_a, 1)
            sc.wait_ge(st1, 32 * 8)

        @block.tensor
        def _(t):
            t.wait_ge(ld_bt, 16 * 16)
            for pg in range(8):
                for half in range(2):
                    hi = pg * 2 + half
                    for hc in range(8):
                        gi = hi * 8 + hc
                        t.wait_ge(ld_xfm[gi % 2], 16 * (gi // 2 + 1))
                        if hc == 0 and hi >= 1:
                            t.wait_ge(rot_done, 4 * hi)
                        last = None
                        for nch in range(4):
                            last = t.matmul(ps_re[nch][:, :],
                                            Bt[0][hc][:, pg * 128:(pg + 1) * 128],
                                            xfm[gi % 2][:, nch * C:(nch + 1) * C],
                                            start=(hc == 0), stop=(hc == 7))
                            last = t.matmul(ps_im[nch][:, :],
                                            Bt[1][hc][:, pg * 128:(pg + 1) * 128],
                                            xfm[gi % 2][:, nch * C:(nch + 1) * C],
                                            start=(hc == 0), stop=(hc == 7))
                        last.then_inc(pe_hc, 1)

        @block.vector
        def _(v):
            v.wait_ge(prep_a, 1)
            vs.mark(v.tensor_single_scalar(thC8[:, :], th8[:, :], 512.0, ALU.mult))
            vs.wait(v)
            vs.mark(v.tensor_single_scalar(psiC8[:, :], thC8[:, :], 1.0 / TWO_PI, ALU.mult))
            vs.wait(v)
            vs.mark(v.tensor_copy(kC8i[:, :], psiC8[:, :]))
            vs.wait(v)
            vs.mark(v.tensor_copy(kC8f[:, :], kC8i[:, :]))
            vs.wait(v)
            vs.mark(v.scalar_tensor_tensor(psiC8[:, :], kC8f[:, :], -TWO_PI, thC8[:, :],
                                           ALU.mult, ALU.add))
            vs.wait(v)
            vs.mark(v.tensor_single_scalar(mC8[:, :], psiC8[:, :], float(np.pi), ALU.is_gt))
            vs.wait(v)
            v.scalar_tensor_tensor(phiC8[:, :], mC8[:, :], -TWO_PI, psiC8[:, :],
                                   ALU.mult, ALU.add).then_inc(prep_b, 1)
            v.wait_ge(prep_c, 1)
            vs.mark(v.tensor_scalar_mul(nsC8[:, :], sinC8[:, :], -1.0))
            vs.wait(v)
            v.wait_ge(ld_io, 16)
            for pg in range(8):
                g_ap = lambda tile: tile[:, pg:pg + 1]
                if pg >= 1:
                    v.wait_ge(gp_done, pg * 8)
                    v.wait_ge(st1, 32 * pg)
                v.tensor_scalar(tba[:, :], iota_sb[:, :], g_ap(th8), None, ALU.mult)
                v.tensor_single_scalar(tbb[:, :], tba[:, :], 1.0 / TWO_PI, ALU.mult)
                v.tensor_copy(tbi[:, :], tbb[:, :])
                v.tensor_copy(tbb[:, :], tbi[:, :])
                v.scalar_tensor_tensor(tba[:, :], tbb[:, :], -TWO_PI, tba[:, :],
                                       ALU.mult, ALU.add)
                v.tensor_single_scalar(tbb[:, :], tba[:, :], float(np.pi), ALU.is_gt)
                v.scalar_tensor_tensor(tba[:, :], tbb[:, :], -TWO_PI, tba[:, :],
                                       ALU.mult, ALU.add).then_inc(tb_v, 1)
                v.tensor_scalar(rbc[:, :], iota_sb[:, :], 0.0, g_ap(r8), ALU.mult, ALU.add)
                v.wait_ge(tb_a, pg + 1)
                v.tensor_scalar(cosg[:, :], cos_t[:, :], g_ap(gam8), None, ALU.mult)
                v.tensor_scalar(sing[:, :], sin_t[:, :], g_ap(gam8), None, ALU.mult)
                for c in range(8):
                    half = c // 4
                    nch = c % 4
                    gc = pg * 8 + c
                    v.wait_ge(pe_hc, (pg * 2 + half + 1) * 8)
                    pr, pi_ = ps_re[nch], ps_im[nch]
                    v.tensor_tensor(t1[:, :], pr[:, :], cosg[:, :], ALU.mult)
                    v.tensor_tensor(t2[:, :], pi_[:, :], sing[:, :], ALU.mult)
                    v.tensor_tensor(ur[:, :], t1[:, :], t2[:, :], ALU.add)
                    v.tensor_tensor(t1[:, :], pi_[:, :], cosg[:, :], ALU.mult)
                    v.tensor_tensor(t2[:, :], pr[:, :], sing[:, :], ALU.mult)
                    v.tensor_tensor(ui[:, :], t1[:, :], t2[:, :], ALU.subtract).then_inc(rot_done, 1)
                    if gc >= 2:
                        v.wait_ge(gp_done, gc - 1)
                    wi_re, wi_im = w_re[c % 2], w_im[c % 2]
                    init_r = init_re[:, 0:1] if c > 0 else 0.0
                    init_i = init_im[:, 0:1] if c > 0 else 0.0
                    v.tensor_tensor_scan(wi_re[:, :], rbc[:, :], ur[:, :], init_r,
                                         ALU.mult, ALU.add)
                    v.tensor_tensor_scan(wi_im[:, :], rbc[:, :], ui[:, :], init_i,
                                         ALU.mult, ALU.add).then_inc(scan_done, 1)
                    if c < 7:
                        v.wait_ge(scan_done, gc + 1)
                        vs.mark(v.tensor_scalar(sA[:, :], wi_re[:, C - 1:C],
                                                g_ap(cosC8), None, ALU.mult))
                        vs.mark(v.tensor_scalar(sB[:, :], wi_im[:, C - 1:C],
                                                g_ap(cosC8), None, ALU.mult))
                        vs.wait(v)
                        v.scalar_tensor_tensor(init_re[:, :], wi_im[:, C - 1:C],
                                               g_ap(nsC8), sA[:, :], ALU.mult, ALU.add)
                        v.scalar_tensor_tensor(init_im[:, :], wi_re[:, C - 1:C],
                                               g_ap(sinC8), sB[:, :], ALU.mult, ALU.add)
                    v.tensor_tensor(g1[:, :], wi_re[:, :], cos_t[:, :], ALU.mult)
                    v.tensor_tensor(g2[:, :], wi_im[:, :], sin_t[:, :], ALU.mult)
                    v.tensor_tensor(xsre_buf[:, c * C:(c + 1) * C], g1[:, :], g2[:, :],
                                    ALU.subtract).then_inc(chunk_done, 1)

        @block.gpsimd
        def _(g):
            for pg in range(8):
                e = pg
                g.dma_start(up0_b[e], up0[e]).then_inc(cast_m0, 16)
                g.dma_start(dn0_b[e], dn0[e]).then_inc(cast_m0, 16)
                for c in range(8):
                    gc = pg * 8 + c
                    g.wait_ge(scan_done, gc + 1)
                    wi_re, wi_im = w_re[c % 2], w_im[c % 2]
                    g.tensor_tensor(gpt1[:, :], wi_im[:, :], cos_t[:, :], ALU.mult)
                    g.tensor_tensor(gpt2[:, :], wi_re[:, :], sin_t[:, :], ALU.mult)
                    g.tensor_tensor(xsim_buf[:, c * C:(c + 1) * C], gpt1[:, :],
                                    gpt2[:, :], ALU.add).then_inc(gp_done, 1)

        _bl.__exit__(None, None, None)
        ph.release()

    _emit_p1_pass2(nc, stack, ctx, glob, cast_done, cnt, C_AP, d1_b)



def _emit_p1_pass2(nc, stack, ctx, glob, cast_done, cnt, C_AP, d1_b):
    x_in = ctx["x_in"]
    cre_b, cim_b = ctx["cre_b"], ctx["cim_b"]
    xs_re, xs_im = ctx["xs_re"], ctx["xs_im"]
    s1_f, s1_b = ctx["s1_f"], ctx["s1_b"]
    w1, w2, w3 = ctx["w1"], ctx["w2"], ctx["w3"]
    w1_b, w2_b, w3_b = ctx["w1_b"], ctx["w2_b"], ctx["w3_b"]
    cast_m1 = cast_done["m1"]
    NSPAN = 8
    SW = 512

    with ExitStack() as pst:
        ph = Ph(nc, pst, "p1b", glob.sempool)
        Ct = {pl: [ph.sb(f"ct{pl}{pc}", [128, D], BF16) for pc in range(8)]
              for pl in (0, 1)}
        xst = [[[ph.sb(f"xs{pl}{pg}{b}", [128, SW], BF16) for b in range(2)]
                for pg in range(8)] for pl in (0, 1)]
        xtile = [ph.sb(f"xt{b}", [128, D], F32) for b in range(4)]
        pre4 = [ph.sb(f"pre{i}", [128, D], F32) for i in range(4)]
        sq_t = ph.sb("sq", [128, D], F32)
        mean_raw = ph.sb("meanr", [128, 1], F32)
        var_raw = ph.sb("varr", [128, 1], F32)
        var_t = ph.sb("var", [128, 1], F32)
        rstd_t = ph.sb("rstd", [128, 1], F32)
        nm_t = ph.sb("nm", [128, 1], F32)
        tt1 = ph.sb("tt1", [128, 1], F32)
        tact = ph.sb("tact", [128, 1], F32)
        of32 = [ph.sb(f"of{b}", [128, 4 * D], F32) for b in range(2)]
        obf = [ph.sb(f"ob{b}", [128, 4 * D], BF16) for b in range(2)]
        psc = [ph.ps(f"psc{i}", [128, SW], F32) for i in range(8)]

        ld_ct = ph.sem("ld_ct")
        ld_xs = [ph.sem(f"ld_xs{b}") for b in range(2)]
        ld_x = [ph.sem(f"ld_x{b}") for b in range(4)]
        neg_done = ph.sem("neg")
        mm_done = ph.sem("mm")
        evac_done = ph.sem("evac")
        ln_v = ph.sem("ln_v")
        ln_r = ph.sem("ln_r")
        stage_done = ph.sem("stage")
        st2 = ph.sem("st2")
        vs = VS(ph, "vs")

        _bl = nc.Block(); block = _bl.__enter__()

        @block.sync
        def _(s):
            for pc in range(8):
                s.dma_start(Ct[0][pc][:, :], cre_b[0:D, pc * 128:(pc + 1) * 128],
                            transpose=True).then_inc(ld_ct, 16)
            for pc in range(8):
                s.dma_start(Ct[1][pc][:, :], cim_b[0:D, pc * 128:(pc + 1) * 128],
                            transpose=True).then_inc(ld_ct, 16)

        def _emit_span_loads(sc, sp):
            b = sp % 2
            if sp >= 2:
                sc.wait_ge(mm_done, (sp - 1) * 8)
            for pl in range(2):
                src = xs_re if pl == 0 else xs_im
                for pg in range(8):
                    sc.dma_start(xst[pl][pg][b][:, :],
                                 src[pg * 128:(pg + 1) * 128, sp * SW:(sp + 1) * SW]
                                 ).then_inc(ld_xs[b], 16)
            for sub in range(4):
                gs = sp * 4 + sub
                if gs >= 4:
                    sc.wait_ge(evac_done, (gs - 3) * 2)
                sc.dma_start(xtile[gs % 4][:, :],
                             x_in[sp * SW + sub * 128: sp * SW + (sub + 1) * 128, :]
                             ).then_inc(ld_x[gs % 4], 16)

        @block.scalar
        def _(sc):
            _emit_span_loads(sc, 0)
            for sp in range(NSPAN + 1):
                if sp >= 1:
                    for sub in range(4):
                        gs = (sp - 1) * 4 + sub
                        sc.wait_ge(ln_v, gs + 1)
                        ln_rstd_act(sc, var_t, rstd_t, tact, C_AP).then_inc(ln_r, 1)
                if sp + 1 < NSPAN:
                    _emit_span_loads(sc, sp + 1)
                if sp >= 1:
                    sc.wait_ge(stage_done, sp)
                    sc.dma_start(s1_f[(sp - 1) * SW:sp * SW, :].rearrange("(s p) d -> p s d", p=128),
                                 of32[(sp - 1) % 2][:, :].rearrange("p (s d) -> p s d", s=4)
                                 ).then_inc(st2, 16)
                    sc.dma_start(s1_b[(sp - 1) * SW:sp * SW, :].rearrange("(s p) d -> p s d", p=128),
                                 obf[(sp - 1) % 2][:, :].rearrange("p (s d) -> p s d", s=4)
                                 ).then_inc(st2, 16)

        @block.vector
        def _(v):
            v.wait_ge(ld_ct, 16 * 16)
            inst = None
            for pc in range(8):
                inst = v.tensor_scalar_mul(Ct[1][pc][:, :], Ct[1][pc][:, :], -1.0)
            inst.then_inc(neg_done, 1)
            for sp in range(NSPAN):
                for sub in range(4):
                    gs = sp * 4 + sub
                    v.wait_ge(ld_x[gs % 4], 16 * (gs // 4 + 1))
                    v.tensor_tensor(pre4[sub][:, :], xtile[gs % 4][:, :], d1_b[:, :], ALU.mult)
                    for half in range(2):
                        v.wait_ge(mm_done, sp * 8 + sub * 2 + half + 1)
                        v.tensor_tensor(pre4[sub][:, half * SW:(half + 1) * SW],
                                        pre4[sub][:, half * SW:(half + 1) * SW],
                                        psc[sub * 2 + half][:, :], ALU.add).then_inc(evac_done, 1)
                for sub in range(4):
                    gs = sp * 4 + sub
                    ln_stats_dve(v, vs, pre4[sub][:, :], D, sq_t[:, :], mean_raw, var_raw,
                                 tt1, var_t).then_inc(ln_v, 1)
                    v.wait_ge(ln_r, gs + 1)
                    if sp >= 2 and sub == 0:
                        v.wait_ge(st2, 32 * (sp - 1))
                    ln_apply_dve(v, vs, pre4[sub][:, :], of32[sp % 2][:, sub * D:(sub + 1) * D],
                                 mean_raw, rstd_t, nm_t, D)
                    inst = v.tensor_copy(obf[sp % 2][:, sub * D:(sub + 1) * D],
                                         of32[sp % 2][:, sub * D:(sub + 1) * D])
                inst.then_inc(stage_done, 1)

        @block.tensor
        def _(t):
            t.wait_ge(ld_ct, 16 * 16)
            t.wait_ge(neg_done, 1)
            for sp in range(NSPAN):
                t.wait_ge(ld_xs[sp % 2], 16 * 16 * (sp // 2 + 1))
                for sub in range(4):
                    for half in range(2):
                        gg = sp * 8 + sub * 2 + half
                        if gg >= 8:
                            t.wait_ge(evac_done, gg - 7)
                        last = None
                        for pg in range(8):
                            for pl in range(2):
                                k = pg * 2 + pl
                                last = t.matmul(psc[sub * 2 + half][:, :],
                                                xst[pl][pg][sp % 2][:, sub * 128:(sub + 1) * 128],
                                                Ct[pl][pg][:, half * SW:(half + 1) * SW],
                                                start=(k == 0), stop=(k == 15))
                        last.then_inc(mm_done, 1)

        @block.gpsimd
        def _(g):
            for e in range(NEXP):
                g.dma_start(w1_b[e], w1[e]).then_inc(cast_m1, 16)
                g.dma_start(w2_b[e], w2[e]).then_inc(cast_m1, 16)
                g.dma_start(w3_b[e], w3[e]).then_inc(cast_m1, 16)

        _bl.__exit__(None, None, None)
        ph.release()




def _emit_moe(nc, stack, ctx, glob, cast_done, cnt, consts_sb, C_AP, layer,
              rb0_b=None, l2s_b=None, l2h_b=None):
    assert layer == 0
    s_in_b, s_in_f = ctx["s1_b"], ctx["s1_f"]
    s_out_f, s_out_b = ctx["s2_f"], ctx["s2_b"]
    up_b, dn_b = ctx["up0_b"], ctx["dn0_b"]
    r_w = ctx["r0_w"]
    cast_m0 = cast_done["m0"]
    TT_ = 512
    NSUB = 4
    NT = NTOK // TT_
    NQ = 4
    QW = DFF0 // NQ
    NUPG = 32
    NDNG = NSUB * 2

    with ExitStack() as pst:
        ph = Ph(nc, pst, "p2", glob.sempool)
        fm = [[ph.sb(f"fm{b}_{k}", [128, TT_], BF16) for k in range(8)] for b in range(2)]
        rfm = [[ph.sb(f"rfm{b}_{k}", [128, TT_], BF16) for k in range(8)] for b in range(2)]
        upT = [[ph.sb(f"upT{b}_{k}", [128, QW], BF16) for k in range(8)] for b in range(2)]
        dnT = [ph.sb(f"dnT{c}", [128, D], BF16) for c in range(32)]
        h_sb = ph.sb("h_sb", [128, NUPG * TT_], BF16)
        acc = ph.sb("acc", [128, NSUB * D], F32)
        obf = ph.sb("obf", [128, NSUB * D], BF16)
        xres = [ph.sb(f"xres{i}", [128, D], F32) for i in range(2)]
        rw_sb = ph.sb("rw", [128, 8 * NEXP], BF16)
        comb = ph.sb("comb", [128, NT * NSUB * NEXP], F32)
        lg = ph.sb("lg", [128, NEXP], F32)
        xm = ph.sb("xm", [128, NEXP], F32)
        pr_t = ph.sb("pr_t", [128, NEXP], F32)
        p2_t = ph.sb("p2_t", [128, NEXP], F32)
        m1_t = ph.sb("m1_t", [128, NEXP], F32)
        m2_t = ph.sb("m2_t", [128, NEXP], F32)
        mm_t = ph.sb("mm_t", [128, NEXP], F32)
        mx1 = ph.sb("mx1", [128, 1], F32)
        mx2 = ph.sb("mx2", [128, 1], F32)
        sm1 = ph.sb("sm1", [128, 1], F32)
        rs1 = ph.sb("rs1", [128, 1], F32)
        cn1 = ph.sb("cn1", [128, 1], F32)
        f_t = ph.sb("f_t", [128, 1], F32)
        sq_t = ph.sb("sq", [128, D], F32)
        mean_raw = ph.sb("meanr", [128, 1], F32)
        var_raw = ph.sb("varr", [128, 1], F32)
        var_t = ph.sb("var", [128, 1], F32)
        rstd_t = ph.sb("rstd", [128, 1], F32)
        nm_t = ph.sb("nm", [128, 1], F32)
        tt1 = ph.sb("tt1", [128, 1], F32)
        tact = ph.sb("tact", [128, 1], F32)
        up_ps = [ph.ps(f"up_ps{b}", [128, TT_], F32) for b in range(2)]
        dn_ps = [ph.ps(f"dn_ps{b}", [128, 512], F32) for b in range(2)]
        rt_ps = ph.ps("rt_ps", [128, NEXP], F32)

        ld_rf = [ph.sem(f"ld_rf{b}") for b in range(2)]
        ld_rw = ph.sem("ld_rw", pooled=False)
        ld_fm = [ph.sem(f"ld_fm{b}") for b in range(2)]
        ld_w = [ph.sem(f"ld_w{b}") for b in range(2)]
        ld_d = ph.sem("ld_d")
        ld_xr = ph.sem("ld_xr", pooled=False)
        rt_mm = ph.sem("rt_mm")
        rt_v = ph.sem("rt_v")
        rt_a = ph.sem("rt_a")
        rt_done = ph.sem("rt_done")
        up_done = ph.sem("up_done")
        gelu_done = ph.sem("gelu_done")
        dn_done = ph.sem("dn_done")
        evac_done = ph.sem("evac_done")
        ln_v = ph.sem("ln_v")
        ln_r = ph.sem("ln_r")
        st_o = ph.sem("st_o", pooled=False)
        stg = ph.sem("stg")
        vs = VS(ph, "vs")

        _bl = nc.Block(); block = _bl.__enter__()

        @block.sync
        def _(s):
            s.wait_ge(cast_m0, cnt["m0"])
            for T in range(NT):
                if T >= 2:
                    s.wait_ge(rt_mm, (T - 1) * NSUB)
                for k in range(8):
                    s.dma_start(rfm[T % 2][k][:, :],
                                s_in_b[T * TT_:(T + 1) * TT_, k * 128:(k + 1) * 128],
                                transpose=True).then_inc(ld_rf[T % 2], 16)
            for T in range(NT):
                if T >= 2:
                    s.wait_ge(up_done, (T - 1) * NEXP * NUPG)
                for k in range(8):
                    s.dma_start(fm[T % 2][k][:, :],
                                s_in_b[T * TT_:(T + 1) * TT_, k * 128:(k + 1) * 128],
                                transpose=True).then_inc(ld_fm[T % 2], 16)
                for x in range(NEXP):
                    p = T * NEXP + x
                    for q in range(NQ):
                        g = p * NQ + q
                        if g >= 2:
                            s.wait_ge(up_done, (g - 1) * (NUPG // NQ))
                        for k in range(8):
                            s.dma_start(upT[g % 2][k][:, :],
                                        up_b[x, q * QW:(q + 1) * QW, k * 128:(k + 1) * 128],
                                        transpose=True).then_inc(ld_w[g % 2], 16)
                    if p >= 1:
                        s.wait_ge(dn_done, p * NDNG)
                    for c in range(32):
                        s.dma_start(dnT[c][:, :],
                                    dn_b[x, 0:D, c * 128:(c + 1) * 128],
                                    transpose=True).then_inc(ld_d, 16)

        @block.scalar
        def _(sc):
            for T in range(NT):
                for sub in range(NSUB):
                    i = T * NSUB + sub
                    sc.wait_ge(rt_v, i + 1)
                    sc.activation(pr_t[:, :], xm[:, :], AF.Exp).then_inc(rt_a, 1)
            for T in range(NT):
                if T >= 1:
                    for sub in range(NSUB):
                        gs = (T - 1) * NSUB + sub
                        sc.wait_ge(ln_v, gs + 1)
                        ln_rstd_act(sc, var_t, rstd_t, tact, C_AP).then_inc(ln_r, 1)
                for x in range(NEXP):
                    pidx = T * NEXP + x
                    if pidx >= 1:
                        sc.wait_ge(dn_done, pidx * NDNG)
                    for m in range(NUPG):
                        gu = pidx * NUPG + m
                        sc.wait_ge(up_done, gu + 1)
                        sc.activation(h_sb[:, m * TT_:(m + 1) * TT_],
                                      up_ps[gu % 2][:, :], AF.Gelu).then_inc(gelu_done, 1)
            for sub in range(NSUB):
                gs = (NT - 1) * NSUB + sub
                sc.wait_ge(ln_v, gs + 1)
                ln_rstd_act(sc, var_t, rstd_t, tact, C_AP).then_inc(ln_r, 1)

        @block.tensor
        def _(t):
            t.wait_ge(ld_rw, 16)
            for T in range(NT):
                t.wait_ge(ld_rf[T % 2], 16 * 8 * (T // 2 + 1))
                for sub in range(NSUB):
                    i = T * NSUB + sub
                    if i >= 1:
                        t.wait_ge(rt_done, i)
                    last = None
                    for k in range(8):
                        last = t.matmul(rt_ps[:, :],
                                        rfm[T % 2][k][:, sub * 128:(sub + 1) * 128],
                                        rw_sb[:, k * NEXP:(k + 1) * NEXP],
                                        start=(k == 0), stop=(k == 7))
                    last.then_inc(rt_mm, 1)
            for T in range(NT):
                t.wait_ge(ld_fm[T % 2], 16 * 8 * (T // 2 + 1))
                for x in range(NEXP):
                    pidx = T * NEXP + x
                    for q in range(NQ):
                        g = pidx * NQ + q
                        t.wait_ge(ld_w[g % 2], 16 * 8 * (g // 2 + 1))
                        for mm in range(NUPG // NQ):
                            m = q * (NUPG // NQ) + mm
                            gu = pidx * NUPG + m
                            if gu >= 2:
                                t.wait_ge(gelu_done, gu - 1)
                            last = None
                            for k in range(8):
                                last = t.matmul(up_ps[gu % 2][:, :],
                                                upT[g % 2][k][:, mm * 128:(mm + 1) * 128],
                                                fm[T % 2][k][:, :],
                                                start=(k == 0), stop=(k == 7))
                            last.then_inc(up_done, 1)
                    t.wait_ge(ld_d, 16 * 32 * (pidx + 1))
                    t.wait_ge(gelu_done, (pidx + 1) * NUPG)
                    for sub in range(NSUB):
                        for nh in range(2):
                            gd = pidx * NDNG + sub * 2 + nh
                            if gd >= 2:
                                t.wait_ge(evac_done, gd - 1)
                            last = None
                            for c in range(32):
                                last = t.matmul(dn_ps[gd % 2][:, :],
                                                h_sb[:, c * TT_ + sub * 128: c * TT_ + (sub + 1) * 128],
                                                dnT[c][:, nh * 512:(nh + 1) * 512],
                                                start=(c == 0), stop=(c == 31))
                            last.then_inc(dn_done, 1)

        @block.vector
        def _(v):
            for T in range(NT):
                for sub in range(NSUB):
                    i = T * NSUB + sub
                    v.wait_ge(rt_mm, i + 1)
                    v.tensor_tensor(lg[:, :], rt_ps[:, :], rb0_b[:, :],
                                    ALU.add).then_inc(rt_done, 1)
                    v.wait_ge(rt_done, i + 1)
                    vs.mark(v.reduce_max(mx1[:, :], lg[:, :], axis=AX))
                    vs.wait(v)
                    v.tensor_scalar(xm[:, :], lg[:, :], mx1[:, 0:1], None,
                                    ALU.subtract).then_inc(rt_v, 1)
                    v.wait_ge(rt_a, i + 1)
                    vs.mark(v.reduce_sum(sm1[:, :], pr_t[:, :], axis=AX))
                    vs.wait(v)
                    vs.mark(v.reciprocal(rs1[:, :], sm1[:, :]))
                    vs.wait(v)
                    vs.mark(v.tensor_scalar(pr_t[:, :], pr_t[:, :], rs1[:, 0:1], None, ALU.mult))
                    vs.wait(v)
                    vs.mark(v.reduce_max(mx1[:, :], pr_t[:, :], axis=AX))
                    vs.wait(v)
                    vs.mark(v.tensor_scalar(m1_t[:, :], pr_t[:, :], mx1[:, 0:1], None, ALU.is_ge))
                    vs.wait(v)
                    vs.mark(v.tensor_tensor(p2_t[:, :], pr_t[:, :], m1_t[:, :], ALU.mult))
                    vs.wait(v)
                    v.tensor_tensor(p2_t[:, :], pr_t[:, :], p2_t[:, :], ALU.subtract)
                    vs.mark(v.reduce_sum(cn1[:, :], m1_t[:, :], axis=AX))
                    vs.wait(v)
                    vs.mark(v.reduce_max(mx2[:, :], p2_t[:, :], axis=AX))
                    vs.mark(v.tensor_scalar(f_t[:, :], cn1[:, :], -1.0, 2.0, ALU.mult, ALU.add))
                    vs.wait(v)
                    vs.mark(v.tensor_scalar(m2_t[:, :], p2_t[:, :], mx2[:, 0:1], None, ALU.is_ge))
                    vs.mark(v.tensor_single_scalar(f_t[:, :], f_t[:, :], 0.0, ALU.max))
                    vs.wait(v)
                    vs.mark(v.scalar_tensor_tensor(mm_t[:, :], m2_t[:, :], f_t[:, 0:1],
                                                   m1_t[:, :], ALU.mult, ALU.add))
                    vs.wait(v)
                    v.tensor_tensor(comb[:, i * NEXP:(i + 1) * NEXP], pr_t[:, :],
                                    mm_t[:, :], ALU.mult)
            for T in range(NT):
                for x in range(NEXP):
                    pidx = T * NEXP + x
                    for sub in range(NSUB):
                        for nh in range(2):
                            gd = pidx * NDNG + sub * 2 + nh
                            v.wait_ge(dn_done, gd + 1)
                            if x == 0 and sub == 0 and nh == 0 and T >= 1:
                                v.wait_ge(st_o, 32 * T)
                            dst = acc[:, sub * D + nh * 512: sub * D + (nh + 1) * 512]
                            cap = comb[:, (T * NSUB + sub) * NEXP + x:
                                       (T * NSUB + sub) * NEXP + x + 1]
                            if x == 0:
                                v.tensor_scalar(dst, dn_ps[gd % 2][:, :], cap, None,
                                                ALU.mult).then_inc(evac_done, 1)
                            else:
                                v.scalar_tensor_tensor(dst, dn_ps[gd % 2][:, :], cap,
                                                       dst, ALU.mult, ALU.add).then_inc(evac_done, 1)
                inst = None
                for sub in range(NSUB):
                    gs = T * NSUB + sub
                    accs = acc[:, sub * D:(sub + 1) * D]
                    v.wait_ge(ld_xr, 16 * (gs + 1))
                    v.tensor_tensor(accs, accs, xres[gs % 2][:, :], ALU.add)
                    ln_stats_dve(v, vs, accs, D, sq_t[:, :], mean_raw, var_raw,
                                 tt1, var_t).then_inc(ln_v, 1)
                    v.wait_ge(ln_r, gs + 1)
                    ln_apply_dve(v, vs, accs, accs, mean_raw, rstd_t, nm_t, D)
                    inst = v.tensor_copy(obf[:, sub * D:(sub + 1) * D], accs)
                inst.then_inc(stg, 1)

        @block.gpsimd
        def _(g):
            g.dma_start(rw_sb[:, :].rearrange("p (k e) -> p k e", k=8),
                        r_w.rearrange("(k p) e -> p k e", p=128)).then_inc(ld_rw, 16)
            for T in range(NT + 1):
                if T >= 1:
                    g.wait_ge(stg, T)
                    g.dma_start(s_out_f[(T - 1) * TT_:T * TT_, :].rearrange("(s p) d -> p s d", p=128),
                                acc[:, :].rearrange("p (s d) -> p s d", s=NSUB)).then_inc(st_o, 16)
                    g.dma_start(s_out_b[(T - 1) * TT_:T * TT_, :].rearrange("(s p) d -> p s d", p=128),
                                obf[:, :].rearrange("p (s d) -> p s d", s=NSUB)).then_inc(st_o, 16)
                if T < NT:
                    for sub in range(NSUB):
                        gs = T * NSUB + sub
                        if gs >= 2:
                            g.wait_ge(ln_v, gs - 1)
                        g.dma_start(xres[gs % 2][:, :],
                                    s_in_f[T * TT_ + sub * 128: T * TT_ + (sub + 1) * 128, :]
                                    ).then_inc(ld_xr, 16)
            g.wait_ge(st_o, 32 * NT)

        _bl.__exit__(None, None, None)
        ph.release()




def _emit_taylor(nc, stack, ctx, glob, cast_done, cnt, consts_sb, C_AP, ones_sb):
    s_in_f, s_in_b = ctx["s2_f"], ctx["s2_b"]
    s_out_f, s_out_b = ctx["s3_f"], ctx["s3_b"]
    tmp_b = ctx["tmp_b"]
    t_gamma, t_wq, t_wkv, t_wout = ctx["t_gamma"], ctx["t_wq"], ctx["t_wkv"], ctx["t_wout"]
    ident_b = ctx["ident_b"]
    NH = 8
    DH = 16
    FB = 384
    NSP = 8
    NCHK = 32

    with ExitStack() as pst:
        ph = Ph(nc, pst, "p3", glob.sempool)
        tg_b = ph.sb("tg_b", [128, D], F32)
        row_st = ph.sb("row_st", [1, D], F32)
        id_sb = ph.sb("id_sb", [128, 128], BF16)
        wqkv = ph.sb("wqkv", [128, 8 * FB], BF16)
        wout_sb = ph.sb("wout", [128, D], BF16)
        fm = [[ph.sb(f"fm{b}_{k}", [128, 512], BF16) for k in range(8)] for b in range(2)]
        xa = [ph.sb(f"xa{i}", [128, D], F32) for i in range(2)]
        nrm = ph.sb("nrm", [128, D], F32)
        stg_a = ph.sb("stg_a", [128, 4 * D], BF16)
        q_all = ph.sb("q_all", [128, NCHK * 128], BF16)
        k_bf = ph.sb("k_bf", [128, 128], BF16)
        va = ph.sb("va", [128, NH * 17], BF16)
        Fk = ph.sb("Fk", [128, NH * FB], BF16)
        Fq = ph.sb("Fq", [128, NH * FB], BF16)
        fqf = ph.sb("fqf", [128, 24 * 128], BF16)
        kvf = ph.sb("kvf", [128, 3 * 136], BF16)
        ctx_t = ph.sb("ctx_t", [128, 128], BF16)
        ctxf = ph.sb("ctxf", [128, 128], BF16)
        xr = [ph.sb(f"xr{i}", [128, D], F32) for i in range(2)]
        pre_t = ph.sb("pre_t", [128, D], F32)
        stg_f = ph.sb("stg_f", [128, 4 * D], F32)
        stg_b = ph.sb("stg_b", [128, 4 * D], BF16)
        dmx = ph.sb("dmx", [128, 1], F32)
        rec = ph.sb("rec", [128, 1], F32)
        ssum = ph.sb("ssum", [128, 1], F32)
        rn_t = ph.sb("rn_t", [128, 1], F32)
        sq_t = ph.sb("sq", [128, D], F32)
        mean_raw = ph.sb("meanr", [128, 1], F32)
        var_raw = ph.sb("varr", [128, 1], F32)
        var_t = ph.sb("var", [128, 1], F32)
        rstd_t = ph.sb("rstd", [128, 1], F32)
        nm_t = ph.sb("nm", [128, 1], F32)
        tt1 = ph.sb("tt1", [128, 1], F32)
        tact = ph.sb("tact", [128, 1], F32)
        qkv_ps = [ph.ps(f"qkv{b}", [128, 512], F32) for b in range(2)]
        tr_ps = [ph.ps(f"tr{b}", [128, 128], BF16) for b in range(2)]
        _kv_cm = [nc.psum_tensor(f"p3_kv{s}", [128, NH * 17], F32) for s in range(3)]
        kv_ps = [cm.__enter__() for cm in _kv_cm]
        for cm in reversed(_kv_cm):
            cm.__exit__(None, None, None)
        out_ps = ph.ps("out_ps", [128, NH * 17], F32)
        wo_ps = ph.ps("wo_ps", [128, D], F32)

        ld_g = ph.sem("ld_g", pooled=False)
        ld_w = ph.sem("ld_w", pooled=False)
        ld_xa = ph.sem("ld_xa", pooled=False)
        ld_fm = [ph.sem(f"ld_fm{b}") for b in range(2)]
        ld_xr = ph.sem("ld_xr", pooled=False)
        bcd = ph.sem("bcd")
        bce = ph.sem("bce")
        an_v = ph.sem("an_v")
        xa_done = ph.sem("xa_done")
        an_a = ph.sem("an_a")
        stga = ph.sem("stga")
        st_a = ph.sem("st_a", pooled=False)
        qkv_mm = ph.sem("qkv_mm")
        qe_done = ph.sem("qe")
        fb_done = ph.sem("fb")
        kv_done = ph.sem("kv")
        kve = ph.sem("kve")
        fq_done = ph.sem("fq")
        tr_mm = ph.sem("tr_mm")
        tr_ev = ph.sem("tr_ev")
        om_done = ph.sem("om")
        nrm_done = ph.sem("nrmd")
        ctr_mm = ph.sem("ctr")
        ctr_ev = ph.sem("ctre")
        wo_done = ph.sem("wo")
        fe_done = ph.sem("fe")
        ln_v = ph.sem("ln_v")
        ln_r = ph.sem("ln_r")
        stgf = ph.sem("stgf")
        st_o = ph.sem("st_o", pooled=False)
        vs = VS(ph, "vs")

        _bl = nc.Block(); block = _bl.__enter__()

        @block.sync
        def _(s):
            for sp in range(NSP):
                if sp >= 2:
                    s.wait_ge(qkv_mm, (sp - 1) * 4)
                s.wait_ge(st_a, 16 * (sp + 1))
                for k in range(8):
                    s.dma_start(fm[sp % 2][k][:, :],
                                tmp_b[sp * 512:(sp + 1) * 512, k * 128:(k + 1) * 128],
                                transpose=True).then_inc(ld_fm[sp % 2], 16)

        @block.gpsimd
        def _(g):
            g.dma_start(wqkv[:, :].rearrange("p (k f) -> p k f", k=8)[:, :, 0:128],
                        t_wq.rearrange("(k p) q -> p k q", p=128)).then_inc(ld_w, 16)
            g.dma_start(wqkv[:, :].rearrange("p (k f) -> p k f", k=8)[:, :, 128:384],
                        t_wkv.rearrange("(k p) q -> p k q", p=128)).then_inc(ld_w, 16)
            g.dma_start(wout_sb[:, :], t_wout).then_inc(ld_w, 16)
            g.dma_start(row_st[:, :], t_gamma).then_inc(ld_g, 16)
            g.dma_start(id_sb[:, :], ident_b).then_inc(ld_w, 16)
            for sp in range(NSP + 1):
                if sp >= 1:
                    g.wait_ge(stga, sp)
                    g.dma_start(tmp_b[(sp - 1) * 512:sp * 512, :].rearrange("(s p) d -> p s d", p=128),
                                stg_a[:, :].rearrange("p (s d) -> p s d", s=4)).then_inc(st_a, 16)
                if sp < NSP:
                    for sub in range(4):
                        gs = sp * 4 + sub
                        if gs >= 2:
                            g.wait_ge(xa_done, gs - 1)
                        g.dma_start(xa[gs % 2][:, :],
                                    s_in_f[gs * 128:(gs + 1) * 128, :]).then_inc(ld_xa, 16)
            for sp in range(NSP + 1):
                if sp >= 1:
                    g.wait_ge(stgf, sp)
                    g.dma_start(s_out_f[(sp - 1) * 512:sp * 512, :].rearrange("(s p) d -> p s d", p=128),
                                stg_f[:, :].rearrange("p (s d) -> p s d", s=4)).then_inc(st_o, 16)
                    g.dma_start(s_out_b[(sp - 1) * 512:sp * 512, :].rearrange("(s p) d -> p s d", p=128),
                                stg_b[:, :].rearrange("p (s d) -> p s d", s=4)).then_inc(st_o, 16)
                if sp < NSP:
                    for sub in range(4):
                        i = sp * 4 + sub
                        if i >= 2:
                            g.wait_ge(fe_done, i - 1)
                        g.dma_start(xr[i % 2][:, :],
                                    s_in_f[i * 128:(i + 1) * 128, :]).then_inc(ld_xr, 16)
            g.wait_ge(st_o, 32 * NSP)

        @block.tensor
        def _(t):
            t.wait_ge(ld_g, 16)
            for h in range(2):
                t.matmul(qkv_ps[h][:, :], ones_sb[:, :], row_st[:, h * 512:(h + 1) * 512],
                         start=True, stop=True).then_inc(bcd, 1)
            t.wait_ge(ld_w, 16 * 4)
            t.wait_ge(bce, 2)
            for i in range(NCHK):
                sp = i // 4
                t.wait_ge(ld_fm[sp % 2], 16 * 8 * (sp // 2 + 1))
                if i >= 2:
                    t.wait_ge(qe_done, i - 1)
                last = None
                for k in range(8):
                    last = t.matmul(qkv_ps[i % 2][:, 0:FB],
                                    fm[sp % 2][k][:, (i % 4) * 128:(i % 4 + 1) * 128],
                                    wqkv[:, k * FB:(k + 1) * FB],
                                    start=(k == 0), stop=(k == 7))
                last.then_inc(qkv_mm, 1)
                t.wait_ge(fb_done, i + 1)
                last = None
                for h in range(NH):
                    for sl in range(3):
                        last = t.matmul(kv_ps[sl][:, h * 17:(h + 1) * 17],
                                        Fk[:, h * FB + sl * 128: h * FB + (sl + 1) * 128],
                                        va[:, h * 17:(h + 1) * 17],
                                        start=(i == 0 and h == 0),
                                        stop=(i == NCHK - 1 and h == NH - 1))
                last.then_inc(kv_done, 1)
            for i in range(NCHK):
                t.wait_ge(fq_done, i + 1)
                for tsl in range(24):
                    gt = i * 24 + tsl
                    if gt >= 2:
                        t.wait_ge(tr_ev, gt - 1)
                    if tsl == 0 and i >= 1:
                        t.wait_ge(ctr_ev, i)
                    t.transpose(tr_ps[gt % 2][:, :], Fq[:, tsl * 128:(tsl + 1) * 128],
                                id_sb[:, :]).then_inc(tr_mm, 1)
                if i == 0:
                    t.wait_ge(kve, 1)
                t.wait_ge(tr_ev, i * 24 + 24)
                if i >= 1:
                    t.wait_ge(nrm_done, i * 8)
                last = None
                for h in range(NH):
                    for sl in range(3):
                        last = t.matmul(out_ps[:, h * 17:(h + 1) * 17],
                                        fqf[:, (h * 3 + sl) * 128:(h * 3 + sl + 1) * 128],
                                        kvf[:, sl * 136 + h * 17: sl * 136 + (h + 1) * 17],
                                        start=(sl == 0), stop=(sl == 2))
                    last.then_inc(om_done, 1)
                t.wait_ge(nrm_done, i * 8 + 8)
                if i >= 1:
                    t.wait_ge(ctr_ev, i)
                t.transpose(tr_ps[0][:, :], ctx_t[:, :], id_sb[:, :]).then_inc(ctr_mm, 1)
                t.wait_ge(ctr_ev, i + 1)
                if i >= 1:
                    t.wait_ge(fe_done, i)
                t.matmul(wo_ps[:, 0:512], ctxf[:, :], wout_sb[:, 0:512],
                         start=True, stop=True)
                t.matmul(wo_ps[:, 512:1024], ctxf[:, :], wout_sb[:, 512:1024],
                         start=True, stop=True).then_inc(wo_done, 1)

        @block.scalar
        def _(sc):
            for gs in range(32):
                sc.wait_ge(an_v, gs + 1)
                sc.activation(tact[:, :], ssum[:, :], AF.Ln)
                sc.activation(rn_t[:, :], tact[:, :], AF.Exp,
                              scale=C_AP(C_HALF)).then_inc(an_a, 1)
            for i in range(32):
                sc.wait_ge(ln_v, i + 1)
                ln_rstd_act(sc, var_t, rstd_t, tact, C_AP).then_inc(ln_r, 1)

        @block.vector
        def _(v):
            for h in range(2):
                v.wait_ge(bcd, h + 1)
                v.tensor_copy(tg_b[:, h * 512:(h + 1) * 512], qkv_ps[h][:, :]).then_inc(bce, 1)
            v.memset(va[:, :], 0.0)
            v.memset(va[:, :].rearrange("p (h e) -> p h e", h=NH)[:, :, 16:17], 1.0)
            v.memset(Fk[:, :], 0.0)
            v.memset(Fq[:, :], 0.0)
            v.memset(Fk[:, :].rearrange("p (h f) -> p h f", h=NH)[:, :, 272:273], 1.0)
            v.memset(Fq[:, :].rearrange("p (h f) -> p h f", h=NH)[:, :, 272:273], 1.0)
            for gs in range(32):
                v.wait_ge(ld_xa, 16 * (gs + 1))
                xat = xa[gs % 2]
                v.tensor_tensor(sq_t[:, :], xat[:, :], xat[:, :], ALU.mult)
                v.reduce_sum(ssum[:, :], sq_t[:, :], axis=AX).then_inc(an_v, 1)
                v.wait_ge(an_a, gs + 1)
                vs.mark(v.tensor_single_scalar(rn_t[:, :], rn_t[:, :], 1e-12, ALU.max))
                vs.wait(v)
                vs.mark(v.reciprocal(rec[:, :], rn_t[:, :]))
                vs.wait(v)
                vs.mark(v.tensor_scalar(rec[:, :], rec[:, :], 32.0, None, ALU.mult))
                vs.wait(v)
                v.tensor_scalar(nrm[:, :], xat[:, :], rec[:, 0:1], None,
                                ALU.mult).then_inc(xa_done, 1)
                v.tensor_tensor(nrm[:, :], nrm[:, :], tg_b[:, :], ALU.mult)
                if gs % 4 == 0 and gs >= 4:
                    v.wait_ge(st_a, 16 * (gs // 4))
                inst = v.tensor_copy(stg_a[:, (gs % 4) * D:(gs % 4 + 1) * D], nrm[:, :])
                if gs % 4 == 3:
                    inst.then_inc(stga, 1)
            for i in range(NCHK):
                v.wait_ge(qkv_mm, i + 1)
                v.tensor_scalar(q_all[:, i * 128:(i + 1) * 128], qkv_ps[i % 2][:, 0:128],
                                0.25, None, ALU.mult)
                v.tensor_copy(k_bf[:, :], qkv_ps[i % 2][:, 128:256])
                inst = v.tensor_copy(va[:, :].rearrange("p (h e) -> p h e", h=NH)[:, :, 0:16],
                                     qkv_ps[i % 2][:, 256:384].rearrange("p (h e) -> p h e", h=NH)
                                     )
                inst.then_inc(qe_done, 1)
                v.tensor_copy(Fk[:, :].rearrange("p (h f) -> p h f", h=NH)[:, :, 0:16],
                              k_bf[:, :].rearrange("p (h j) -> p h j", h=NH))
                kr = k_bf[:, :].rearrange("p (h j) -> p h j", h=NH)
                for iq in range(16):
                    inst = v.tensor_tensor(
                        Fk[:, :].rearrange("p (h f) -> p h f", h=NH)[:, :, 16 + iq * 16:16 + (iq + 1) * 16],
                        kr, kr[:, :, iq:iq + 1].broadcast_to([128, NH, 16]), ALU.mult)
                inst.then_inc(fb_done, 1)
            v.wait_ge(kv_done, NCHK)
            for sl in range(3):
                inst = v.tensor_copy(kvf[:, sl * 136:(sl + 1) * 136], kv_ps[sl][:, :])
            inst.then_inc(kve, 1)
            for i in range(NCHK):
                qr = q_all[:, i * 128:(i + 1) * 128].rearrange("p (h j) -> p h j", h=NH)
                v.tensor_copy(Fq[:, :].rearrange("p (h f) -> p h f", h=NH)[:, :, 0:16], qr)
                for iq in range(16):
                    inst = v.tensor_tensor(
                        Fq[:, :].rearrange("p (h f) -> p h f", h=NH)[:, :, 16 + iq * 16:16 + (iq + 1) * 16],
                        qr, qr[:, :, iq:iq + 1].broadcast_to([128, NH, 16]), ALU.mult)
                inst.then_inc(fq_done, 1)
                for tsl in range(24):
                    gt = i * 24 + tsl
                    v.wait_ge(tr_mm, gt + 1)
                    v.tensor_copy(fqf[:, tsl * 128:(tsl + 1) * 128],
                                  tr_ps[gt % 2][:, :]).then_inc(tr_ev, 1)
                v.wait_ge(om_done, (i + 1) * 8)
                for h in range(NH):
                    vs.mark(v.tensor_single_scalar(dmx[:, :],
                                                   out_ps[:, h * 17 + 16:h * 17 + 17],
                                                   1e-5, ALU.max))
                    vs.wait(v)
                    vs.mark(v.reciprocal(rec[:, :], dmx[:, :]))
                    vs.wait(v)
                    v.tensor_scalar(ctx_t[:, h * 16:(h + 1) * 16],
                                    out_ps[:, h * 17:h * 17 + 16],
                                    rec[:, 0:1], None, ALU.mult).then_inc(nrm_done, 1)
                v.wait_ge(ctr_mm, i + 1)
                v.tensor_copy(ctxf[:, :], tr_ps[0][:, :]).then_inc(ctr_ev, 1)
                v.wait_ge(wo_done, i + 1)
                v.wait_ge(ld_xr, 16 * (i + 1))
                v.tensor_tensor(pre_t[:, :], xr[i % 2][:, :], wo_ps[:, :], ALU.add
                                ).then_inc(fe_done, 1)
                ln_stats_dve(v, vs, pre_t[:, :], D, sq_t[:, :], mean_raw, var_raw,
                             tt1, var_t).then_inc(ln_v, 1)
                v.wait_ge(ln_r, i + 1)
                if i >= 4 and i % 4 == 0:
                    v.wait_ge(st_o, 32 * (i // 4 - 1) + 32)
                ln_apply_dve(v, vs, pre_t[:, :], stg_f[:, (i % 4) * D:(i % 4 + 1) * D],
                             mean_raw, rstd_t, nm_t, D)
                inst = v.tensor_copy(stg_b[:, (i % 4) * D:(i % 4 + 1) * D],
                                     stg_f[:, (i % 4) * D:(i % 4 + 1) * D])
                if i % 4 == 3:
                    inst.then_inc(stgf, 1)

        _bl.__exit__(None, None, None)
        ph.release()



def _emit_attn(nc, stack, ctx, glob, cast_done, cnt, consts_sb, C_AP, ones_sb,
               cmask_sb):
    import os as _os
    s_in_f, s_in_b = ctx["s3_f"], ctx["s3_b"]
    s_out_f, s_out_b = ctx["s4_f"], ctx["s4_b"]
    tmp_b = ctx["tmp_b"]
    q_sc, k_sc, v_sc = ctx["q_sc"], ctx["k_sc"], ctx["v_sc"]
    ln1_s, ln1_h, a_bo = ctx["ln1_s"], ctx["ln1_h"], ctx["a_bo"]
    a_wq, a_wk, a_wv, a_wo = ctx["a_wq"], ctx["a_wk"], ctx["a_wv"], ctx["a_wo"]
    NQP = int(_os.environ.get("P5_NQP", "8"))
    NGRP = int(_os.environ.get("P5_NGRP", "4"))
    VW = 4 * 65

    sched = []
    n_sc = n_ex = n_pv = n_msk = 0
    nld = [0, 0]
    last_ex_in_bank = {}
    last_pv_of_slot = {}
    last_score_user_of_kv = {}
    for qp in range(NQP):
        NKC = (qp + 1) * 4
        for grp in range(NGRP):
            gi = qp * NGRP + grp
            b = gi % 2
            need_sc = last_score_user_of_kv.get(b, 0)
            sched.append(("kvld", qp, grp, b, need_sc, nld[b]))
            nld[b] += 16 * 5
            for kc in range(NKC):
                for hh in range(4):
                    diag = (kc * 128 >= qp * 512)
                    off = kc * 128 - qp * 512
                    ex_prev = last_ex_in_bank.get(hh, 0)
                    slot = (n_ex) % 4
                    pv_prev = last_pv_of_slot.get(slot, 0)
                    sched.append(("sc", qp, grp, kc, hh, b, ex_prev, n_sc))
                    n_sc += 1
                    sched.append(("ex", qp, grp, kc, hh, slot, pv_prev, n_ex, n_sc))
                    n_ex += 1
                    last_ex_in_bank[hh] = n_ex
                    if diag:
                        sched.append(("msk", slot, off // 128, n_ex, n_msk))
                        n_msk += 1
                    sched.append(("pv", qp, grp, kc, hh, slot, b,
                                  n_ex, n_msk if diag else None, n_pv, kc == 0,
                                  kc == NKC - 1))
                    n_pv += 1
                    last_pv_of_slot[slot] = n_pv
            last_score_user_of_kv[b] = n_sc
            for hh in range(4):
                sched.append(("den", qp, grp, hh, n_pv - 3 + hh))
        sched.append(("wo", qp))
        sched.append(("fin", qp))

    PLEAD = 3
    sched_pe = []
    _grp = []

    def _flush_grp():
        scs = [e for e in _grp if e[0] == "sc"]
        pvs = [e for e in _grp if e[0] == "pv"]
        dens = [e for e in _grp if e[0] == "den"]
        for j, e in enumerate(scs):
            sched_pe.append(e)
            if j >= PLEAD:
                sched_pe.append(pvs[j - PLEAD])
        sched_pe.extend(pvs[max(len(scs) - PLEAD, 0):])
        sched_pe.extend(dens)
        _grp.clear()

    for ev in sched:
        if ev[0] in ("kvld", "wo", "fin"):
            _flush_grp()
            sched_pe.append(ev)
        elif ev[0] in ("sc", "pv", "den"):
            _grp.append(ev)
    _flush_grp()

    with ExitStack() as pst:
        ph = Ph(nc, pst, "p5", glob.sempool)
        l1s_b = ph.sb("l1s_b", [128, D], F32)
        l1h_b = ph.sb("l1h_b", [128, D], F32)
        bo_b = ph.sb("bo_b", [128, D], F32)
        rows3 = ph.sb("rows3", [1, 3 * D], F32)
        wo_sb = ph.sb("wo_sb", [128, 8 * D], BF16)
        fm = [[ph.sb(f"fm{b}_{k}", [128, 512], BF16) for k in range(8)] for b in range(2)]
        xa = [ph.sb(f"xa{i}", [128, D], F32) for i in range(2)]
        nrm = ph.sb("nrm", [128, D], F32)
        stg_a = ph.sb("stg_a", [128, 4 * D], BF16)
        pb = [ph.sb(f"pb{i}", [128, 512], BF16) for i in range(4)]
        ctx_all = [ph.sb(f"ctxa{p}", [128, 512], BF16) for p in range(8)]
        rd_sb = ph.sb("rd_sb", [1, 512], F32)
        rb_sb = ph.sb("rb_sb", [64, 512], F32)
        vtmp = ph.sb("vtmp", [128, D], BF16)
        _wcm = [nc.sbuf_tensor("p5_wq_sb", [128, 8 * D], BF16),
                nc.sbuf_tensor("p5_wk_sb", [128, 8 * D], BF16),
                nc.sbuf_tensor("p5_wv_sb", [128, 8 * D], BF16),
                nc.sbuf_tensor("p5_qstg", [128, 8 * 512], BF16),
                nc.sbuf_tensor("p5_kstg", [128, 8 * 512], BF16),
                nc.sbuf_tensor("p5_vstg", [128, 4 * 1040], BF16)]
        wq_sb, wk_sb, wv_sb, qstg, kstg, vstg = [cm.__enter__() for cm in _wcm]
        for cm in reversed(_wcm):
            cm.__exit__(None, None, None)
        kb = [[ph.sb(f"kb{b}_{j}", [128, NTOK], BF16) for j in range(2)] for b in range(2)]
        qb = [[ph.sb(f"qb{b}_{j}", [128, 512], BF16) for j in range(2)] for b in range(2)]
        vb = [ph.sb(f"vb{b}", [128, 32 * VW], BF16) for b in range(2)]
        xr = [ph.sb(f"xr{i}", [128, D], F32) for i in range(2)]
        stg_f = ph.sb("stg_f", [128, 4 * D], F32)
        stg_b2 = ph.sb("stg_b2", [128, 4 * D], BF16)
        sq_t = ph.sb("sq", [128, D], F32)
        mean_raw = ph.sb("meanr", [128, 1], F32)
        var_raw = ph.sb("varr", [128, 1], F32)
        var_t = ph.sb("var", [128, 1], F32)
        rstd_t = ph.sb("rstd", [128, 1], F32)
        nm_t = ph.sb("nm", [128, 1], F32)
        tt1 = ph.sb("tt1", [128, 1], F32)
        tact = ph.sb("tact", [128, 1], F32)

        ld_r3 = ph.sem("ld_r3", pooled=False)
        ld_wo = ph.sem("ld_wo", pooled=False)
        ld_wq = ph.sem("ld_wq", pooled=False)
        ld_xa = ph.sem("ld_xa", pooled=False)
        xa_done = ph.sem("xa_done")
        ld_fm = [ph.sem(f"ld_fm{b}") for b in range(2)]
        an_v = ph.sem("an_v")
        an_a = ph.sem("an_a")
        stga = ph.sem("stga")
        st_a = ph.sem("st_a", pooled=False)
        qk_mm = ph.sem("qk_mm")
        qk_ev = ph.sem("qk_ev")
        st_qkv = ph.sem("st_qkv", pooled=False)
        bcd = ph.sem("bcd")
        bce = ph.sem("bce")
        ld_kv = [ph.sem(f"ld_kv{b}", pooled=False) for b in range(2)]
        sc_sem = ph.sem("sc")
        ex_sem = ph.sem("ex")
        msk_sem = ph.sem("msk")
        pv_sem = ph.sem("pv")
        den_sem = ph.sem("den")
        bcm_sem = ph.sem("bcm")
        rbc_sem = ph.sem("rbc")
        ectx_sem = ph.sem("ectx")
        wo_sem = ph.sem("wo")
        fe_sem = ph.sem("fe")
        ld_xr = ph.sem("ld_xr", pooled=False)
        stgf = ph.sem("stgf")
        st_o = ph.sem("st_o", pooled=False)
        vs = VS(ph, "vs")


        _pp = [nc.psum_tensor(f"p5_pp{b}", [128, 512], F32) for b in range(2)]
        pp_ps = [cm.__enter__() for cm in _pp]
        _vp = [nc.psum_tensor(f"p5_vp{h}", [128, 512], F32) for h in range(2)]
        vp_ps = [cm.__enter__() for cm in _vp]
        for cm in reversed(_vp):
            cm.__exit__(None, None, None)
        for cm in reversed(_pp):
            cm.__exit__(None, None, None)
        ctx_ps = [ph.ps(f"cx{h}", [128, 512], F32) for h in range(4)]
        _sc = nc.psum_tensor("p5_sc", [128, 2048], F32)
        sc_ps = _sc.__enter__()
        _sc.__exit__(None, None, None)
        wo_ps = ph.ps("wo_ps", [128, D], F32)

        _bl = nc.Block(); block = _bl.__enter__()

        @block.sync
        def _(s):
            for sp in range(NQP):
                if sp >= 2:
                    s.wait_ge(qk_mm, (sp - 1) * 20)
                s.wait_ge(st_a, 16 * (sp + 1))
                for k in range(8):
                    s.dma_start(fm[sp % 2][k][:, :],
                                tmp_b[sp * 512:(sp + 1) * 512, k * 128:(k + 1) * 128],
                                transpose=True).then_inc(ld_fm[sp % 2], 16)

        @block.gpsimd
        def _(g):
            g.dma_start(rows3[:, 0:D], ln1_s).then_inc(ld_r3, 16)
            g.dma_start(rows3[:, D:2 * D], ln1_h).then_inc(ld_r3, 16)
            g.dma_start(rows3[:, 2 * D:3 * D], a_bo).then_inc(ld_r3, 16)
            g.dma_start(wo_sb[:, :].rearrange("p (c q) -> p c q", c=8),
                        a_wo.rearrange("(c p) q -> p c q", p=128)).then_inc(ld_wo, 16)
            for wsrc, wdst in ((a_wq, wq_sb), (a_wk, wk_sb), (a_wv, wv_sb)):
                g.dma_start(wdst[:, :].rearrange("p (c q) -> p c q", c=8),
                            wsrc.rearrange("(c p) q -> p c q", p=128)).then_inc(ld_wq, 16)
            for sp in range(NQP + 1):
                if sp >= 1:
                    g.wait_ge(stga, sp)
                    g.dma_start(tmp_b[(sp - 1) * 512:sp * 512, :].rearrange("(s p) d -> p s d", p=128),
                                stg_a[:, :].rearrange("p (s d) -> p s d", s=4)).then_inc(st_a, 16)
                if sp < NQP:
                    for sub in range(4):
                        gs = sp * 4 + sub
                        if gs >= 2:
                            g.wait_ge(xa_done, gs - 1)
                        g.dma_start(xa[gs % 2][:, :],
                                    s_in_f[gs * 128:(gs + 1) * 128, :]).then_inc(ld_xa, 16)
            for sp in range(NQP):
                g.wait_ge(qk_ev, sp * 20 + 20)
                g.dma_start(q_sc[0:D, sp * 512:(sp + 1) * 512].rearrange("(m p) c -> p m c", p=128),
                            qstg[:, :].rearrange("p (m c) -> p m c", m=8)).then_inc(st_qkv, 16)
                g.dma_start(k_sc[0:D, sp * 512:(sp + 1) * 512].rearrange("(m p) c -> p m c", p=128),
                            kstg[:, :].rearrange("p (m c) -> p m c", m=8)).then_inc(st_qkv, 16)
                g.dma_start(v_sc[sp * 512:(sp + 1) * 512, :].rearrange("(s p) e -> p s e", p=128),
                            vstg[:, :].rearrange("p (s e) -> p s e", s=4)).then_inc(st_qkv, 16)
            for ev in sched:
                if ev[0] == "kvld":
                    _, qp, grp, b, need_sc, base = ev
                    S = (qp + 1) * 512
                    NKC = (qp + 1) * 4
                    if qp == 0 and grp == 0:
                        g.wait_ge(st_qkv, 16 * 3 * NQP)
                    if need_sc > 0:
                        g.wait_ge(sc_sem, need_sc)
                    for j in range(2):
                        g.dma_start(kb[b][j][:, 0:S],
                                    k_sc[grp * 256 + j * 128: grp * 256 + (j + 1) * 128, 0:S]
                                    ).then_inc(ld_kv[b], 16)
                        g.dma_start(qb[b][j][:, :],
                                    q_sc[grp * 256 + j * 128: grp * 256 + (j + 1) * 128,
                                         qp * 512:(qp + 1) * 512]).then_inc(ld_kv[b], 16)
                    g.dma_start(vb[b][:, 0:NKC * VW].rearrange("p (c e) -> p c e", c=NKC),
                                v_sc[0:S, grp * VW:(grp + 1) * VW].rearrange("(c p) e -> p c e", p=128)
                                ).then_inc(ld_kv[b], 16)
                elif ev[0] == "fin":
                    qp = ev[1]
                    if qp >= 1:
                        g.wait_ge(stgf, qp)
                        g.dma_start(s_out_f[(qp - 1) * 512:qp * 512, :].rearrange("(s p) d -> p s d", p=128),
                                    stg_f[:, :].rearrange("p (s d) -> p s d", s=4)).then_inc(st_o, 16)
                        g.dma_start(s_out_b[(qp - 1) * 512:qp * 512, :].rearrange("(s p) d -> p s d", p=128),
                                    stg_b2[:, :].rearrange("p (s d) -> p s d", s=4)).then_inc(st_o, 16)
                    for sub in range(4):
                        i = qp * 4 + sub
                        if i >= 2:
                            g.wait_ge(fe_sem, i - 1)
                        g.dma_start(xr[i % 2][:, :],
                                    s_in_f[i * 128:(i + 1) * 128, :]).then_inc(ld_xr, 16)
            g.wait_ge(stgf, NQP)
            g.dma_start(s_out_f[(NQP - 1) * 512:NQP * 512, :].rearrange("(s p) d -> p s d", p=128),
                        stg_f[:, :].rearrange("p (s d) -> p s d", s=4)).then_inc(st_o, 16)
            g.dma_start(s_out_b[(NQP - 1) * 512:NQP * 512, :].rearrange("(s p) d -> p s d", p=128),
                        stg_b2[:, :].rearrange("p (s d) -> p s d", s=4)).then_inc(st_o, 16)
            g.wait_ge(st_o, 32 * NQP)

        @block.scalar
        def _(sc):
            for gs in range(NQP * 4):
                sc.wait_ge(an_v, gs + 1)
                ln_rstd_act(sc, var_t, rstd_t, tact, C_AP).then_inc(an_a, 1)
            for ev in sched:
                if ev[0] == "ex":
                    _, qp, grp, kc, hh, slot, pv_prev, ei, sc_need = ev
                    sc.wait_ge(sc_sem, sc_need)
                    if pv_prev > 0:
                        sc.wait_ge(pv_sem, pv_prev)
                    sc.activation(pb[slot][:, :], sc_ps[:, hh * 512:(hh + 1) * 512],
                                  AF.Exp).then_inc(ex_sem, 1)

        @block.tensor
        def _(t):
            t.wait_ge(ld_r3, 48)
            k6 = 0
            for i3 in range(3):
                for hf in range(2):
                    if k6 >= 2:
                        t.wait_ge(bce, k6 - 1)
                    t.matmul(pp_ps[k6 % 2][:, :], ones_sb[:, :],
                             rows3[:, i3 * D + hf * 512: i3 * D + (hf + 1) * 512],
                             start=True, stop=True).then_inc(bcd, 1)
                    k6 += 1
            t.wait_ge(ld_wq, 48)
            t.wait_ge(bce, 6)
            for sp in range(NQP):
                t.wait_ge(ld_fm[sp % 2], 16 * 8 * (sp // 2 + 1))
                gidx = sp * 20
                for m in range(8):
                    if gidx >= 2:
                        t.wait_ge(qk_ev, gidx - 1)
                    last = None
                    for k in range(8):
                        last = t.matmul(pp_ps[gidx % 2][:, :],
                                        wq_sb[:, k * D + m * 128: k * D + (m + 1) * 128],
                                        fm[sp % 2][k][:, :], start=(k == 0), stop=(k == 7))
                    last.then_inc(qk_mm, 1)
                    gidx += 1
                for m in range(8):
                    if gidx >= 2:
                        t.wait_ge(qk_ev, gidx - 1)
                    last = None
                    for k in range(8):
                        last = t.matmul(pp_ps[gidx % 2][:, :],
                                        wk_sb[:, k * D + m * 128: k * D + (m + 1) * 128],
                                        fm[sp % 2][k][:, :], start=(k == 0), stop=(k == 7))
                    last.then_inc(qk_mm, 1)
                    gidx += 1
                for sub in range(4):
                    if sub >= 1:
                        t.wait_ge(qk_ev, sp * 20 + 16 + sub)
                    elif sp >= 1:
                        t.wait_ge(qk_ev, sp * 20)
                    last = None
                    for hf in range(2):
                        for k in range(8):
                            last = t.matmul(vp_ps[hf][:, :],
                                            fm[sp % 2][k][:, sub * 128:(sub + 1) * 128],
                                            wv_sb[:, k * D + hf * 512: k * D + (hf + 1) * 512],
                                            start=(k == 0), stop=(k == 7))
                    last.then_inc(qk_mm, 1)
                    gidx += 1
            den_cnt = 0
            ectx_cnt = 0
            for ev in sched_pe:
                if ev[0] == "sc":
                    _, qp, grp, kc, hh, b, ex_prev, sci = ev
                    gi = qp * NGRP + grp
                    if kc == 0 and hh == 0:
                        t.wait_ge(ld_kv[b], nld_target(gi))
                        if grp == 0 and qp >= 1:
                            t.wait_ge(fe_sem, qp * 4)
                    if ex_prev > 0:
                        t.wait_ge(ex_sem, ex_prev)
                    jj = hh // 2
                    base = (hh % 2) * 64
                    t.matmul(sc_ps[:, hh * 512:(hh + 1) * 512],
                             kb[b][jj][base:base + 64, kc * 128:(kc + 1) * 128],
                             qb[b][jj][base:base + 64, :],
                             start=True, stop=True).then_inc(sc_sem, 1)
                elif ev[0] == "pv":
                    _, qp, grp, kc, hh, slot, b, ex_need, msk_need, pvi, first, last_kc = ev
                    if msk_need is not None:
                        t.wait_ge(msk_sem, msk_need)
                    else:
                        t.wait_ge(ex_sem, ex_need)
                    if first:
                        gi = qp * NGRP + grp
                        if gi >= 1:
                            t.wait_ge(ectx_sem, gi * 4)
                    t.matmul(ctx_ps[hh][0:65, :],
                             vb[b][:, kc * VW + hh * 65: kc * VW + (hh + 1) * 65],
                             pb[slot][:, :],
                             start=first, stop=last_kc).then_inc(pv_sem, 1)
                elif ev[0] == "den":
                    _, qp, grp, hh, pv_need = ev
                    t.wait_ge(den_sem, den_cnt + 1)
                    den_cnt += 1
                    t.matmul(ctx_ps[hh][64:128, :], ones_sb[:, 0:64],
                             rd_sb[:, :], start=True, stop=True).then_inc(bcm_sem, 1)
                elif ev[0] == "wo":
                    qp = ev[1]
                    t.wait_ge(ectx_sem, (qp + 1) * 16)
                    for sub in range(4):
                        if qp * 4 + sub >= 1:
                            t.wait_ge(fe_sem, qp * 4 + sub)
                        last = None
                        for pair in range(8):
                            for hf in range(2):
                                last = t.matmul(wo_ps[:, hf * 512:(hf + 1) * 512],
                                                ctx_all[pair][:, sub * 128:(sub + 1) * 128],
                                                wo_sb[:, pair * D + hf * 512: pair * D + (hf + 1) * 512],
                                                start=(pair == 0), stop=(pair == 7))
                        last.then_inc(wo_sem, 1)

        @block.vector
        def _(v):
            k6 = 0
            for i3, dstt in enumerate((l1s_b, l1h_b, bo_b)):
                for hf in range(2):
                    v.wait_ge(bcd, k6 + 1)
                    v.tensor_copy(dstt[:, hf * 512:(hf + 1) * 512],
                                  pp_ps[k6 % 2][:, :]).then_inc(bce, 1)
                    k6 += 1
            v.memset(vstg[:, :].rearrange("p (s h e) -> p s h e", s=4, h=16)[:, :, :, 64:65], 1.0)
            for gs in range(NQP * 4):
                v.wait_ge(ld_xa, 16 * (gs + 1))
                xat = xa[gs % 2]
                ln_stats_dve(v, vs, xat[:, :], D, sq_t[:, :], mean_raw, var_raw,
                             tt1, var_t).then_inc(an_v, 1)
                v.wait_ge(an_a, gs + 1)
                ln_apply_dve(v, vs, xat[:, :], nrm[:, :], mean_raw, rstd_t, nm_t, D)
                v.engine_nop().then_inc(xa_done, 1)
                v.tensor_tensor(nrm[:, :], nrm[:, :], l1s_b[:, :], ALU.mult)
                v.tensor_tensor(nrm[:, :], nrm[:, :], l1h_b[:, :], ALU.add)
                if gs % 4 == 0 and gs >= 4:
                    v.wait_ge(st_a, 16 * (gs // 4))
                inst = v.tensor_copy(stg_a[:, (gs % 4) * D:(gs % 4 + 1) * D], nrm[:, :])
                if gs % 4 == 3:
                    inst.then_inc(stga, 1)
            for sp in range(NQP):
                gidx = sp * 20
                if sp >= 1:
                    v.wait_ge(st_qkv, 16 * 3 * sp)
                for m in range(8):
                    v.wait_ge(qk_mm, gidx + 1)
                    v.tensor_scalar(qstg[:, m * 512:(m + 1) * 512], pp_ps[gidx % 2][:, :],
                                    0.125, None, ALU.mult).then_inc(qk_ev, 1)
                    gidx += 1
                for m in range(8):
                    v.wait_ge(qk_mm, gidx + 1)
                    v.tensor_copy(kstg[:, m * 512:(m + 1) * 512],
                                  pp_ps[gidx % 2][:, :]).then_inc(qk_ev, 1)
                    gidx += 1
                for sub in range(4):
                    v.wait_ge(qk_mm, gidx + 1)
                    v.tensor_copy(vtmp[:, 0:512], vp_ps[0][:, :])
                    v.tensor_copy(vtmp[:, 512:1024], vp_ps[1][:, :])
                    v.tensor_copy(vstg[:, :].rearrange("p (s h e) -> p s h e", s=4, h=16)
                                  [:, sub, :, 0:64],
                                  vtmp[:, :].rearrange("p (h e) -> p h e", h=16)
                                  ).then_inc(qk_ev, 1)
                    gidx += 1
            n_ex2 = 0
            den_cnt = 0
            bcm_cnt = 0
            ectx_cnt = 0
            for ev in sched:
                if ev[0] == "msk":
                    _, slot, moff, ex_need, mi = ev
                    v.wait_ge(ex_sem, ex_need)
                    v.tensor_tensor(pb[slot][:, :], pb[slot][:, :],
                                    cmask_sb[:, moff * 512:(moff + 1) * 512],
                                    ALU.mult).then_inc(msk_sem, 1)
                elif ev[0] == "den":
                    _, qp, grp, hh, pv_need = ev
                    v.wait_ge(pv_sem, pv_need)
                    vs.mark(v.reciprocal(rd_sb[:, :], ctx_ps[hh][64:65, :]))
                    vs.wait(v)
                    v.engine_nop().then_inc(den_sem, 1)
                    den_cnt += 1
                    v.wait_ge(bcm_sem, den_cnt)
                    h_glob = grp * 4 + hh
                    pair = h_glob // 2
                    base = (h_glob % 2) * 64
                    v.tensor_copy(rb_sb[:, :], ctx_ps[hh][64:128, :])
                    v.tensor_tensor(ctx_all[pair][base:base + 64, :],
                                    ctx_ps[hh][0:64, :], rb_sb[:, :],
                                    ALU.mult).then_inc(ectx_sem, 1)
                elif ev[0] == "fin":
                    qp = ev[1]
                    inst = None
                    for sub in range(4):
                        i = qp * 4 + sub
                        v.wait_ge(wo_sem, i + 1)
                        v.wait_ge(ld_xr, 16 * (i + 1))
                        if sub == 0 and qp >= 1:
                            v.wait_ge(st_o, 32 * qp)
                        dst = stg_f[:, sub * D:(sub + 1) * D]
                        v.tensor_tensor(dst, wo_ps[:, :], bo_b[:, :], ALU.add)
                        v.tensor_tensor(dst, dst, xr[i % 2][:, :], ALU.add).then_inc(fe_sem, 1)
                        inst = v.tensor_copy(stg_b2[:, sub * D:(sub + 1) * D], dst)
                    inst.then_inc(stgf, 1)

        _bl.__exit__(None, None, None)
        ph.release()


def nld_target(gi):
    b = gi % 2
    n = 0
    for g2 in range(gi + 1):
        if g2 % 2 == b:
            n += 16 * 5
    return n



def _emit_moe1(nc, stack, ctx, glob, cast_done, cnt, consts_sb, C_AP, ones_sb):
    s4f, s3f = ctx["s4_f"], ctx["s3_f"]
    out = ctx["out"]
    tmp_b = ctx["tmp_b"]
    g1_w = ctx["g1_w"]
    ln2_s, ln2_h = ctx["ln2_s"], ctx["ln2_h"]
    w1_b, w2_b, w3_b = ctx["w1_b"], ctx["w2_b"], ctx["w3_b"]
    cast_m1 = cast_done["m1"]
    TT_ = 512
    NSUB = 4
    NT = NTOK // TT_
    NQ = 4
    QW = HID1 // NQ
    NUPG = 16
    NDNG = NSUB * 2

    with ExitStack() as pst:
        ph = Ph(nc, pst, "p6", glob.sempool)
        l2s_b = ph.sb("l2s_b", [128, D], F32)
        l2h_b = ph.sb("l2h_b", [128, D], F32)
        rows2 = ph.sb("rows2", [1, 2 * D], F32)
        fm = [[ph.sb(f"fm{b}_{k}", [128, TT_], BF16) for k in range(8)] for b in range(2)]
        rfm = [[ph.sb(f"rfm{b}_{k}", [128, TT_], BF16) for k in range(8)] for b in range(2)]
        w1t = [[ph.sb(f"w1t{b}_{k}", [128, QW], BF16) for k in range(8)] for b in range(2)]
        w2t = [[ph.sb(f"w2t{b}_{k}", [128, QW], BF16) for k in range(8)] for b in range(2)]
        w3t = [ph.sb(f"w3t{c}", [128, D], BF16) for c in range(NUPG)]
        h_sb = ph.sb("h_sb", [128, NUPG * TT_], BF16)
        s_sb = [ph.sb(f"s_sb{b}", [128, TT_], F32) for b in range(2)]
        acc = ph.sb("acc", [128, NSUB * D], F32)
        xres = [ph.sb(f"xres{i}", [128, D], F32) for i in range(2)]
        x3res = [ph.sb(f"x3res{i}", [128, D], F32) for i in range(2)]
        rw_sb = ph.sb("rw", [128, 8 * NEXP], BF16)
        comb = ph.sb("comb", [128, NT * NSUB * NEXP], F32)
        xa = [ph.sb(f"xa{i}", [128, D], F32) for i in range(2)]
        nrm = ph.sb("nrm", [128, D], F32)
        stg_a = ph.sb("stg_a", [128, 4 * D], BF16)
        lg = ph.sb("lg", [128, NEXP], F32)
        s1v = ph.sb("s1v", [128, 1], F32)
        s2v = ph.sb("s2v", [128, 1], F32)
        dif = ph.sb("dif", [128, 1], F32)
        w1v = ph.sb("w1v", [128, 1], F32)
        w2v = ph.sb("w2v", [128, 1], F32)
        sc2 = ph.sb("sc2", [128, NEXP], F32)
        m1_t = ph.sb("m1_t", [128, NEXP], F32)
        m2_t = ph.sb("m2_t", [128, NEXP], F32)
        mm_t = ph.sb("mm_t", [128, NEXP], F32)
        cn1 = ph.sb("cn1", [128, 1], F32)
        f_t = ph.sb("f_t", [128, 1], F32)
        sq_t = ph.sb("sq", [128, D], F32)
        mean_raw = ph.sb("meanr", [128, 1], F32)
        var_raw = ph.sb("varr", [128, 1], F32)
        var_t = ph.sb("var", [128, 1], F32)
        rstd_t = ph.sb("rstd", [128, 1], F32)
        nm_t = ph.sb("nm", [128, 1], F32)
        tt1 = ph.sb("tt1", [128, 1], F32)
        tact = ph.sb("tact", [128, 1], F32)
        h1_ps = [ph.ps(f"h1ps{b}", [128, TT_], F32) for b in range(2)]
        h2_ps = [ph.ps(f"h2ps{b}", [128, TT_], F32) for b in range(2)]
        dn_ps = [ph.ps(f"dnps{b}", [128, 512], F32) for b in range(2)]
        rt_ps = ph.ps("rt_ps", [128, NEXP], F32)

        ld_r2 = ph.sem("ld_r2", pooled=False)
        ld_rw = ph.sem("ld_rw", pooled=False)
        ld_xa = ph.sem("ld_xa", pooled=False)
        xa_done = ph.sem("xa_done")
        an_v = ph.sem("an_v")
        an_a = ph.sem("an_a")
        stga = ph.sem("stga")
        st_a = ph.sem("st_a", pooled=False)
        bcd = ph.sem("bcd")
        bce = ph.sem("bce")
        ld_rf = [ph.sem(f"ld_rf{b}") for b in range(2)]
        ld_fm = [ph.sem(f"ld_fm{b}") for b in range(2)]
        ld_w = [ph.sem(f"ld_w{b}") for b in range(2)]
        ld_d = ph.sem("ld_d")
        ld_xr = ph.sem("ld_xr", pooled=False)
        rt_mm = ph.sem("rt_mm")
        rt_v = ph.sem("rt_v")
        rt_a = ph.sem("rt_a")
        rt_done = ph.sem("rt_done")
        up_mm = ph.sem("up_mm")
        silu_done = ph.sem("silu")
        hprod = ph.sem("hprod")
        dn_done = ph.sem("dn_done")
        evac_done = ph.sem("evac_done")
        ln_v = ph.sem("ln_v")
        ln_r = ph.sem("ln_r")
        st_o = ph.sem("st_o", pooled=False)
        stg2 = ph.sem("stg2")
        vs = VS(ph, "vs")

        _bl = nc.Block(); block = _bl.__enter__()

        def _emit_wloads(s, p):
            x = p % NEXP
            for q in range(NQ):
                g = p * NQ + q
                if g >= 2:
                    s.wait_ge(up_mm, (g - 1) * (NUPG // NQ))
                for k in range(8):
                    s.dma_start(w1t[g % 2][k][:, :],
                                w1_b[x, k * 128:(k + 1) * 128, q * QW:(q + 1) * QW]
                                ).then_inc(ld_w[g % 2], 16)
                    s.dma_start(w2t[g % 2][k][:, :],
                                w2_b[x, k * 128:(k + 1) * 128, q * QW:(q + 1) * QW]
                                ).then_inc(ld_w[g % 2], 16)
            if p >= 1:
                s.wait_ge(dn_done, p * NDNG)
            for c in range(NUPG):
                s.dma_start(w3t[c][:, :],
                            w3_b[x, c * 128:(c + 1) * 128, 0:D]).then_inc(ld_d, 16)

        @block.sync
        def _(s):
            s.wait_ge(cast_m1, cnt["m1"])
            for T in range(NT):
                if T >= 2:
                    s.wait_ge(rt_mm, (T - 1) * NSUB)
                s.wait_ge(st_a, 16 * (T + 1))
                for k in range(8):
                    s.dma_start(rfm[T % 2][k][:, :],
                                tmp_b[T * TT_:(T + 1) * TT_, k * 128:(k + 1) * 128],
                                transpose=True).then_inc(ld_rf[T % 2], 16)
                if T >= 2:
                    s.wait_ge(up_mm, (T - 1) * NEXP * NUPG)
                for k in range(8):
                    s.dma_start(fm[T % 2][k][:, :],
                                tmp_b[T * TT_:(T + 1) * TT_, k * 128:(k + 1) * 128],
                                transpose=True).then_inc(ld_fm[T % 2], 16)
                for x in range(NEXP):
                    _emit_wloads(s, T * NEXP + x)


        @block.gpsimd
        def _(g):
            g.dma_start(rw_sb[:, :].rearrange("p (k e) -> p k e", k=8),
                        g1_w.rearrange("(k p) e -> p k e", p=128)).then_inc(ld_rw, 16)
            g.dma_start(rows2[:, 0:D], ln2_s).then_inc(ld_r2, 16)
            g.dma_start(rows2[:, D:2 * D], ln2_h).then_inc(ld_r2, 16)
            for sp in range(NT + 1):
                if sp >= 1:
                    g.wait_ge(stga, sp)
                    g.dma_start(tmp_b[(sp - 1) * 512:sp * 512, :].rearrange("(s p) d -> p s d", p=128),
                                stg_a[:, :].rearrange("p (s d) -> p s d", s=4)).then_inc(st_a, 16)
                if sp < NT:
                    for sub in range(4):
                        gs = sp * 4 + sub
                        if gs >= 2:
                            g.wait_ge(xa_done, gs - 1)
                        g.dma_start(xa[gs % 2][:, :],
                                    s4f[gs * 128:(gs + 1) * 128, :]).then_inc(ld_xa, 16)
            for T in range(NT + 1):
                if T >= 1:
                    g.wait_ge(stg2, T)
                    g.dma_start(out[(T - 1) * TT_:T * TT_, :].rearrange("(s p) d -> p s d", p=128),
                                acc[:, :].rearrange("p (s d) -> p s d", s=NSUB)).then_inc(st_o, 16)
                if T < NT:
                    for sub in range(NSUB):
                        gs = T * NSUB + sub
                        if gs >= 2:
                            g.wait_ge(ln_v, 2 * (gs - 2) + 1)
                        g.dma_start(xres[gs % 2][:, :],
                                    s4f[T * TT_ + sub * 128: T * TT_ + (sub + 1) * 128, :]
                                    ).then_inc(ld_xr, 16)
                        g.dma_start(x3res[gs % 2][:, :],
                                    s3f[T * TT_ + sub * 128: T * TT_ + (sub + 1) * 128, :]
                                    ).then_inc(ld_xr, 16)
            g.wait_ge(st_o, 16 * NT)

        @block.tensor
        def _(t):
            t.wait_ge(ld_r2, 32)
            for k2 in range(2):
                t.matmul(h1_ps[k2][:, :], ones_sb[:, :], rows2[:, k2 * 512:(k2 + 1) * 512],
                         start=True, stop=True)
            for k2 in range(2):
                t.matmul(h2_ps[k2][:, :], ones_sb[:, :],
                         rows2[:, D + k2 * 512:D + (k2 + 1) * 512],
                         start=True, stop=True).then_inc(bcd, 1)
            t.wait_ge(bce, 4)
            t.wait_ge(ld_rw, 16)
            for T in range(NT):
                t.wait_ge(ld_rf[T % 2], 16 * 8 * (T // 2 + 1))
                for sub in range(NSUB):
                    i = T * NSUB + sub
                    if i >= 1:
                        t.wait_ge(rt_done, i)
                    last = None
                    for k in range(8):
                        last = t.matmul(rt_ps[:, :],
                                        rfm[T % 2][k][:, sub * 128:(sub + 1) * 128],
                                        rw_sb[:, k * NEXP:(k + 1) * NEXP],
                                        start=(k == 0), stop=(k == 7))
                    last.then_inc(rt_mm, 1)
                t.wait_ge(ld_fm[T % 2], 16 * 8 * (T // 2 + 1))
                for x in range(NEXP):
                    pidx = T * NEXP + x
                    for q in range(NQ):
                        gq = pidx * NQ + q
                        t.wait_ge(ld_w[gq % 2], 16 * 16 * (gq // 2 + 1))
                        for mm in range(NUPG // NQ):
                            m = q * (NUPG // NQ) + mm
                            gu = pidx * NUPG + m
                            if gu >= 2:
                                t.wait_ge(hprod, gu - 1)
                            last = None
                            for k in range(8):
                                last = t.matmul(h1_ps[gu % 2][:, :],
                                                w1t[gq % 2][k][:, mm * 128:(mm + 1) * 128],
                                                fm[T % 2][k][:, :],
                                                start=(k == 0), stop=(k == 7))
                            for k in range(8):
                                last = t.matmul(h2_ps[gu % 2][:, :],
                                                w2t[gq % 2][k][:, mm * 128:(mm + 1) * 128],
                                                fm[T % 2][k][:, :],
                                                start=(k == 0), stop=(k == 7))
                            last.then_inc(up_mm, 1)
                    t.wait_ge(ld_d, 16 * NUPG * (pidx + 1))
                    t.wait_ge(hprod, (pidx + 1) * NUPG)
                    for sub in range(NSUB):
                        for nh in range(2):
                            gd = pidx * NDNG + sub * 2 + nh
                            if gd >= 2:
                                t.wait_ge(evac_done, gd - 1)
                            last = None
                            for c in range(NUPG):
                                last = t.matmul(dn_ps[gd % 2][:, :],
                                                h_sb[:, c * TT_ + sub * 128: c * TT_ + (sub + 1) * 128],
                                                w3t[c][:, nh * 512:(nh + 1) * 512],
                                                start=(c == 0), stop=(c == NUPG - 1))
                            last.then_inc(dn_done, 1)

        @block.vector
        def _(v):
            v.wait_ge(bcd, 1)
            for k2 in range(2):
                v.tensor_copy(l2s_b[:, k2 * 512:(k2 + 1) * 512], h1_ps[k2][:, :]).then_inc(bce, 1)
            for k2 in range(2):
                v.tensor_copy(l2h_b[:, k2 * 512:(k2 + 1) * 512], h2_ps[k2][:, :]).then_inc(bce, 1)
            for gs in range(NT * 4):
                v.wait_ge(ld_xa, 16 * (gs + 1))
                xat = xa[gs % 2]
                ln_stats_dve(v, vs, xat[:, :], D, sq_t[:, :], mean_raw, var_raw,
                             tt1, var_t).then_inc(an_v, 1)
                v.wait_ge(an_a, gs + 1)
                ln_apply_dve(v, vs, xat[:, :], nrm[:, :], mean_raw, rstd_t, nm_t, D)
                v.engine_nop().then_inc(xa_done, 1)
                v.tensor_tensor(nrm[:, :], nrm[:, :], l2s_b[:, :], ALU.mult)
                v.tensor_tensor(nrm[:, :], nrm[:, :], l2h_b[:, :], ALU.add)
                if gs % 4 == 0 and gs >= 4:
                    v.wait_ge(st_a, 16 * (gs // 4))
                inst = v.tensor_copy(stg_a[:, (gs % 4) * D:(gs % 4 + 1) * D], nrm[:, :])
                if gs % 4 == 3:
                    inst.then_inc(stga, 1)
            for T in range(NT):
                for sub in range(NSUB):
                    i = T * NSUB + sub
                    v.wait_ge(rt_mm, i + 1)
                    v.tensor_copy(lg[:, :], rt_ps[:, :]).then_inc(rt_done, 1)
                    v.wait_ge(rt_done, i + 1)
                    vs.mark(v.reduce_max(s1v[:, :], lg[:, :], axis=AX))
                    vs.wait(v)
                    vs.mark(v.tensor_scalar(m1_t[:, :], lg[:, :], s1v[:, 0:1], None, ALU.is_ge))
                    vs.wait(v)
                    vs.mark(v.scalar_tensor_tensor(sc2[:, :], m1_t[:, :], -1e30, lg[:, :],
                                                   ALU.mult, ALU.add))
                    vs.mark(v.reduce_sum(cn1[:, :], m1_t[:, :], axis=AX))
                    vs.wait(v)
                    vs.mark(v.reduce_max(s2v[:, :], sc2[:, :], axis=AX))
                    vs.mark(v.tensor_scalar(f_t[:, :], cn1[:, :], -1.0, 2.0, ALU.mult, ALU.add))
                    vs.wait(v)
                    vs.mark(v.tensor_scalar(m2_t[:, :], sc2[:, :], s2v[:, 0:1], None, ALU.is_ge))
                    vs.mark(v.tensor_tensor(dif[:, :], s1v[:, :], s2v[:, :], ALU.subtract))
                    vs.mark(v.tensor_single_scalar(f_t[:, :], f_t[:, :], 0.0, ALU.max))
                    v.engine_nop().then_inc(rt_v, 1)
                    v.wait_ge(rt_a, i + 1)
                    vs.mark(v.tensor_scalar(w2v[:, :], w1v[:, :], -1.0, 1.0, ALU.mult, ALU.add))
                    vs.wait(v)
                    vs.mark(v.tensor_scalar(m2_t[:, :], m2_t[:, :], w2v[:, 0:1], None, ALU.mult))
                    vs.wait(v)
                    vs.mark(v.tensor_scalar(m2_t[:, :], m2_t[:, :], f_t[:, 0:1], None, ALU.mult))
                    vs.wait(v)
                    v.scalar_tensor_tensor(comb[:, i * NEXP:(i + 1) * NEXP], m1_t[:, :],
                                           w1v[:, 0:1], m2_t[:, :], ALU.mult, ALU.add)
                for x in range(NEXP):
                    pidx = T * NEXP + x
                    for m in range(NUPG):
                        gu = pidx * NUPG + m
                        v.wait_ge(silu_done, gu + 1)
                        v.tensor_tensor(h_sb[:, m * TT_:(m + 1) * TT_],
                                        s_sb[gu % 2][:, :], h2_ps[gu % 2][:, :],
                                        ALU.mult).then_inc(hprod, 1)
                    for sub in range(NSUB):
                        for nh in range(2):
                            gd = pidx * NDNG + sub * 2 + nh
                            v.wait_ge(dn_done, gd + 1)
                            if x == 0 and sub == 0 and nh == 0 and T >= 1:
                                v.wait_ge(st_o, 16 * T)
                            dst = acc[:, sub * D + nh * 512: sub * D + (nh + 1) * 512]
                            cap = comb[:, (T * NSUB + sub) * NEXP + x:
                                       (T * NSUB + sub) * NEXP + x + 1]
                            if x == 0:
                                v.tensor_scalar(dst, dn_ps[gd % 2][:, :], cap, None,
                                                ALU.mult).then_inc(evac_done, 1)
                            else:
                                v.scalar_tensor_tensor(dst, dn_ps[gd % 2][:, :], cap,
                                                       dst, ALU.mult, ALU.add).then_inc(evac_done, 1)
                inst = None
                for sub in range(NSUB):
                    gs = T * NSUB + sub
                    accs = acc[:, sub * D:(sub + 1) * D]
                    v.wait_ge(ld_xr, 32 * (gs + 1))
                    v.tensor_tensor(accs, accs, xres[gs % 2][:, :], ALU.add)
                    v.tensor_tensor(accs, accs, x3res[gs % 2][:, :], ALU.add).then_inc(ln_v, 1)
                    ln_stats_dve(v, vs, accs, D, sq_t[:, :], mean_raw, var_raw,
                                 tt1, var_t).then_inc(ln_v, 1)
                    v.wait_ge(ln_r, gs + 1)
                    inst = ln_apply_dve(v, vs, accs, accs, mean_raw, rstd_t, nm_t, D)
                inst.then_inc(stg2, 1)

        @block.scalar
        def _(sc):
            for gs in range(NT * 4):
                sc.wait_ge(an_v, gs + 1)
                ln_rstd_act(sc, var_t, rstd_t, tact, C_AP).then_inc(an_a, 1)
            for T in range(NT):
                if T >= 1:
                    for sub in range(NSUB):
                        gs = (T - 1) * NSUB + sub
                        sc.wait_ge(ln_v, 2 * gs + 2)
                        ln_rstd_act(sc, var_t, rstd_t, tact, C_AP).then_inc(ln_r, 1)
                for sub in range(NSUB):
                    i = T * NSUB + sub
                    sc.wait_ge(rt_v, i + 1)
                    sc.activation(w1v[:, :], dif[:, :], AF.Sigmoid).then_inc(rt_a, 1)
                for x in range(NEXP):
                    pidx = T * NEXP + x
                    if pidx >= 1:
                        sc.wait_ge(dn_done, pidx * NDNG)
                    for m in range(NUPG):
                        gu = pidx * NUPG + m
                        sc.wait_ge(up_mm, gu + 1)
                        sc.activation(s_sb[gu % 2][:, :], h1_ps[gu % 2][:, :],
                                      AF.Silu).then_inc(silu_done, 1)
            for sub in range(NSUB):
                gs = (NT - 1) * NSUB + sub
                sc.wait_ge(ln_v, 2 * gs + 2)
                ln_rstd_act(sc, var_t, rstd_t, tact, C_AP).then_inc(ln_r, 1)

        _bl.__exit__(None, None, None)
        ph.release()


def kernel(**inputs):
    consts = np.tile(np.array([CONST_VALS], dtype=np.float32), (128, 1))
    ones_f = np.ones((1, 128), dtype=np.float32)
    iota_bc = np.tile(np.arange(C, dtype=np.float32)[None, :], (128, 1))
    ident_b = np.eye(128, dtype=ml_dtypes.bfloat16)
    cm = np.zeros((4, 128, 512), dtype=np.float32)
    ki = np.arange(128)[:, None]
    qj = np.arange(512)[None, :]
    for o in range(4):
        cm[o] = (o * 128 + ki) <= qj
    cmask = cm.astype(ml_dtypes.bfloat16)

    x_full = np.asarray(inputs["L0_attn_inputs"], dtype=np.float32)

    def w(name):
        return np.ascontiguousarray(np.asarray(inputs[name], dtype=np.float32))

    shared = {
        "nu_log": w("L0_attn_nu_log"), "theta_log": w("L0_attn_theta_log"),
        "gamma_log": w("L0_attn_gamma_log"),
        "B_re": w("L0_attn_B_re"), "B_im": w("L0_attn_B_im"),
        "C_re": w("L0_attn_C_re"), "C_im": w("L0_attn_C_im"),
        "Dvec": w("L0_attn_D").reshape(1, D),
        "r0_w": w("L0_ffn_router_w"), "r0_b": w("L0_ffn_router_b").reshape(1, NEXP),
        "up0": w("L0_ffn_expert_up"), "dn0": w("L0_ffn_expert_down"),
        "t_gamma": w("L1_attn_gamma").reshape(1, D),
        "t_wq": w("L1_attn_w_q"), "t_wkv": w("L1_attn_w_kv"), "t_wout": w("L1_attn_w_out"),
        "ln1_s": w("L1_ffn_ln1_scale").reshape(1, D), "ln1_h": w("L1_ffn_ln1_shift").reshape(1, D),
        "a_wq": w("L1_ffn_wq"), "a_wk": w("L1_ffn_wk"), "a_wv": w("L1_ffn_wv"),
        "a_wo": w("L1_ffn_wo"), "a_bo": w("L1_ffn_bo").reshape(1, D),
        "ln2_s": w("L1_ffn_ln2_scale").reshape(1, D), "ln2_h": w("L1_ffn_ln2_shift").reshape(1, D),
        "g1_w": w("L1_ffn_gate_w"),
        "w1": w("L1_ffn_w1"), "w2": w("L1_ffn_w2"), "w3": w("L1_ffn_w3"),
        "consts": consts, "ones_f": ones_f, "cmask": cmask,
        "iota_bc": iota_bc, "ident_b": ident_b,
    }

    nc = build_program()
    in_maps = []
    for i in range(8):
        m = dict(shared)
        m["x"] = np.ascontiguousarray(x_full[i])
        in_maps.append(m)
    res = run_bass_kernel_spmd(nc, in_maps, core_ids=list(range(8)))
    return np.stack([r["out"] for r in res.results], axis=0)
